# version 17
# baseline (speedup 1.0000x reference)
import contextlib
import numpy as np
import concourse.bass as bass
import concourse.mybir as mybir
from concourse.bass_utils import run_bass_kernel_spmd

F32 = mybir.dt.float32
BF16 = mybir.dt.bfloat16
AF = mybir.ActivationFunctionType
ALU = mybir.AluOpType
AX = mybir.AxisListType

D = 1024
T = 2048
NCD = 8
NT = 4
TC = 512
EPS = 1e-5
NEG = -3.0e38

ENGINES = ("tensor", "vector", "scalar", "gpsimd", "sync")
ARENA_COLS = [102400]
XRES_OFF = 0
FREE0 = 32768


class Ctx:
    def __init__(self, nc):
        self.nc = nc
        self.streams = {e: [] for e in ENGINES}
        self.semh = {}
        self.semc = {}
        self.watermark = {e: {} for e in ENGINES}
        self.C = {}
        self._arena = None
        self._off = 0

    def add_sem(self, name, handle):
        self.semh[name] = handle
        self.semc[name] = 0

    def val(self, sem):
        return self.semc[sem]

    def wait(self, eng, sem, val=None):
        v = self.semc[sem] if val is None else val
        if v <= 0 or self.watermark[eng].get(sem, 0) >= v:
            return
        self.watermark[eng][sem] = v
        h = self.semh[sem]
        self.streams[eng].append(lambda e, h=h, v=v: e.wait_ge(h, v))

    AUTO = {"vector": "dve", "scalar": "act", "gpsimd": "pool"}

    def do(self, eng, fn, inc=None, n=1, is_dma=False):
        if eng in self.AUTO and not is_dma:
            sname = self.AUTO[eng]
            assert inc is None or inc == sname
            self.wait(eng, sname)
            self.semc[sname] += 1
            h = self.semh[sname]
            self.streams[eng].append(lambda e, fn=fn, h=h: fn(e).then_inc(h, 1))
            return self.semc[sname]
        if inc is None:
            self.streams[eng].append(fn)
            return None
        self.semc[inc] += n
        h = self.semh[inc]
        self.streams[eng].append(lambda e, fn=fn, h=h, n=n: fn(e).then_inc(h, n))
        return self.semc[inc]

    def dma(self, eng, out_ap, in_ap, inc):
        self.wait(eng, inc)
        return self.do(eng, lambda e, o=out_ap, i=in_ap: e.dma_start(out=o, in_=i),
                       inc=inc, n=16, is_dma=True)

    def set_arena(self, handle):
        self._arena = handle
        self._off = 0

    def amark(self):
        return self._off

    def arelease(self, mark):
        self._off = mark

    def alloc_b(self, cols):
        cols = (cols + 15) // 16 * 16
        off = self._off
        assert off + cols <= ARENA_COLS[0], f"arena overflow {off}+{cols}"
        self._off = off + cols
        return self._arena.ap()[:, off:off + cols]

    def alloc_f(self, cols):
        return self.alloc_b(2 * cols).bitcast(F32)

    def at_b(self, off, cols):
        assert off + cols <= ARENA_COLS[0], f"arena overflow {off}+{cols}"
        return self._arena.ap()[:, off:off + cols]

    def at_f(self, off, cols):
        return self.at_b(off, 2 * cols).bitcast(F32)

    def fence(self, engines=ENGINES, sems=("pe", "dve", "act", "pool", "ld", "gld", "st", "qa", "qb")):
        for e in engines:
            for s in sems:
                self.wait(e, s)

    def tap(self, ap):
        if "dbg" not in self.C:
            return
        for s in ("pe", "dve", "act", "pool"):
            self.wait("gpsimd", s)
        P, Cc = ap.shape[-2], ap.shape[-1]
        off = self.C["dbg_off"]
        assert off + Cc <= 16384
        self.dma("gpsimd", self.C["dbg"][0:P, off:off + Cc], ap, inc="gld")
        self.C["dbg_off"] = off + Cc



import traceback as _tb


def mm(ctx, out, lhsT, rhs, start, stop, inc=None):
    site = "".join(_tb.format_stack(limit=3)[:-1]) if MM_DEBUG else None

    def run(e, o=out, l=lhsT, r=rhs, s=start, p=stop, site=site):
        try:
            return e.matmul(o, l, r, start=s, stop=p)
        except Exception:
            if site:
                print("MM FAIL at:\n", site)
                print("out", o.shape, "lhsT", l.shape, "rhs", r.shape)
            raise
    return ctx.do("tensor", run, inc=inc)


MM_DEBUG = False


def tt(ctx, eng, op, out, a, b, inc=None):
    return ctx.do(eng, lambda e, o=out, x=a, y=b, q=op: e.tensor_tensor(o, x, y, q),
                  inc=inc)


def copy(ctx, eng, out, a, inc=None):
    return ctx.do(eng, lambda e, o=out, x=a: e.tensor_copy(o, x), inc=inc)


def act(ctx, out, a, func, bias=0.0, scale=1.0, accum=None, inc=None):
    if accum is None:
        return ctx.do("scalar", lambda e, o=out, x=a, f=func, b=bias, s=scale:
                      e.activation(o, x, f, bias=b, scale=s), inc=inc)
    return ctx.do("scalar", lambda e, o=out, x=a, f=func, b=bias, s=scale, ac=accum:
                  e.activation(o, x, f, bias=b, scale=s, accum_out=ac), inc=inc)



def emit_ln_plain(ctx, ps, xres):
    m0 = ctx.amark()
    sqf = ctx.alloc_b(T * NCD)
    rows = ctx.alloc_f(3 * T)
    t1 = ctx.alloc_f(TC)
    C = ctx.C
    rS = rows[0:1, 0:T]
    rQ = rows[0:1, T:2 * T]
    rT = rows[0:1, 2 * T:3 * T]
    for cc in range(NCD):
        act(ctx, sqf[:, cc * T:(cc + 1) * T], xres[:, cc * T:(cc + 1) * T],
            AF.Square, inc="act" if cc == NCD - 1 else None)
    ctx.wait("tensor", "act")
    evdone = {}
    for nt in range(NT):
        t0 = nt * TC
        pA = ps[nt % 2][0:1, 0:TC]
        pB = ps[2 + nt % 2][0:1, 0:TC]
        if nt >= 2:
            ctx.wait("tensor", "dve", evdone[nt % 2])
        for cc in range(NCD):
            mm(ctx, pA, C["onesf"], xres[:, cc * T + t0: cc * T + t0 + TC],
               cc == 0, cc == NCD - 1)
            mm(ctx, pB, C["onesb"], sqf[:, cc * T + t0: cc * T + t0 + TC],
               cc == 0, cc == NCD - 1, inc="pe" if cc == NCD - 1 else None)
        ctx.wait("vector", "pe")
        copy(ctx, "vector", rS[:, t0:t0 + TC], pA)
        evdone[nt % 2] = copy(ctx, "vector", rQ[:, t0:t0 + TC], pB, inc="dve")
    ctx.do("vector", lambda e, o=rS, i=rS: e.tensor_scalar_mul(o, i, 1.0 / D))
    ctx.do("vector", lambda e, o=rQ, i=rQ: e.tensor_scalar_mul(o, i, 1.0 / D))
    tt(ctx, "vector", ALU.mult, rT, rS, rS)
    tt(ctx, "vector", ALU.subtract, rQ, rQ, rT, inc="dve")
    ctx.wait("scalar", "dve")
    act(ctx, rQ, rQ, AF.Ln, bias=C["eps1"])
    act(ctx, rQ, rQ, AF.Exp, scale=-0.5, inc="act")
    ctx.wait("tensor", "act")
    ctx.wait("tensor", "dve")
    apdone = {}

    def bcast(nt):
        t0 = nt * TC
        if nt >= 2:
            ctx.wait("tensor", "dve", apdone[nt % 2])
        mm(ctx, ps[4 + nt % 2], C["ones1f"], rS[:, t0:t0 + TC], True, True)
        mm(ctx, ps[6 + nt % 2], C["ones1f"], rQ[:, t0:t0 + TC], True, True,
           inc="pe")
        return ctx.val("pe")

    def apply(nt, vb):
        t0 = nt * TC
        ctx.wait("vector", "pe", vb)
        for cc in range(NCD):
            xc = xres[:, cc * T + t0: cc * T + t0 + TC]
            tt(ctx, "vector", ALU.subtract, t1, xc, ps[4 + nt % 2])
            tt(ctx, "vector", ALU.mult, xc, t1, ps[6 + nt % 2],
               inc="dve" if cc == NCD - 1 else None)
        apdone[nt % 2] = ctx.val("dve")

    vbs = {}
    vbs[0] = bcast(0)
    vbs[1] = bcast(1)
    apply(0, vbs[0])
    vbs[2] = bcast(2)
    apply(1, vbs[1])
    vbs[3] = bcast(3)
    apply(2, vbs[2])
    apply(3, vbs[3])
    ctx.wait("tensor", "dve")
    ctx.wait("scalar", "dve")
    ctx.arelease(m0)



def build_program(phases, extra_inputs, dbg=False):
    nc = bass.Bass("TRN2", target_bir_lowering=False)
    io = {}
    io["x_in"] = nc.dram_tensor("x_in", [D, T], F32, kind="ExternalInput").ap()
    io["out"] = nc.dram_tensor("out", [D, T], F32, kind="ExternalOutput").ap()
    if dbg:
        io["dbg"] = nc.dram_tensor("dbg", [128, 16384], F32, kind="ExternalOutput").ap()
    for name, (shape, dtype) in extra_inputs.items():
        io[name] = nc.dram_tensor(name, list(shape), dtype, kind="ExternalInput").ap()
    io["spill_x"] = nc.dram_tensor("spill_x", [D, T], F32, kind="Internal").ap()
    io["spill_x2"] = nc.dram_tensor("spill_x2", [D, T], F32, kind="Internal").ap()
    io["spill_b"] = nc.dram_tensor("spill_b", [D, T], BF16, kind="Internal").ap()

    with contextlib.ExitStack() as st:
        ctx = Ctx(nc)
        arena_h = None
        while arena_h is None:
            try:
                arena_h = nc.alloc_sbuf_tensor("arena", [128, ARENA_COLS[0]], BF16)
            except Exception:
                ARENA_COLS[0] -= 2048
                if ARENA_COLS[0] < 90112:
                    raise
        kon_h = nc.alloc_sbuf_tensor("konst", [128, 16], F32)
        onesf_h = nc.alloc_sbuf_tensor("onesf", [128, 2], F32)
        onesb_h = nc.alloc_sbuf_tensor("onesb", [128, 2], BF16)
        ones1f_h = nc.alloc_sbuf_tensor("ones1f", [1, 128], F32)
        ones1b_h = nc.alloc_sbuf_tensor("ones1b", [1, 128], BF16)
        ident_h = nc.alloc_sbuf_tensor("identb", [128, 128], BF16)
        ctx.set_arena(arena_h)
        xres = ctx.at_f(XRES_OFF, NCD * T)
        ctx._off = FREE0
        C = ctx.C
        C["xres"] = xres
        C["onesf"] = onesf_h.ap()[:, 0:1]
        C["onesb"] = onesb_h.ap()[:, 0:1]
        C["ones1f"] = ones1f_h.ap()
        C["ones1b"] = ones1b_h.ap()
        C["ident"] = ident_h.ap()
        C["eps1"] = kon_h.ap()[0:1, 0:1]
        C["eps64"] = kon_h.ap()[64:65, 0:1]
        C["epsc"] = kon_h.ap()[:, 0:1]
        if dbg:
            C["dbg"] = io["dbg"]
            C["dbg_off"] = 0

        for s in ("ld", "gld", "st", "pe", "dve", "act", "pool", "qa", "qb"):
            ctx.add_sem(s, st.enter_context(nc.semaphore(s)))

        ps = [st.enter_context(nc.psum_tensor(f"ps{i}", [128, TC], F32)).ap()
              for i in range(8)]
        C["psb"] = ps[6].bitcast(BF16)

        ctx.do("vector", lambda e: e.memset(onesf_h.ap(), 1.0))
        ctx.do("vector", lambda e: e.memset(onesb_h.ap(), 1.0))
        ctx.do("vector", lambda e: e.memset(ones1f_h.ap(), 1.0))
        ctx.do("vector", lambda e: e.memset(ones1b_h.ap(), 1.0))
        ctx.do("vector", lambda e: e.memset(kon_h.ap()[:, 0:1], EPS), inc="dve")
        ctx.dma("gpsimd", C["ident"], io["ident"], inc="gld")
        ctx.dma("sync", xres.rearrange("p (c t) -> p c t", c=NCD),
                io["x_in"].rearrange("(c p) t -> p c t", p=128), inc="ld")
        for f in phases:
            f(ctx, ps, io)

        for s in ("dve", "pe", "act", "pool"):
            ctx.wait("gpsimd", s)
        vst = ctx.dma("gpsimd", io["out"].rearrange("(c p) t -> p c t", p=128),
                      xres.rearrange("p (c t) -> p c t", c=NCD), inc="st")
        ctx.wait("gpsimd", "st", vst)

        with nc.Block() as block:
            for eng_name in ENGINES:
                stream = ctx.streams[eng_name]

                def make(fns):
                    def body(h):
                        for fn in fns:
                            fn(h)
                    return body
                getattr(block, eng_name)(make(stream))
    return nc



def phase_ln_only(ctx, ps, io):
    ctx.fence()
    emit_ln_plain(ctx, ps, ctx.C["xres"])



NF = 2176
NKC = 17

L0A_A1R = 32768
L0A_A1I = 34944
L0A_OUTW = 37120
L0A_S = 38144
L0A_V = 39168
L0A_UVW = 55552
L0A_XBF = 63744
L0A_UT = 80128
L0A_ZR = 0
L0A_STREAM = 17408
L0A_ZI = 55552
L0A_GRT = 17408
L0A_GIT = 26112
L0A_G = 39168
L0A_OW = 55552


def phase_l0_attn(ctx, ps, io):
    C = ctx.C
    xres = C["xres"]
    psb = C["psb"]

    S_sb = ctx.at_b(L0A_S, 1024)
    a1r_b = ctx.at_b(L0A_A1R, NF)[0:32, :]
    a1i_b = ctx.at_b(L0A_A1I, NF)[0:32, :]
    outw_b = ctx.at_b(L0A_OUTW, 1024)[0:32, :]

    ctx._off = L0A_V
    pos_sb = ctx.alloc_f(32)
    grep_sb = ctx.alloc_f(3 * 1024)
    brep_sb = ctx.alloc_f(3 * 1024)
    go_sb = ctx.alloc_f(1024)
    bo_sb = ctx.alloc_f(1024)
    wrep_sb = ctx.alloc_f(1024)
    fc_b = ctx.alloc_b(96)[0:32, :]
    h_f = ctx.alloc_f(1024)
    hs_f = ctx.alloc_f(1024)
    st_f = ctx.alloc_f(256)
    sil_b = ctx.alloc_b(1024)
    sT_b = ctx.alloc_b(4096)

    ctx.dma("sync", pos_sb, io["rpe_pos_tiles"], inc="ld")
    ctx.dma("sync", wrep_sb, io["rpe_wrep32"], inc="ld")
    ctx.dma("sync", go_sb, io["rpe_gorep32"], inc="ld")
    ctx.dma("sync", bo_sb, io["rpe_borep32"], inc="ld")
    ctx.dma("sync", grep_sb, io["rpe_grep32"], inc="ld")
    ctx.dma("sync", brep_sb, io["rpe_brep32"], inc="ld")
    ctx.dma("gpsimd", fc_b.rearrange("p (i c) -> p i c", i=3),
            io["rpe_fc"].rearrange("i k c -> k i c"), inc="gld")
    ctx.dma("gpsimd", outw_b, io["rpe_outw"], inc="gld")
    for e in ("tensor", "vector", "scalar"):
        ctx.wait(e, "ld")
        ctx.wait(e, "gld")
        ctx.wait(e, "dve")

    def rpe_ln_silu_batch(grep, brep, out_bf):
        s0 = st_f[:, 0:32]
        s1 = st_f[:, 32:64]
        mu = st_f[:, 64:96]
        va = st_f[:, 96:128]
        tmp = st_f[:, 128:160]
        ctx.do("vector", lambda e, o=s0,
               i=h_f.rearrange("p (t c) -> p t c", c=32):
               e.reduce_sum(o, i, axis=AX.X))
        ctx.wait("scalar", "dve")
        vq = act(ctx, hs_f, h_f, AF.Square, inc="act")
        ctx.wait("vector", "act", vq)
        ctx.do("vector", lambda e, o=s1,
               i=hs_f.rearrange("p (t c) -> p t c", c=32):
               e.reduce_sum(o, i, axis=AX.X))
        ctx.do("vector", lambda e, o=mu, i=s0: e.tensor_scalar_mul(o, i, 1.0 / 32))
        ctx.do("vector", lambda e, o=va, i=s1: e.tensor_scalar_mul(o, i, 1.0 / 32))
        tt(ctx, "vector", ALU.mult, tmp, mu, mu)
        tt(ctx, "vector", ALU.subtract, va, va, tmp, inc="dve")
        ctx.wait("scalar", "dve")
        act(ctx, va, va, AF.Ln, bias=C["epsc"])
        va2 = act(ctx, va, va, AF.Exp, scale=-0.5, inc="act")
        ctx.wait("vector", "act", va2)
        for t in range(32):
            ctx.do("vector", lambda e, o=hs_f[:, t * 32:(t + 1) * 32],
                   i=h_f[:, t * 32:(t + 1) * 32], m=mu[:, t:t + 1],
                   r=va[:, t:t + 1]:
                   e.tensor_scalar(o, i, m, r, ALU.subtract, ALU.mult))
        tt(ctx, "vector", ALU.mult, hs_f, hs_f, grep)
        tt(ctx, "vector", ALU.add, hs_f, hs_f, brep, inc="dve")
        ctx.wait("scalar", "dve")
        return act(ctx, out_bf, hs_f, AF.Silu, inc="act")

    for t in range(32):
        ctx.do("vector", lambda e, o=h_f[:, t * 32:(t + 1) * 32],
               i=wrep_sb[:, t * 32:(t + 1) * 32], s=pos_sb[:, t:t + 1]:
               e.tensor_scalar_mul(o, i, s),
               inc="dve" if t == 31 else None)
    for i in range(3):
        vs = rpe_ln_silu_batch(grep_sb[:, i * 1024:(i + 1) * 1024],
                               brep_sb[:, i * 1024:(i + 1) * 1024], sil_b)
        ctx.wait("tensor", "act", vs)
        for grp in range(2):
            if grp == 1:
                ctx.wait("tensor", "dve")
            for b2 in range(2):
                psbk = C["psb"] if b2 == 0 else ps[7].bitcast(BF16)
                for tt_i in range(8):
                    t = grp * 16 + b2 * 8 + tt_i
                    ctx.do("tensor",
                           lambda e, o=psbk[0:32, tt_i * 128:(tt_i + 1) * 128],
                           i=sil_b[:, t * 32:(t + 1) * 32], d=C["ident"]:
                           e.transpose(o, i, d),
                           inc="pe" if (b2 == 1 and tt_i == 7) else None)
            ctx.wait("vector", "pe")
            copy(ctx, "vector", sT_b[0:32, grp * 2048:grp * 2048 + 1024],
                 C["psb"][0:32, 0:1024])
            copy(ctx, "vector", sT_b[0:32, grp * 2048 + 1024:grp * 2048 + 2048],
                 ps[7].bitcast(BF16)[0:32, 0:1024], inc="dve")
        ctx.wait("tensor", "dve")
        for t in range(32):
            bk = t // 16
            mm(ctx, ps[4 + bk][:, (t % 16) * 32:(t % 16) * 32 + 32],
               sT_b[0:32, t * 128:(t + 1) * 128],
               fc_b[:, i * 32:(i + 1) * 32], True, True,
               inc="pe" if t == 31 else None)
        ctx.wait("vector", "pe")
        copy(ctx, "vector", h_f[:, 0:512], ps[4])
        vv2 = copy(ctx, "vector", h_f[:, 512:1024], ps[5], inc="dve")
        ctx.wait("scalar", "dve", vv2)
        ctx.wait("tensor", "dve", vv2)
    rpe_ln_silu_batch(go_sb, bo_sb, S_sb)
    ctx.wait("tensor", "act")

    ctx.fence()
    ctx._off = L0A_V
    fg_sb = [ctx.alloc_b(NF), ctx.alloc_b(NF)]
    a1_f = ctx.alloc_f(NF)[0:32, :]
    NC5 = [(0, 512), (512, 1024), (1024, 1536), (1536, 2048), (2048, NF)]
    pe_done = {}
    for ri, (srcname, dstb) in enumerate((("fgr", a1r_b), ("fgi", a1i_b))):
        src = io[srcname]
        for p in range(32):
            buf = fg_sb[p % 2]
            qsem = "qa" if p % 2 == 0 else "qb"
            key = (p % 2)
            if key in pe_done:
                ctx.wait("gpsimd", "pe", pe_done[key])
            vq = ctx.dma("gpsimd", buf, src[p * 128:(p + 1) * 128, :], inc=qsem)
            ctx.wait("tensor", qsem, vq)
            for ci, (c0, c1) in enumerate(NC5):
                mm(ctx, ps[ci][0:32, 0:c1 - c0], S_sb[:, p * 32:(p + 1) * 32],
                   buf[:, c0:c1], p == 0, p == 31,
                   inc="pe" if ci == len(NC5) - 1 else None)
            pe_done[p % 2] = ctx.val("pe")
        ctx.wait("vector", "pe")
        for ci, (c0, c1) in enumerate(NC5):
            copy(ctx, "vector", a1_f[:, c0:c1], ps[ci][0:32, 0:c1 - c0])
        ctx.do("vector", lambda e, o=dstb, i=a1_f: e.tensor_copy(o, i), inc="dve")
        ctx.wait("tensor", "dve")

    ctx.tap(a1r_b)
    ctx.tap(a1i_b)
    ctx.fence()
    v_sb = ctx.at_b(L0A_V, 16 * 1024)
    uvw_sb = ctx.at_b(L0A_UVW, NCD * 1024)
    xbf = ctx.at_b(L0A_XBF, NCD * T)
    uT_sb = ctx.at_b(L0A_UT, NCD * T)
    for k in range(NCD):
        copy(ctx, "vector", xbf[:, k * T:(k + 1) * T], xres[:, k * T:(k + 1) * T],
             inc="dve" if k == NCD - 1 else None)
    vxbf = ctx.val("dve")
    ctx.wait("tensor", "dve", vxbf)
    ctx.wait("sync", "dve", vxbf)
    vspill = ctx.dma("sync", io["spill_x"].rearrange("(c p) t -> p c t", p=128),
                     xres.rearrange("p (c t) -> p c t", c=NCD), inc="ld")
    ctx.dma("gpsimd", uvw_sb.rearrange("p (c t) -> p c t", c=NCD),
            io["uv_w"].rearrange("(c p) t -> p c t", p=128)[:, :, 1024:2048],
            inc="gld")
    ctx.wait("tensor", "gld")
    bankdone = {}
    for tt_i in range(16):
        for nc2 in range(2):
            bk = (tt_i % 2) * 2 + nc2
            if bk in bankdone:
                ctx.wait("tensor", "act", bankdone[bk])
            for k in range(NCD):
                mm(ctx, ps[bk], xbf[:, k * T + tt_i * 128: k * T + tt_i * 128 + 128],
                   uvw_sb[:, k * 1024 + nc2 * 512: k * 1024 + (nc2 + 1) * 512],
                   k == 0, k == NCD - 1, inc="pe" if k == NCD - 1 else None)
            ctx.wait("scalar", "pe")
            bankdone[bk] = act(
                ctx, v_sb[:, tt_i * 1024 + nc2 * 512: tt_i * 1024 + (nc2 + 1) * 512],
                ps[bk], AF.Silu, inc="act")
    ctx.wait("gpsimd", "pe")
    ctx.dma("gpsimd", uvw_sb.rearrange("p (c t) -> p c t", c=NCD),
            io["uv_w"].rearrange("(c p) t -> p c t", p=128)[:, :, 0:1024],
            inc="gld")
    ctx.wait("tensor", "gld")
    ctx.wait("tensor", "act")
    bankdone = {}
    for m in range(NCD):
        for nt in range(NT):
            bk = (m * NT + nt) % 4
            if bk in bankdone:
                ctx.wait("tensor", "act", bankdone[bk])
            for k in range(NCD):
                mm(ctx, ps[bk], uvw_sb[:, k * 1024 + m * 128: k * 1024 + (m + 1) * 128],
                   xbf[:, k * T + nt * TC: k * T + (nt + 1) * TC],
                   k == 0, k == NCD - 1, inc="pe" if k == NCD - 1 else None)
            ctx.wait("scalar", "pe")
            bankdone[bk] = act(
                ctx, uT_sb[:, m * T + nt * TC: m * T + (nt + 1) * TC],
                ps[bk], AF.Silu, inc="act")
    ctx.wait("vector", "act")

    ctx.tap(v_sb[:, 0:1024])
    ctx.tap(uT_sb[:, 0:512])
    ctx.fence()
    zr_sb = ctx.at_b(L0A_ZR, NKC * 1024)
    zi_sb = ctx.at_b(L0A_ZI, NKC * 1024)
    ctx._off = L0A_STREAM
    fr_sb = [ctx.alloc_b(2048), ctx.alloc_b(2048)]
    fi_sb = [ctx.alloc_b(2048), ctx.alloc_b(2048)]
    vh_b = ctx.alloc_b(2048)
    zf_f = ctx.alloc_f(1024)
    assert ctx._off <= 32768
    ctx.wait("vector", "ld", vspill)
    frdone = {}
    uinfo = []

    def emit_z(u):
        g = u["g"]
        kc, nc2 = u["kc"], u["nc2"]
        vh = vh_b[:, nc2 * 1024:(nc2 + 1) * 1024]
        ctx.wait("vector", "pe", u["vmm"])
        copy(ctx, "vector", vh[:, 0:512], ps[g + 0])
        copy(ctx, "vector", vh[:, 512:1024], ps[g + 1])
        t1 = zf_f[:, 0:512]
        t2 = zf_f[:, 512:1024]
        zrd = zr_sb[:, kc * 1024 + nc2 * 512: kc * 1024 + (nc2 + 1) * 512]
        zid = zi_sb[:, kc * 1024 + nc2 * 512: kc * 1024 + (nc2 + 1) * 512]
        tt(ctx, "vector", ALU.mult, t1, vh[:, 0:512], ps[g + 2])
        tt(ctx, "vector", ALU.mult, t2, vh[:, 512:1024], ps[g + 3])
        tt(ctx, "vector", ALU.subtract, zrd, t1, t2)
        tt(ctx, "vector", ALU.mult, t1, vh[:, 512:1024], ps[g + 2])
        tt(ctx, "vector", ALU.mult, t2, vh[:, 0:512], ps[g + 3])
        tt(ctx, "vector", ALU.add, zid, t1, t2, inc="dve")
        u["vz"] = ctx.val("dve")

    for kc in range(NKC):
        pp = kc % 2
        qsem = "qa" if pp == 0 else "qb"
        if pp in frdone:
            ctx.wait("gpsimd", "pe", frdone[pp])
        ctx.dma("gpsimd", fr_sb[pp].rearrange("p (t f) -> p t f", f=128),
                io["frt"][kc].rearrange("(t p) f -> p t f", p=128), inc=qsem)
        vq = ctx.dma("gpsimd", fi_sb[pp].rearrange("p (t f) -> p t f", f=128),
                io["fit"][kc].rearrange("(t p) f -> p t f", p=128), inc=qsem)
        for nc2 in range(2):
            u = {}
            g = 4 * ((2 * kc + nc2) % 2)
            u["g"] = g
            u["kc"] = kc
            u["nc2"] = nc2
            ctx.wait("tensor", qsem, vq)
            if ("zdone", g) in dict(uinfo[-2:-1] and [] or []):
                pass
            if len(uinfo) >= 2 and uinfo[-2].get("vz"):
                ctx.wait("tensor", "dve", uinfo[-2]["vz"])
            for tt_i in range(16):
                mm(ctx, ps[g + 0], fr_sb[pp][:, tt_i * 128:(tt_i + 1) * 128],
                   v_sb[:, tt_i * 1024 + nc2 * 512: tt_i * 1024 + (nc2 + 1) * 512],
                   tt_i == 0, tt_i == 15)
            for tt_i in range(16):
                mm(ctx, ps[g + 1], fi_sb[pp][:, tt_i * 128:(tt_i + 1) * 128],
                   v_sb[:, tt_i * 1024 + nc2 * 512: tt_i * 1024 + (nc2 + 1) * 512],
                   tt_i == 0, tt_i == 15)
            mm(ctx, ps[g + 2], a1r_b[:, kc * 128:(kc + 1) * 128],
               outw_b[:, nc2 * 512:(nc2 + 1) * 512], True, True)
            u["vmm"] = mm(ctx, ps[g + 3], a1i_b[:, kc * 128:(kc + 1) * 128],
                          outw_b[:, nc2 * 512:(nc2 + 1) * 512], True, True,
                          inc="pe")
            uinfo.append(u)
            if len(uinfo) >= 2:
                emit_z(uinfo[-2])
        frdone[pp] = ctx.val("pe")
    emit_z(uinfo[-1])

    ctx.fence()
    g_sb = ctx.at_b(L0A_G, NCD * T)
    grt_sb = ctx.at_b(L0A_GRT, NKC * 512)
    git_sb = ctx.at_b(L0A_GIT, NKC * 512)
    gdone = {}
    for nt in range(NT):
        if nt > 0:
            ctx.wait("gpsimd", "pe")
        ctx.dma("gpsimd", grt_sb.rearrange("p (k f) -> p k f", k=NKC),
                io["grt"][nt].rearrange("k p f -> p k f"), inc="gld")
        ctx.dma("gpsimd", git_sb.rearrange("p (k f) -> p k f", k=NKC),
                io["git"][nt].rearrange("k p f -> p k f"), inc="gld")
        ctx.wait("tensor", "gld")
        for m in range(NCD):
            bk = m % 4
            if bk in gdone:
                ctx.wait("tensor", "dve", gdone[bk])
            pst = ps[bk]
            for kc in range(NKC):
                mm(ctx, pst, zr_sb[:, kc * 1024 + m * 128: kc * 1024 + (m + 1) * 128],
                   grt_sb[:, kc * 512:(kc + 1) * 512], kc == 0, False)
            for kc in range(NKC):
                mm(ctx, pst, zi_sb[:, kc * 1024 + m * 128: kc * 1024 + (m + 1) * 128],
                   git_sb[:, kc * 512:(kc + 1) * 512], False, kc == NKC - 1,
                   inc="pe" if kc == NKC - 1 else None)
            ctx.wait("vector", "pe")
            gdone[bk] = tt(
                ctx, "vector", ALU.mult,
                g_sb[:, m * T + nt * TC: m * T + (nt + 1) * TC],
                uT_sb[:, m * T + nt * TC: m * T + (nt + 1) * TC], pst, inc="dve")

    ctx.tap(g_sb[:, 0:2048])
    ctx.fence()
    ow_sb = ctx.at_b(L0A_OW, NCD * 1024)
    ctx.wait("sync", "dve")
    vx = ctx.dma("sync", xres.rearrange("p (c t) -> p c t", c=NCD),
                 io["spill_x"].rearrange("(c p) t -> p c t", p=128), inc="ld")
    ctx.wait("gpsimd", "pe")
    vw = ctx.dma("gpsimd", ow_sb.rearrange("p (c t) -> p c t", c=NCD),
                 io["o_w"].rearrange("(c p) t -> p c t", p=128), inc="gld")
    ctx.wait("tensor", "gld", vw)
    ctx.wait("tensor", "ld", vx)
    ctx.wait("vector", "ld", vx)
    resdone = {}
    for m in range(NCD):
        for nt in range(NT):
            bk = (m * NT + nt) % 4
            if bk in resdone:
                ctx.wait("tensor", "dve", resdone[bk])
            pst = ps[bk]
            for k in range(NCD):
                mm(ctx, pst, ow_sb[:, k * 1024 + m * 128: k * 1024 + (m + 1) * 128],
                   g_sb[:, k * T + nt * TC: k * T + (nt + 1) * TC],
                   k == 0, k == NCD - 1, inc="pe" if k == NCD - 1 else None)
            ctx.wait("vector", "pe")
            xs = xres[:, m * T + nt * TC: m * T + (nt + 1) * TC]
            resdone[bk] = tt(ctx, "vector", ALU.add, xs, xs, pst, inc="dve")
    ctx.wait("tensor", "dve")
    ctx.wait("scalar", "dve")
    ctx.tap(xres[:, 0:2048])
    ctx._off = L0A_G + NCD * T
    ctx.fence()
    emit_ln_plain(ctx, ps, xres)


def prep_l0attn(w):
    n, m, nf = T, 2 * T, T + 1
    F8 = np.float64
    ext = {}
    ext["uv_w"] = np.ascontiguousarray(w["L0_attn_uv_w"])
    ext["o_w"] = np.ascontiguousarray(w["L0_attn_o_w"])
    p = np.arange(m, dtype=F8)
    rel = np.where(p <= n, p, p - m)
    rel[n] = 0.0
    ext["rpe_pos_tiles"] = np.ascontiguousarray(
        rel.reshape(32, 128).T).astype(np.float32)
    ext["rpe_wrep32"] = np.tile(w["L0_attn_rpe_pos_w"], (128, 32)).astype(np.float32)
    ext["rpe_grep32"] = np.ascontiguousarray(
        np.tile(w["L0_attn_rpe_ln_w"][:, None, :], (1, 32, 1)).reshape(1, 3 * 1024)
        * np.ones((128, 1), np.float32)).astype(np.float32)
    ext["rpe_brep32"] = np.ascontiguousarray(
        np.tile(w["L0_attn_rpe_ln_b"][:, None, :], (1, 32, 1)).reshape(1, 3 * 1024)
        * np.ones((128, 1), np.float32)).astype(np.float32)
    ext["rpe_gorep32"] = np.tile(w["L0_attn_rpe_out_ln_w"][None, :], (128, 32)).astype(np.float32)
    ext["rpe_borep32"] = np.tile(w["L0_attn_rpe_out_ln_b"][None, :], (128, 32)).astype(np.float32)
    ext["rpe_fc"] = np.ascontiguousarray(w["L0_attn_rpe_fc_w"])
    ext["rpe_outw"] = np.ascontiguousarray(w["L0_attn_rpe_out_w"])
    gamma = 0.99
    g = gamma ** np.abs(rel)
    g[n] = 0.0
    k = np.arange(NF, dtype=F8)
    angA = 2.0 * np.pi * (k[None, :] * p[:, None]) / m
    fgr = np.cos(angA) * g[:, None]
    fgi = -np.sin(angA) * g[:, None]
    fgr[:, nf:] = 0.0
    fgi[:, nf:] = 0.0
    ext["fgr"] = fgr.astype(np.float32)
    ext["fgi"] = fgi.astype(np.float32)
    t = np.arange(n, dtype=F8)
    ang = 2.0 * np.pi * (t[:, None] * k[None, :]) / m
    frt = np.cos(ang)
    fit = -np.sin(ang)
    frt[:, nf:] = 0.0
    fit[:, nf:] = 0.0
    ext["frt"] = np.ascontiguousarray(
        frt.reshape(n, NKC, 128).transpose(1, 0, 2)).astype(np.float32)
    ext["fit"] = np.ascontiguousarray(
        fit.reshape(n, NKC, 128).transpose(1, 0, 2)).astype(np.float32)
    cw = np.full(NF, 2.0, dtype=F8)
    cw[0] = 1.0
    cw[n] = 1.0
    cw[nf:] = 0.0
    grT = (np.cos(ang) * cw[None, :] / m).T
    cwi = cw.copy()
    cwi[0] = 0.0
    cwi[n] = 0.0
    giT = (-np.sin(ang) * cwi[None, :] / m).T
    ext["grt"] = np.ascontiguousarray(
        grT.reshape(NKC, 128, NT, TC).transpose(2, 0, 1, 3)).astype(np.float32)
    ext["git"] = np.ascontiguousarray(
        giT.reshape(NKC, 128, NT, TC).transpose(2, 0, 1, 3)).astype(np.float32)
    ext["ident"] = np.eye(128, dtype=np.float32)
    return ext


L0A_SHAPES = {
    "uv_w": ((D, T), F32), "o_w": ((D, D), F32),
    "rpe_pos_tiles": ((128, 32), F32), "rpe_wrep32": ((128, 1024), F32),
    "rpe_grep32": ((128, 3072), F32), "rpe_brep32": ((128, 3072), F32),
    "rpe_gorep32": ((128, 1024), F32), "rpe_borep32": ((128, 1024), F32),
    "rpe_fc": ((3, 32, 32), F32), "rpe_outw": ((32, D), F32),
    "fgr": ((2 * T, NF), F32), "fgi": ((2 * T, NF), F32),
    "frt": ((NKC, T, 128), F32), "fit": ((NKC, T, 128), F32),
    "grt": ((NT, NKC, 128, TC), F32), "git": ((NT, NKC, 128, TC), F32),
    "ident": ((128, 128), F32),
}



L0F_XBF = 32768
L0F_BDMK = 36864
L0F_BDUP = 40960
L0F_WCT = 57344
L0F_Y = 73728
L0F_Z = 77824
L0F_ACC = 86016


def phase_l0_ffn(ctx, ps, io):
    C = ctx.C
    xres = C["xres"]
    ctx.fence()
    xbf_tc = ctx.at_b(L0F_XBF, 8 * TC)
    bdmk = ctx.at_b(L0F_BDMK, 32 * 128)
    bdup = ctx.at_b(L0F_BDUP, 32 * 512)
    wct_sb = ctx.at_b(L0F_WCT, 16 * 1024)
    ybuf = ctx.at_b(L0F_Y, 8 * TC)
    zbuf = ctx.at_b(L0F_Z, 16 * TC)
    outacc = ctx.at_f(L0F_ACC, 8 * TC)

    ctx.dma("gpsimd", bdmk.rearrange("p (g n) -> p g n", g=32),
            io["bdmk"].rearrange("h c p n -> p (h c) n"), inc="gld")
    ctx.dma("gpsimd", bdup.rearrange("p (g n) -> p g n", g=32),
            io["bdup"].rearrange("h c p n -> p (h c) n"), inc="gld")
    wct_use = None
    for tc in range(NT):
        t0 = tc * TC
        for c in range(NCD):
            copy(ctx, "vector", xbf_tc[:, c * TC:(c + 1) * TC],
                 xres[:, c * T + t0:c * T + t0 + TC])
        vxb = ctx.val("dve")
        ctx.wait("tensor", "dve", vxb)
        ctx.wait("tensor", "gld")
        for h in range(4):
            ydone = {}
            for c in range(NCD):
                bk = c % 4
                if bk in ydone:
                    ctx.wait("tensor", "dve", ydone[bk])
                mm(ctx, ps[bk], bdmk[:, (h * 8 + c) * 128:(h * 8 + c + 1) * 128],
                   xbf_tc[:, c * TC:(c + 1) * TC], True, True, inc="pe")
                ctx.wait("vector", "pe")
                ydone[bk] = copy(ctx, "vector", ybuf[:, c * TC:(c + 1) * TC], ps[bk])
            for half in range(2):
                if wct_use is not None:
                    ctx.wait("gpsimd", "pe", wct_use)
                r0 = h * 4096 + half * 2048
                vw = ctx.dma("gpsimd", wct_sb.rearrange("p (z n) -> p z n", z=16),
                             io["wct"][r0:r0 + 2048, :].rearrange(
                                 "(z p) n -> p z n", p=128), inc="qa")
                zdone = {}
                for zc in range(16):
                    zg = half * 16 + zc
                    kc, sub = zg // 4, zg % 4
                    bk = zc % 4
                    if bk in zdone:
                        ctx.wait("tensor", "act", zdone[bk])
                    ctx.wait("tensor", "dve")
                    mm(ctx, ps[bk],
                       bdup[:, (h * 8 + kc) * 512 + sub * 128:(h * 8 + kc) * 512 + (sub + 1) * 128],
                       ybuf[:, kc * TC:(kc + 1) * TC], True, True, inc="pe")
                    ctx.wait("scalar", "pe")
                    zdone[bk] = act(ctx, zbuf[:, zc * TC:(zc + 1) * TC], ps[bk],
                                    AF.Gelu, inc="act")
                ctx.wait("tensor", "act")
                ctx.wait("tensor", "qa", vw)
                cdone = {}
                for m in range(NCD):
                    bk = 4 + m % 4
                    if bk in cdone:
                        ctx.wait("tensor", "dve", cdone[bk])
                    for zc in range(16):
                        mm(ctx, ps[bk],
                           wct_sb[:, zc * 1024 + m * 128:zc * 1024 + (m + 1) * 128],
                           zbuf[:, zc * TC:(zc + 1) * TC], zc == 0, zc == 15,
                           inc="pe" if zc == 15 else None)
                    ctx.wait("vector", "pe")
                    dst = outacc[:, m * TC:(m + 1) * TC]
                    if h == 0 and half == 0:
                        cdone[bk] = copy(ctx, "vector", dst, ps[bk])
                    else:
                        cdone[bk] = tt(ctx, "vector", ALU.add, dst, dst, ps[bk])
                wct_use = ctx.val("pe")
        ctx.wait("vector", "pe")
        for c in range(NCD):
            xs = xres[:, c * T + t0:c * T + t0 + TC]
            tt(ctx, "vector", ALU.add, xs, xs, outacc[:, c * TC:(c + 1) * TC])
        ctx.wait("tensor", "dve")
    ctx.fence()
    emit_ln_plain(ctx, ps, xres)


def prep_l0ffn(w):
    RM = w["L0_ffn_rail_maker"]
    RU = w["L0_ffn_rail_up_proj"]
    bdmk = np.zeros((4, 8, 128, 128), dtype=np.float32)
    bdup = np.zeros((4, 8, 128, 512), dtype=np.float32)
    for h in range(4):
        for c in range(8):
            for rho in range(8):
                r = 8 * c + rho
                bdmk[h, c, 16 * rho:16 * rho + 16, 16 * rho:16 * rho + 16] = RM[h, r]
                bdup[h, c, 16 * rho:16 * rho + 16, 64 * rho:64 * rho + 64] = \
                    RU[h, r].T
    return {
        "bdmk": bdmk, "bdup": bdup,
        "wct": np.ascontiguousarray(w["L0_ffn_w_cproj"].T),
    }


L0F_SHAPES = {
    "bdmk": ((4, 8, 128, 128), F32), "bdup": ((4, 8, 128, 512), F32),
    "wct": ((16384, D), F32),
}



def attn_pipe(ctx, ps, C, units, P4, PT2, cols, psb2, pvdone):
    def bankwait(bk):
        if bk in pvdone:
            v = pvdone[bk]
            s, v = v if isinstance(v, tuple) else ("dve", v)
            ctx.wait("tensor", s, v)

    def stA(k, u):
        bankwait(k % 4)
        u["_sv"] = u["smm"](ps[k % 4])

    def stB(k, u):
        cl = cols[:, 4 * (k % 4):4 * (k % 4) + 4]
        ctx.wait("vector", "pe", u["_sv"])
        u["_vb"] = ctx.do("vector", lambda e, o=cl[:, 0:1],
                          i=ps[k % 4][:, 0:u["nk"]]:
                          e.reduce_max(o, i, axis=AX.X, negate=True), inc="dve")

    def stC(k, u):
        cl = cols[:, 4 * (k % 4):4 * (k % 4) + 4]
        P_b = P4[k % 4]
        if k - 4 >= 0 and "_vt" in units[k - 4]:
            ctx.wait("scalar", "pe", units[k - 4]["_vt"])
        ctx.wait("scalar", "dve", u["_vb"])
        u["_ve"] = act(ctx, P_b[:, 0:u["nk"]], ps[k % 4][:, 0:u["nk"]], AF.Exp,
                       bias=cl[:, 0:1], inc="act")
        pvdone[k % 4] = ("act", u["_ve"])

    def stD(k, u):
        cl = cols[:, 4 * (k % 4):4 * (k % 4) + 4]
        P_b = P4[k % 4]
        nk = u["nk"]
        ctx.wait("vector", "act", u["_ve"])
        vm = ctx.do("vector", lambda e, o=P_b[:, 0:nk], i=P_b[:, 0:nk],
                    m=u["mask_ap"], a=cl[:, 1:2]:
                    e.scalar_tensor_tensor(o, i, 1.0, m, ALU.mult, ALU.mult,
                                           accum_out=a))
        u["_vr"] = ctx.do("vector", lambda e, o=cl[:, 2:3], i=cl[:, 1:2]:
                          e.reciprocal(o, i), inc="dve")
        u["_vm"] = u["_vr"]

    def stN(k, u):
        cl = cols[:, 4 * (k % 4):4 * (k % 4) + 4]
        P_b = P4[k % 4]
        ctx.wait("scalar", "dve", u["_vr"])
        u["_vn"] = ctx.do("scalar", lambda e, o=P_b[:, 0:u["nk"]],
                          i=P_b[:, 0:u["nk"]], s=cl[:, 2:3]:
                          e.activation(o, i, AF.Identity, scale=s), inc="act")

    def stE(k, u):
        P_b = P4[k % 4]
        psb = psb2[k % 2]
        nk = u["nk"]
        ctx.wait("tensor", "act", u["_vn"])
        pk = 6 + (k % 2)
        bankwait(pk)
        if nk < 128:
            ctx.do("tensor", lambda e, o=psb[0:nk, 0:128], i=P_b[:, 0:nk],
                   d=C["ident"]: e.transpose(o, i, d), inc="pe")
        else:
            for kk in range(nk // 128):
                ctx.do("tensor", lambda e, o=psb[:, kk * 128:(kk + 1) * 128],
                       i=P_b[:, kk * 128:(kk + 1) * 128], d=C["ident"]:
                       e.transpose(o, i, d),
                       inc="pe" if kk == nk // 128 - 1 else None)
        u["_vt"] = ctx.val("pe")

    def stF(k, u):
        PT_b = PT2[k % 2]
        psb = psb2[k % 2]
        nk = u["nk"]
        eng = "vector" if k % 2 == 0 else "scalar"
        sem = "dve" if k % 2 == 0 else "act"
        if k - 2 >= 0 and "_vg" in units[k - 2]:
            ctx.wait(eng, "pe", units[k - 2]["_vg"])
        ctx.wait(eng, "pe", u["_vt"])
        def cp(e, o, i):
            return e.tensor_copy(o, i) if hasattr(e, "tensor_copy") else e.copy(o, i)
        if nk < 128:
            u["_vf"] = ctx.do(eng, lambda e, o=PT_b[0:nk, 0:128],
                              i=psb[0:nk, 0:128]: cp(e, o, i), inc=sem)
        else:
            u["_vf"] = ctx.do(eng, lambda e, o=PT_b[:, 0:nk], i=psb[:, 0:nk]:
                              cp(e, o, i), inc=sem)
        u["_vfs"] = sem
        pvdone[6 + (k % 2)] = (sem, u["_vf"])

    def stG(k, u):
        PT_b = PT2[k % 2]
        nk = u["nk"]
        ob = 4 + (k % 2)
        bankwait(ob)
        ctx.wait("tensor", u["_vfs"], u["_vf"])
        op = u["out_p"]
        if nk < 128:
            mm(ctx, ps[ob][0:op, 0:128], u["vsl"][0], PT_b[0:nk, 0:128],
               True, True, inc="pe")
        else:
            for kk in range(nk // 128):
                mm(ctx, ps[ob][0:op, 0:128], u["vsl"][kk],
                   PT_b[:, kk * 128:(kk + 1) * 128], kk == 0,
                   kk == nk // 128 - 1, inc="pe" if kk == nk // 128 - 1 else None)
        u["_vg"] = ctx.val("pe")

    def stH(k, u):
        ob = 4 + (k % 2)
        ctx.wait("scalar", "pe", u["_vg"])
        vH = ctx.do("scalar", lambda e, o=u["out_ap"],
                    i=ps[ob][0:u["out_p"], 0:128]: e.copy(o, i), inc="act")
        pvdone[ob] = ("act", vH)

    stages = [stA, stB, stC, stD, stN, stE, stF, stG, stH]
    n = len(units)
    for k in range(n + len(stages) - 1):
        for j in reversed(range(len(stages))):
            i = k - j
            if 0 <= i < n:
                stages[j](i, units[i])



L1A_LOC = 32768
L1A_WQKV = 49152
L1A_QBLK = 73728
L1A_KRING = 75776
L1A_VRING = 79872
L1A_XB = 83968
L1A_MASKS = 86016
L1A_SCR = 88064
L1A_GPK = 91648
L1A_GL = 92160
L1A_GR = 96256
L1A_ESEL = 100352
L1A_GQT = 49152
L1A_WG = 57344
L1A_REMT = 65536
L1A_PX = 73728
L1A_XBNT = 78848


def phase_l1_attn(ctx, ps, io):
    C = ctx.C
    xres = C["xres"]
    psb = C["psb"]
    psb2 = [psb, ps[7].bitcast(BF16)]
    ctx.fence()
    localT = ctx.at_b(L1A_LOC, 16384)
    wq_sb = ctx.at_b(L1A_WQKV, 8192)
    wk_sb = ctx.at_b(L1A_WQKV + 8192, 8192)
    wv_sb = ctx.at_b(L1A_WQKV + 16384, 8192)
    qblk = ctx.at_b(L1A_QBLK, 2048)
    kring = [ctx.at_b(L1A_KRING, 2048), ctx.at_b(L1A_KRING + 2048, 2048)]
    vring = [ctx.at_b(L1A_VRING, 2048), ctx.at_b(L1A_VRING + 2048, 2048)]
    xb_blk = ctx.at_b(L1A_XB, 2048)
    masksL = ctx.at_b(L1A_MASKS, 2048)
    scf = ctx.at_f(L1A_SCR, 512)
    scf2 = ctx.at_f(L1A_SCR + 1024, 512)
    P_b = ctx.at_b(L1A_SCR + 2048, 512)
    PT_b = ctx.at_b(L1A_SCR + 2560, 512)
    P_bB = ctx.at_b(L1A_XB, 512)
    PT_bB = ctx.at_b(L1A_XB + 512, 512)
    P4A = [P_b, P_bB, ctx.at_b(L1A_SCR, 512), ctx.at_b(L1A_SCR + 512, 512)]
    cols16 = ctx.at_f(91904, 16)
    remo_c = ctx.at_b(L1A_SCR + 3072, 512)
    cols = ctx.at_f(91904, 8)
    gpl_sb = ctx.at_b(L1A_GPK, 128)
    gpr_sb = ctx.at_b(L1A_GPK + 128, 128)
    glsig = ctx.at_f(L1A_GL, 2048)[0:16, :]
    grsig = ctx.at_f(L1A_GR, 2048)[0:16, :]
    esel = ctx.at_f(L1A_ESEL, 1024)[0:16, :]

    ctx.dma("gpsimd", wq_sb.rearrange("p (c t) -> p c t", c=NCD),
            io["wq"].rearrange("(c p) t -> p c t", p=128), inc="gld")
    ctx.dma("gpsimd", wk_sb.rearrange("p (c t) -> p c t", c=NCD),
            io["wk"].rearrange("(c p) t -> p c t", p=128), inc="gld")
    ctx.dma("gpsimd", wv_sb.rearrange("p (c t) -> p c t", c=NCD),
            io["wv"].rearrange("(c p) t -> p c t", p=128), inc="gld")
    ctx.dma("gpsimd", masksL.rearrange("p (v q f) -> p v q f", v=2, q=2),
            io["lmask"].rearrange("v q p f -> p v q f"), inc="gld")
    ctx.dma("gpsimd", gpl_sb.rearrange("p (c n) -> p c n", c=NCD),
            io["gpackl"].rearrange("(c p) n -> p c n", p=128), inc="gld")
    ctx.dma("gpsimd", gpr_sb.rearrange("p (c n) -> p c n", c=NCD),
            io["gpackr"].rearrange("(c p) n -> p c n", p=128), inc="gld")
    ctx.dma("sync", esel.rearrange("p (c n) -> p c n", c=8),
            io["esel"].rearrange("c h n -> h c n"), inc="ld")
    ctx.wait("tensor", "gld")
    ctx.wait("tensor", "ld")

    pvdone = {}

    def bankwait(bk):
        if bk in pvdone:
            v = pvdone[bk]
            s, v = v if isinstance(v, tuple) else ("dve", v)
            ctx.wait("tensor", s, v)

    for blk in range(8):
        t0 = blk * 256
        ctx.wait("vector", "pe")
        for k in range(NCD):
            copy(ctx, "vector", xb_blk[:, k * 256:(k + 1) * 256],
                 xres[:, k * T + t0:k * T + t0 + 256])
        ctx.wait("tensor", "dve")
        for dst, wsb in ((qblk, wq_sb), (kring[blk % 2], wk_sb)):
            qscaled = dst is qblk
            for m in range(NCD):
                bk = m % 4
                bankwait(bk)
                for k in range(NCD):
                    mm(ctx, ps[bk][:, 0:256],
                       wsb[:, k * 1024 + m * 128:k * 1024 + (m + 1) * 128],
                       xb_blk[:, k * 256:(k + 1) * 256], k == 0, k == NCD - 1,
                       inc="pe" if k == NCD - 1 else None)
                if qscaled:
                    ctx.wait("scalar", "pe")
                    pvdone[bk] = ("act", ctx.do(
                        "scalar", lambda e, o=dst[:, m * 256:(m + 1) * 256],
                        i=ps[bk][:, 0:256]:
                        e.activation(o, i, AF.Identity, scale=0.125), inc="act"))
                else:
                    ctx.wait("vector", "pe")
                    pvdone[bk] = copy(ctx, "vector", dst[:, m * 256:(m + 1) * 256],
                                      ps[bk][:, 0:256])
        for gp, dst, bk in ((gpl_sb, glsig, 4), (gpr_sb, grsig, 5)):
            bankwait(bk)
            for k in range(NCD):
                mm(ctx, ps[bk][0:16, 0:256], gp[:, k * 16:(k + 1) * 16],
                   qblk[:, k * 256:(k + 1) * 256], k == 0, k == NCD - 1,
                   inc="pe" if k == NCD - 1 else None)
            ctx.wait("scalar", "pe")
            va = act(ctx, dst[:, t0:t0 + 256], ps[bk][0:16, 0:256], AF.Sigmoid,
                     scale=8.0)
            pvdone[bk] = None
            ctx.wait("tensor", "act", va)
            pvdone.pop(bk)
        for ti in range(2):
            for nc2 in range(2):
                bk = 2 * ti + nc2
                bankwait(bk)
                for k in range(NCD):
                    mm(ctx, ps[bk],
                       xb_blk[:, k * 256 + ti * 128:k * 256 + ti * 128 + 128],
                       wv_sb[:, k * 1024 + nc2 * 512:k * 1024 + (nc2 + 1) * 512],
                       k == 0, k == NCD - 1, inc="pe" if k == NCD - 1 else None)
            ctx.wait("vector", "pe")
            for nc2 in range(2):
                pvdone[2 * ti + nc2] = ("dve", copy(
                    ctx, "vector",
                    vring[blk % 2][:, ti * 1024 + nc2 * 512:ti * 1024 + (nc2 + 1) * 512],
                    ps[2 * ti + nc2]))
        nk = 256 if blk == 0 else 512
        nkc = nk // 128
        kprev, kcur = kring[(blk - 1) % 2], kring[blk % 2]
        vprev, vcur = vring[(blk - 1) % 2], vring[blk % 2]
        units = []
        for h in range(16):
            ch, rh = h // 2, 64 * (h % 2)
            for qt in range(2):
                lq = qblk[rh:rh + 64, ch * 256 + qt * 128:ch * 256 + qt * 128 + 128]
                kp = kprev[rh:rh + 64, ch * 256:(ch + 1) * 256]
                kc_ = kcur[rh:rh + 64, ch * 256:(ch + 1) * 256]

                def smm(pbank, lq=lq, kp=kp, kc_=kc_, blk=blk):
                    if blk == 0:
                        return mm(ctx, pbank[:, 0:256], lq, kc_, True, True,
                                  inc="pe")
                    mm(ctx, pbank[:, 0:256], lq, kp, True, True)
                    return mm(ctx, pbank[:, 256:512], lq, kc_, True, True,
                              inc="pe")
                mvar = 0 if blk == 0 else 1
                moff = 256 if blk == 0 else 0
                mslice = masksL[:, mvar * 1024 + qt * 512 + moff:
                                mvar * 1024 + qt * 512 + moff + nk]
                vsl = [(vcur if (blk == 0 or kk >= 2) else vprev)[
                    :, (kk % 2) * 1024 + h * 64:(kk % 2) * 1024 + h * 64 + 64]
                    for kk in range(nkc)]
                out_ap = localT[rh:rh + 64,
                                ch * T + t0 + qt * 128:ch * T + t0 + qt * 128 + 128]
                units.append(dict(nk=nk, smm=smm, mask_ap=mslice, scale=0.125,
                                  vsl=vsl, out_ap=out_ap, out_p=64))
        attn_pipe(ctx, ps, C, units, P4A, [PT_b, PT_bB], cols16, psb2, pvdone)
    ctx.fence()

    gqt_sb = ctx.at_b(L1A_GQT, 8192)
    wg_sb = ctx.at_b(L1A_WG, 4096)
    remT = ctx.at_b(L1A_REMT, 8192)
    pxT = ctx.at_f(L1A_PX, 512)
    pxb = ctx.at_b(L1A_PX + 1024, 512)
    gkt_sb = ctx.at_b(L1A_PX + 1536, 256)
    gv_sb = ctx.at_b(L1A_PX + 1792, 512)[0:64, :]
    latmask = ctx.at_b(L1A_PX + 2304, 1024)
    xbnt = ctx.at_b(L1A_XBNT, 4096)

    ctx.dma("gpsimd", latmask.rearrange("p (q n) -> p q n", q=16),
            io["latmask"].rearrange("q p n -> p q n"), inc="gld")
    for k in range(NCD):
        ctx.do("vector", lambda e, o=pxT[:, k * 64:(k + 1) * 64],
               i=xres[:, k * T:(k + 1) * T].rearrange("p (l w) -> p l w", w=32):
               e.reduce_sum(o, i, axis=AX.X))
    ctx.do("vector", lambda e, o=pxb, i=pxT: e.tensor_scalar_mul(o, i, 1.0 / 32),
           inc="dve")
    ctx.wait("tensor", "dve")
    ctx.dma("gpsimd", wg_sb.rearrange("p (c t) -> p c t", c=NCD),
            io["wgk"].rearrange("(c p) t -> p c t", p=128), inc="gld")
    ctx.wait("tensor", "gld")
    for mc in range(4):
        bk = mc % 2
        bankwait(bk)
        for k in range(NCD):
            mm(ctx, ps[bk][:, 0:64], wg_sb[:, k * 512 + mc * 128:k * 512 + (mc + 1) * 128],
               pxb[:, k * 64:(k + 1) * 64], k == 0, k == NCD - 1,
               inc="pe" if k == NCD - 1 else None)
        ctx.wait("vector", "pe")
        pvdone[bk] = copy(ctx, "vector", gkt_sb[:, mc * 64:(mc + 1) * 64],
                          ps[bk][:, 0:64])
    ctx.wait("gpsimd", "pe")
    ctx.dma("gpsimd", wg_sb.rearrange("p (c t) -> p c t", c=NCD),
            io["wgv"].rearrange("(c p) t -> p c t", p=128), inc="gld")
    ctx.wait("tensor", "gld")
    bankwait(2)
    for k in range(NCD):
        mm(ctx, ps[2][0:64, :], pxb[:, k * 64:(k + 1) * 64],
           wg_sb[:, k * 512:(k + 1) * 512], k == 0, k == NCD - 1,
           inc="pe" if k == NCD - 1 else None)
    ctx.wait("vector", "pe")
    pvdone[2] = copy(ctx, "vector", gv_sb, ps[2][0:64, :])
    ctx.wait("gpsimd", "pe")
    ctx.dma("gpsimd", wg_sb.rearrange("p (c t) -> p c t", c=NCD),
            io["wgq"].rearrange("(c p) t -> p c t", p=128), inc="gld")
    ctx.wait("tensor", "gld")
    for nt in range(NT):
        tn = nt * TC
        ctx.wait("vector", "pe")
        for k in range(NCD):
            copy(ctx, "vector", xbnt[:, k * TC:(k + 1) * TC],
                 xres[:, k * T + tn:k * T + tn + TC])
        vx = ctx.val("dve")
        ctx.wait("tensor", "dve", vx)
        for mc in range(4):
            bk = mc % 2
            bankwait(bk)
            for k in range(NCD):
                mm(ctx, ps[bk], wg_sb[:, k * 512 + mc * 128:k * 512 + (mc + 1) * 128],
                   xbnt[:, k * TC:(k + 1) * TC], k == 0, k == NCD - 1,
                   inc="pe" if k == NCD - 1 else None)
            ctx.wait("scalar", "pe")
            pvdone[bk] = ("act", ctx.do(
                "scalar", lambda e, o=gqt_sb[:, mc * T + tn:mc * T + tn + TC],
                i=ps[bk]: e.activation(o, i, AF.Identity, scale=0.125),
                inc="act"))
    units = []
    for lh in range(8):
        rh = 64 * (lh % 2)
        mc = lh // 2
        for qt in range(16):
            gq = gqt_sb[rh:rh + 64, mc * T + qt * 128:mc * T + qt * 128 + 128]
            gk = gkt_sb[rh:rh + 64, mc * 64:(mc + 1) * 64]

            def smm(pbank, gq=gq, gk=gk):
                return mm(ctx, pbank[:, 0:64], gq, gk, True, True, inc="pe")
            lm = latmask[:, qt * 64:(qt + 1) * 64]
            out_ap = remT[rh:rh + 64, mc * T + qt * 128:mc * T + qt * 128 + 128]
            units.append(dict(nk=64, smm=smm, mask_ap=lm, scale=0.125,
                              vsl=[gv_sb[:, lh * 64:(lh + 1) * 64]],
                              out_ap=out_ap, out_p=64))
    attn_pipe(ctx, ps, C, units, P4A, [PT_b, PT_bB], cols16, psb2, pvdone)
    ctx.fence()

    rw_sb = ctx.at_b(L1A_WG, 4096)
    ctx.dma("gpsimd", rw_sb.rearrange("p (c t) -> p c t", c=4),
            io["r_out_w"].rearrange("(c p) t -> p c t", p=128), inc="gld")
    ctx.wait("tensor", "gld")
    for c in range(NCD):
        for nt in range(NT):
            tn = nt * TC
            bk0 = (c * NT + nt) % 2
            bankwait(bk0)
            for kk in range(4):
                mm(ctx, ps[bk0], rw_sb[:, kk * 1024 + c * 128:kk * 1024 + (c + 1) * 128],
                   remT[:, kk * T + tn:kk * T + tn + TC], kk == 0, kk == 3)
            bankwait(2)
            mm(ctx, ps[2], esel[:, c * 128:(c + 1) * 128], glsig[:, tn:tn + TC],
               True, True)
            bankwait(3)
            mm(ctx, ps[3], esel[:, c * 128:(c + 1) * 128], grsig[:, tn:tn + TC],
               True, True, inc="pe")
            ctx.wait("vector", "pe")
            vv = copy(ctx, "vector", remo_c, ps[bk0])
            lt = localT[:, c * T + tn:c * T + tn + TC]
            tt(ctx, "vector", ALU.mult, scf, ps[2], lt)
            pvdone[2] = ctx.val("dve")
            pvdone[bk0] = pvdone[2]
            tt(ctx, "vector", ALU.mult, scf2, ps[3], remo_c)
            pvdone[3] = ctx.val("dve")
            tt(ctx, "vector", ALU.add, lt, scf, scf2, inc="dve")
    ctx.fence()

    wo_sb = ctx.at_b(L1A_GQT, 8192)
    ctx.dma("gpsimd", wo_sb.rearrange("p (c t) -> p c t", c=NCD),
            io["w_out"].rearrange("(c p) t -> p c t", p=128), inc="gld")
    ctx.wait("tensor", "gld")
    resdone = {}
    for m in range(NCD):
        for nt in range(NT):
            bk = (m * NT + nt) % 4
            if bk in resdone:
                ctx.wait("tensor", "dve", resdone[bk])
            for k in range(NCD):
                mm(ctx, ps[bk], wo_sb[:, k * 1024 + m * 128:k * 1024 + (m + 1) * 128],
                   localT[:, k * T + nt * TC:k * T + (nt + 1) * TC],
                   k == 0, k == NCD - 1, inc="pe" if k == NCD - 1 else None)
            ctx.wait("vector", "pe")
            xs = xres[:, m * T + nt * TC:m * T + (nt + 1) * TC]
            resdone[bk] = tt(ctx, "vector", ALU.add, xs, xs, ps[bk], inc="dve")
    ctx.fence()
    emit_ln_plain(ctx, ps, xres)


def prep_l1attn(w):
    ext = {
        "wq": np.ascontiguousarray(w["L1_attn_wq"]),
        "wk": np.ascontiguousarray(w["L1_attn_wk"]),
        "wv": np.ascontiguousarray(w["L1_attn_wv"]),
        "wgq": np.ascontiguousarray(w["L1_attn_wgq"]),
        "wgk": np.ascontiguousarray(w["L1_attn_wgk"]),
        "wgv": np.ascontiguousarray(w["L1_attn_wgv"]),
        "r_out_w": np.ascontiguousarray(w["L1_attn_r_out_w"]),
        "w_out": np.ascontiguousarray(w["L1_attn_w_out"]),
    }
    gpl = np.zeros((1024, 16), dtype=np.float32)
    gpr = np.zeros((1024, 16), dtype=np.float32)
    for h in range(16):
        gpl[64 * h:64 * h + 64, h] = w["L1_attn_gate_local_w"][h]
        gpr[64 * h:64 * h + 64, h] = w["L1_attn_gate_remote_w"][h]
    ext["gpackl"] = gpl
    ext["gpackr"] = gpr
    i = np.arange(256)[:, None]
    j = np.arange(512)[None, :]
    base = (j > i) & (j <= i + 256)
    m1 = base.astype(np.float32).reshape(2, 128, 512)
    m0 = (base & (j >= 256)).astype(np.float32).reshape(2, 128, 512)
    ext["lmask"] = np.stack([m0, m1])
    tq = np.arange(2048)
    lv = (np.arange(64)[None, :] <= (tq // 32)[:, None])
    ext["latmask"] = lv.astype(np.float32).reshape(16, 128, 64)
    es = np.zeros((8, 16, 128), dtype=np.float32)
    for c in range(8):
        for p in range(128):
            es[c, (128 * c + p) // 64, p] = 1.0
    ext["esel"] = es
    return ext


L1A_SHAPES = {
    "wq": ((D, D), F32), "wk": ((D, D), F32), "wv": ((D, D), F32),
    "wgq": ((D, 512), F32), "wgk": ((D, 512), F32), "wgv": ((D, 512), F32),
    "r_out_w": ((512, D), F32), "w_out": ((D, D), F32),
    "gpackl": ((D, 16), F32), "gpackr": ((D, 16), F32),
    "lmask": ((2, 2, 128, 512), F32), "latmask": ((16, 128, 64), F32),
    "esel": ((8, 16, 128), F32),
}



LF_ATT = 32768
LF_WQKV = 49152
LF_XSB = 73728
LF_QBLK = 75776
LF_KRING = 77824
LF_VRING = 81920
LF_SQ = 86016
LF_RN = 88064
LF_QN = 88576
LF_SCF = 89088
LF_COLS = 90112
LF_BCOL = 90176
LF_HPK = 90240
LF_ROTB = 90368
LF_ESEL = 90496
LF_MASK = 91520
LF_CS = 93568
LF_UT = 73728
LF_SQS = 81920
LF_B1C = 84096


def phase_l1_ffn(ctx, ps, io):
    C = ctx.C
    xres = C["xres"]
    psb = C["psb"]
    psb2 = [psb, ps[7].bitcast(BF16)]
    ctx.fence()
    attT = ctx.at_b(LF_ATT, 16384)
    wqkv = ctx.at_b(LF_WQKV, 24576)
    xs_blk = ctx.at_b(LF_XSB, 2048)
    qblk = ctx.at_b(LF_QBLK, 2048)
    kbuf = ctx.at_b(LF_KRING, 4096)
    vring = [ctx.at_b(LF_VRING, 2048), ctx.at_b(LF_VRING + 2048, 2048)]
    sqb = ctx.at_b(LF_SQ, 2048)
    maskv0 = ctx.at_b(101760, 512)
    rnorm = ctx.at_f(LF_RN, 256)[0:16, :]
    PT_b = ctx.at_b(LF_RN, 512)
    qn_b = ctx.at_b(LF_QN, 256)
    t1_b = ctx.at_b(LF_QN + 256, 256)
    P_b = ctx.at_b(LF_QN, 512)
    scf = ctx.at_f(LF_SCF, 512)
    scfB = ctx.at_f(LF_SQ, 512)
    P_bB = ctx.at_b(LF_SQ + 1024, 512)
    PT_bB = ctx.at_b(LF_SQ + 1536, 512)
    P4F = [P_b, P_bB, ctx.at_b(LF_SCF, 512), ctx.at_b(LF_SQ, 512)]
    cols16F = ctx.at_f(LF_SCF + 512, 16)
    cols = ctx.at_f(LF_COLS, 8)
    bcol = ctx.at_f(LF_BCOL, 32)
    hpack = ctx.at_b(LF_HPK, 128)
    rotb = ctx.at_b(LF_ROTB, 128)
    eselb = ctx.at_b(LF_ESEL, 1024)[0:16, :]
    lmask_b = ctx.at_b(LF_MASK, 2048)
    css = [ctx.at_b(LF_CS + i * 2048, 2048) for i in range(4)]

    ctx.dma("gpsimd", css[0], io["cs_t"], inc="gld")
    ctx.dma("gpsimd", css[1], io["ss_t"], inc="gld")
    ctx.dma("gpsimd", css[2], io["ci_t"], inc="gld")
    ctx.dma("gpsimd", css[3], io["si_t"], inc="gld")
    ctx.dma("gpsimd", hpack.rearrange("p (c n) -> p c n", c=8),
            io["hpack"].rearrange("c p n -> p c n"), inc="gld")
    ctx.dma("gpsimd", rotb, io["rotb"], inc="gld")
    ctx.dma("gpsimd", lmask_b.rearrange("p (v q f) -> p v q f", v=2, q=2),
            io["fmask"].rearrange("v q p f -> p v q f"), inc="gld")
    ctx.dma("gpsimd", maskv0.rearrange("p (q f) -> p q f", q=2),
            io["fmask0"].rearrange("q p f -> p q f"), inc="gld")
    ctx.dma("gpsimd", eselb.rearrange("p (c n) -> p c n", c=8),
            io["esel"].rearrange("c h n -> h c n"), inc="gld")
    ctx.wait("tensor", "gld")
    ctx.wait("sync", "dve")
    vspill = ctx.dma("sync", io["spill_x"].rearrange("(c p) t -> p c t", p=128),
                     xres.rearrange("p (c t) -> p c t", c=NCD), inc="ld")

    pvdone = {}

    def bankwait(bk):
        if bk in pvdone:
            v = pvdone[bk]
            s, v = v if isinstance(v, tuple) else ("dve", v)
            ctx.wait("tensor", s, v)

    CH = [(0, 512), (512, 1024), (1024, 1536), (1536, 2048), (2048, 2560),
          (2560, 2730)]

    for li in range(6):
        ctx.fence()
        pvdone.clear()
        xsT = ctx.at_b(49152, 16384)
        attTf = ctx.at_b(49152, 16384)
        QTf = ctx.at_b(65536, 16384)
        KTf = ctx.at_b(0, 16384)
        Vf = ctx.at_b(16384, 16384)
        slab = ctx.at_b(81920, 8192)
        sqp = [ctx.at_b(86016, 2048), ctx.at_b(88064, 2048)]
        rn16 = ctx.at_f(16384, 2048)[0:16, :]
        rnb16 = ctx.at_b(20480, 2048)[0:16, :]
        qn2w = [ctx.at_b(22528, 512), ctx.at_b(23040, 512)]
        t12w = [ctx.at_b(23552, 512), ctx.at_b(24064, 512)]
        scfw = [ctx.at_f(24576, 512), ctx.at_f(25600, 512)]
        P4w = [ctx.at_b(86016 + i * 512, 512) for i in range(4)]
        PT2w = [ctx.at_b(88064, 512), ctx.at_b(88576, 512)]
        colsw = ctx.at_f(89088, 16)
        ctx.dma("sync", bcol[:, 0:1], io["qsrep"][li], inc="ld")
        ctx.dma("sync", bcol[:, 1:2], io["ksrep"][li], inc="ld")
        for k in range(NCD):
            dst = xsT[:, k * T:(k + 1) * T]
            if k < 4:
                copy(ctx, "vector", dst, xres[:, k * T:(k + 1) * T])
            else:
                ctx.do("vector", lambda e, o=dst[:, 0:1]: e.memset(o, 0.0))
                copy(ctx, "vector", dst[:, 1:T], xres[:, k * T:k * T + T - 1])
        vxs = ctx.val("dve")
        ctx.wait("tensor", "dve", vxs)
        ctx.wait("sync", "dve", vxs)
        vspl = ctx.dma("sync", io["spill_x2"].rearrange("(c p) t -> p c t", p=128),
                       xres.rearrange("p (c t) -> p c t", c=NCD), inc="ld")
        ctx.wait("gpsimd", "pe")
        ctx.wait("gpsimd", "dve")
        ctx.wait("gpsimd", "act")
        for part, dstf in ((0, QTf), (1, KTf)):
            qs2 = "qa" if part == 0 else "qb"
            if part == 1:
                ctx.wait("gpsimd", "pe")
            vw = ctx.dma("gpsimd", slab.rearrange("p (c t) -> p c t", c=NCD),
                         io["Wqkv"][li].rearrange("(c p) t -> p c t", p=128)[
                             :, :, part * 1024:(part + 1) * 1024], inc=qs2)
            ctx.wait("tensor", qs2, vw)
            if part == 1:
                ctx.wait("vector", "ld", vspl)
            for m in range(NCD):
                for nt in range(NT):
                    bk = (m * NT + nt) % 4
                    bankwait(bk)
                    for k in range(NCD):
                        mm(ctx, ps[bk], slab[:, k * 1024 + m * 128:k * 1024 + (m + 1) * 128],
                           xsT[:, k * T + nt * TC:k * T + (nt + 1) * TC],
                           k == 0, k == NCD - 1, inc="pe" if k == NCD - 1 else None)
                    ctx.wait("vector", "pe")
                    pvdone[bk] = copy(
                        ctx, "vector",
                        dstf[:, m * T + nt * TC:m * T + (nt + 1) * TC], ps[bk])
        ctx.wait("scalar", "ld", vspl)
        for which in range(2):
            tgtf = QTf if which == 0 else KTf
            csAf = css[0] if which == 0 else css[2]
            csBf = css[1] if which == 0 else css[3]
            scol = bcol[:, 0:1] if which == 0 else bcol[:, 1:2]
            ctx.wait("scalar", "dve")
            ctx.wait("scalar", "pe")
            sqd = {}
            for bb in (4, 5, 6, 7):
                bankwait(bb)
            for c in range(NCD):
                sq = sqp[c % 2]
                if (c % 2) in sqd:
                    ctx.wait("scalar", "pe", sqd[c % 2])
                vsq = act(ctx, sq, tgtf[:, c * T:(c + 1) * T], AF.Square,
                          inc="act")
                ctx.wait("tensor", "act", vsq)
                for nt in range(NT):
                    sqd[c % 2] = mm(
                        ctx, ps[4 + nt][0:16, 0:TC], hpack[:, c * 16:(c + 1) * 16],
                        sq[:, nt * TC:(nt + 1) * TC], c == 0, c == NCD - 1,
                        inc="pe")
            ctx.wait("vector", "pe")
            for nt in range(NT):
                copy(ctx, "vector", rn16[:, nt * TC:(nt + 1) * TC],
                     ps[4 + nt][0:16, 0:TC],
                     inc="dve" if nt == NT - 1 else None)
            ctx.do("vector", lambda e, o=rn16, i=rn16:
                   e.tensor_scalar_max(o, i, 1e-24), inc="dve")
            for bb in (4, 5, 6, 7):
                pvdone[bb] = ctx.val("dve")
            ctx.wait("scalar", "dve")
            act(ctx, rn16, rn16, AF.Ln)
            va = act(ctx, rn16, rn16, AF.Exp, scale=-0.5, inc="act")
            ctx.wait("vector", "act", va)
            vrb = copy(ctx, "vector", rnb16, rn16, inc="dve")
            ctx.wait("tensor", "dve", vrb)
            st_r = {}

            def R1(u):
                c, nt = u // NT, u % NT
                bankwait(4 + u % 2)
                st_r[("vr1", u)] = mm(ctx, ps[4 + u % 2], eselb[:, c * 128:(c + 1) * 128],
                                      rnb16[:, nt * TC:(nt + 1) * TC], True, True,
                                      inc="pe")

            def R23(u):
                c, nt = u // NT, u % NT
                par = u % 2
                vq = ctx.do("vector", lambda e, o=qn2w[par],
                            i=tgtf[:, c * T + nt * TC:c * T + (nt + 1) * TC],
                            s=scol: e.tensor_scalar_mul(o, i, s), inc="dve")
                ctx.wait("tensor", "dve", vq)
                bankwait(6 + par)
                st_r[("vrot", u)] = mm(ctx, ps[6 + par], rotb, qn2w[par],
                                       True, True, inc="pe")

            def R4(u):
                c, nt = u // NT, u % NT
                par = u % 2
                sc2 = scfw[par]
                tt(ctx, "vector", ALU.mult, t12w[par], qn2w[par],
                   csAf[:, nt * TC:(nt + 1) * TC])
                ctx.wait("vector", "pe", st_r[("vrot", u)])
                tt(ctx, "vector", ALU.mult, sc2, ps[6 + par],
                   csBf[:, nt * TC:(nt + 1) * TC])
                pvdone[6 + par] = ctx.val("dve")
                tt(ctx, "vector", ALU.add, sc2, t12w[par], sc2)
                ctx.wait("vector", "pe", st_r[("vr1", u)])
                tt(ctx, "vector", ALU.mult,
                   tgtf[:, c * T + nt * TC:c * T + (nt + 1) * TC],
                   sc2, ps[4 + par], inc="dve")
                pvdone[4 + par] = ctx.val("dve")

            NU = NCD * NT
            for u in range(NU + 1):
                if u < NU:
                    R1(u)
                    R23(u)
                if u >= 1:
                    R4(u - 1)
        ctx.wait("gpsimd", "dve")
        ctx.wait("gpsimd", "pe")
        vw = ctx.dma("gpsimd", slab.rearrange("p (c t) -> p c t", c=NCD),
                     io["Wqkv"][li].rearrange("(c p) t -> p c t", p=128)[
                         :, :, 2048:3072], inc="qa")
        ctx.wait("tensor", "qa", vw)
        for tt_i in range(16):
            for nc2 in range(2):
                bk = (tt_i * 2 + nc2) % 4
                bankwait(bk)
                for k in range(NCD):
                    mm(ctx, ps[bk],
                       xsT[:, k * T + tt_i * 128:k * T + tt_i * 128 + 128],
                       slab[:, k * 1024 + nc2 * 512:k * 1024 + (nc2 + 1) * 512],
                       k == 0, k == NCD - 1, inc="pe" if k == NCD - 1 else None)
                ctx.wait("vector", "pe")
                pvdone[bk] = copy(
                    ctx, "vector",
                    Vf[:, tt_i * 1024 + nc2 * 512:tt_i * 1024 + (nc2 + 1) * 512],
                    ps[bk])
        units = []
        for blk in range(8):
            t0 = blk * 256
            nk = 256 if blk == 0 else 512
            k0 = 0 if blk == 0 else (blk - 1) * 256
            for h in range(16):
                ch, rh = h // 2, 64 * (h % 2)
                for qt in range(2):
                    lq = QTf[rh:rh + 64,
                             ch * T + t0 + qt * 128:ch * T + t0 + qt * 128 + 128]
                    krhs = KTf[rh:rh + 64, ch * T + k0:ch * T + k0 + nk]

                    def smm(pbank, lq=lq, krhs=krhs, nk=nk):
                        return mm(ctx, pbank[:, 0:nk], lq, krhs, True, True,
                                  inc="pe")
                    if blk == 0:
                        msl = maskv0[:, qt * 256:qt * 256 + 256]
                    else:
                        msl = lmask_b[:, 1024 + qt * 512:1024 + qt * 512 + nk]
                    vt0 = 0 if blk == 0 else 2 * (blk - 1)
                    vsl = [Vf[:, (vt0 + kk) * 512 + (0 if True else 0):0] if False
                           else Vf[:, (vt0 + kk) // 2 * 1024 + ((vt0 + kk) % 2) * 512:0]
                           if False else
                           Vf[:, (k0 + kk * 128) // 128 * 0:0]
                           for kk in range(0)]
                    vsl = []
                    for kk in range(nk // 128):
                        tok = k0 + kk * 128
                        ti2 = tok // 128
                        vsl.append(Vf[:, (ti2 // 2) * 1024 + (ti2 % 2) * 512 + 0:0]
                                   if False else
                                   Vf[:, (ti2 // 2) * 1024 + (ti2 % 2) * 512 + h * 0:0]
                                   if False else None)
                    vsl = []
                    for kk in range(nk // 128):
                        tok = k0 + kk * 128
                        vti = tok // 128
                        base = (vti // 2) * 1024 + 0
                        vsl.append(Vf[:, vti * 1024 + h * 64:vti * 1024 + h * 64 + 64])
                    out_ap = attTf[rh:rh + 64,
                                   ch * T + t0 + qt * 128:ch * T + t0 + qt * 128 + 128]
                    units.append(dict(nk=nk, smm=smm, mask_ap=msl,
                                      vsl=vsl, out_ap=out_ap, out_p=64))
        psb2w = [ps[6].bitcast(BF16), ps[7].bitcast(BF16)]
        attn_pipe(ctx, ps, C, units, P4w, PT2w, colsw, psb2w, pvdone)
        ctx.wait("sync", "dve")
        ctx.wait("sync", "pe")
        vxr = ctx.dma("sync", xres.rearrange("p (c t) -> p c t", c=NCD),
                      io["spill_x2"].rearrange("(c p) t -> p c t", p=128),
                      inc="ld")
        ctx.wait("vector", "ld", vxr)
        ctx.wait("tensor", "ld", vxr)
        ctx.fence()
        pvdone.clear()
        wo_sb = ctx.at_b(65536, 8192)
        ctx.dma("gpsimd", wo_sb.rearrange("p (c t) -> p c t", c=NCD),
                io["Wo"][li].rearrange("(c p) t -> p c t", p=128), inc="gld")
        ctx.wait("tensor", "gld")
        if li == 0:
            ctx.wait("vector", "ld", vspill)
        for m in range(NCD):
            for nt in range(NT):
                bk = (m * NT + nt) % 4
                bankwait(bk)
                for k in range(NCD):
                    mm(ctx, ps[bk], wo_sb[:, k * 1024 + m * 128:k * 1024 + (m + 1) * 128],
                       attTf[:, k * T + nt * TC:k * T + (nt + 1) * TC],
                       k == 0, k == NCD - 1, inc="pe" if k == NCD - 1 else None)
                ctx.wait("vector", "pe")
                xs = xres[:, m * T + nt * TC:m * T + (nt + 1) * TC]
                pvdone[bk] = tt(ctx, "vector", ALU.add, xs, xs, ps[bk], inc="dve")

        ctx.fence()
        pvdone.clear()
        xnT = ctx.at_b(LF_ATT, 16384)
        reg1 = ctx.at_f(LF_WQKV, 2048)
        reg2 = ctx.at_f(LF_WQKV + 4096, 2048)
        uT_c = ctx.at_b(LF_WQKV, 8192)
        wbuf = [ctx.at_b(LF_WQKV + 8192, 12288),
                ctx.at_b(LF_WQKV + 20480, 12288)]
        sqs = [ctx.at_b(LF_SQS, 512), ctx.at_b(LF_SQS + 512, 512)]
        b1c = ctx.at_f(LF_B1C, 8)
        ctx.dma("sync", bcol[:, 4:12], io["lng8"][li], inc="ld")
        ctx.dma("sync", bcol[:, 12:20], io["lnb8"][li], inc="ld")
        ctx.dma("sync", bcol[:, 20:28], io["b28"][li], inc="ld")
        vld = ctx.val("ld")
        ctx.wait("vector", "ld", vld)
        ctx.wait("scalar", "ld", vld)

        CHM = [([(0, 128), (128, 256), (256, 384), (384, 512)] if c1 - c0 == 512
                else [(0, 128), (128, 170)]) for c0, c1 in CH]

        def load_chunk(ci):
            c0, c1 = CH[ci]
            w = c1 - c0
            wb = wbuf[ci % 2]
            qsem = "qa" if ci % 2 == 0 else "qb"
            ctx.dma("gpsimd",
                    wb[:, 0:4096].rearrange("p (c t) -> p c t", c=NCD)[:, :, 0:w],
                    io["W1"][li].rearrange("(c p) t -> p c t", p=128)[:, :, c0:c1],
                    inc=qsem)
            ctx.dma("gpsimd",
                    wb[:, 4096:8192].rearrange("p (c t) -> p c t", c=NCD)[:, :, 0:w],
                    io["W1"][li].rearrange("(c p) t -> p c t", p=128)[
                        :, :, 2730 + c0:2730 + c1], inc=qsem)
            for kk, (k0, k1) in enumerate(CHM[ci]):
                ctx.dma("gpsimd",
                        wb[0:k1 - k0, 8192 + kk * 1024:8192 + (kk + 1) * 1024],
                        io["W2"][li][c0 + k0:c0 + k1, :], inc=qsem)
            return ctx.val(qsem)

        wready = {0: load_chunk(0)}
        sqdone = {}
        for half in range(2):
            for nt in range(NT):
                tn = nt * TC
                for c4 in range(4):
                    c = half * 4 + c4
                    sq = sqs[c4 % 2]
                    if (c4 % 2) in sqdone:
                        ctx.wait("scalar", "pe", sqdone[c4 % 2])
                    vsq = act(ctx, sq, xres[:, c * T + tn:c * T + tn + TC],
                              AF.Square, inc="act")
                    ctx.wait("tensor", "act", vsq)
                    mm(ctx, ps[0][0:1, 0:TC], C["onesf"],
                       xres[:, c * T + tn:c * T + tn + TC], c4 == 0, c4 == 3)
                    sqdone[c4 % 2] = mm(ctx, ps[1][0:1, 0:TC], C["onesb"], sq,
                                        c4 == 0, c4 == 3, inc="pe")
                ctx.wait("vector", "pe")
                rS = reg1[0:1, :] if half == 0 else reg1[32:33, :]
                rQ = reg1[64:65, :] if half == 0 else reg1[96:97, :]
                copy(ctx, "vector", rS[:, tn:tn + TC], ps[0][0:1, 0:TC])
                copy(ctx, "vector", rQ[:, tn:tn + TC], ps[1][0:1, 0:TC],
                     inc="dve")
                ctx.wait("tensor", "dve")
        copy(ctx, "vector", reg2[0:1, 1:T], reg1[32:33, 0:T - 1])
        tt(ctx, "vector", ALU.add, reg2[0:1, 1:T], reg2[0:1, 1:T],
           reg1[0:1, 1:T])
        copy(ctx, "vector", reg2[0:1, 0:1], reg1[0:1, 0:1])
        copy(ctx, "vector", reg2[64:65, 1:T], reg1[96:97, 0:T - 1])
        tt(ctx, "vector", ALU.add, reg2[64:65, 1:T], reg2[64:65, 1:T],
           reg1[64:65, 1:T])
        copy(ctx, "vector", reg2[64:65, 0:1], reg1[64:65, 0:1])
        ctx.do("vector", lambda e, o=reg2[0:1, :], i=reg2[0:1, :]:
               e.tensor_scalar_mul(o, i, 1.0 / D))
        ctx.do("vector", lambda e, o=reg2[64:65, :], i=reg2[64:65, :]:
               e.tensor_scalar_mul(o, i, 1.0 / D))
        tt(ctx, "vector", ALU.mult, reg1[64:65, :], reg2[0:1, :], reg2[0:1, :])
        tt(ctx, "vector", ALU.subtract, reg2[64:65, :], reg2[64:65, :],
           reg1[64:65, :], inc="dve")
        ctx.wait("scalar", "dve")
        act(ctx, reg2[64:65, :], reg2[64:65, :], AF.Ln, bias=C["eps64"])
        va = act(ctx, reg1[64:65, :], reg2[64:65, :], AF.Exp, scale=-0.5,
                 inc="act")
        ctx.wait("vector", "act", va)
        copy(ctx, "vector", reg1[0:1, :], reg1[64:65, :], inc="dve")
        ctx.wait("tensor", "act")
        ctx.wait("tensor", "dve")
        def ffn_bcast(nt):
            tn = nt * TC
            b0, b1 = (2, 3) if nt % 2 == 0 else (6, 7)
            bankwait(b0)
            mm(ctx, ps[b0], C["ones1f"], reg2[0:1, tn:tn + TC], True, True)
            bankwait(b1)
            mm(ctx, ps[b1], C["ones1f"], reg1[0:1, tn:tn + TC], True, True,
               inc="pe")
            return ctx.val("pe")

        vbcs = {0: ffn_bcast(0), 1: ffn_bcast(1)}
        for nt in range(NT):
            tn = nt * TC
            b0, b1 = (2, 3) if nt % 2 == 0 else (6, 7)
            ctx.wait("vector", "pe", vbcs[nt])
            for c in range(NCD):
                gcol = bcol[:, 4 + c:5 + c]
                bcl = bcol[:, 12 + c:13 + c]
                dst = xnT[:, c * T + tn:c * T + tn + TC]
                if c < 4:
                    xsrc = xres[:, c * T + tn:c * T + tn + TC]
                elif nt > 0:
                    xsrc = xres[:, c * T + tn - 1:c * T + tn + TC - 1]
                else:
                    xsrc = None
                if xsrc is not None:
                    tt(ctx, "vector", ALU.subtract, scf, xsrc, ps[b0])
                    ctx.do("vector", lambda e, o=scf, i=scf, s=gcol, r=ps[b1]:
                           e.scalar_tensor_tensor(o, i, s, r, ALU.mult, ALU.mult))
                    ctx.do("vector", lambda e, o=dst, i=scf, s=bcl:
                           e.tensor_scalar_add(o, i, s), inc="dve")
                else:
                    tt(ctx, "vector", ALU.subtract, scf[:, 1:TC],
                       xres[:, c * T:c * T + TC - 1], ps[b0][:, 1:TC])
                    ctx.do("vector", lambda e, o=scf[:, 1:TC], i=scf[:, 1:TC],
                           s=gcol, r=ps[b1][:, 1:TC]:
                           e.scalar_tensor_tensor(o, i, s, r, ALU.mult, ALU.mult))
                    ctx.do("vector", lambda e, o=dst[:, 1:TC], i=scf[:, 1:TC],
                           s=bcl: e.tensor_scalar_add(o, i, s))
                    copy(ctx, "vector", cols[:, 3:4], ps[b0][:, 0:1])
                    tt(ctx, "vector", ALU.mult, cols[:, 3:4], cols[:, 3:4],
                       ps[b1][:, 0:1])
                    ctx.do("vector", lambda e, o=cols[:, 4:5], i=cols[:, 3:4],
                           s=gcol: e.tensor_scalar_mul(o, i, s))
                    ctx.do("vector", lambda e, o=cols[:, 4:5], i=cols[:, 4:5]:
                           e.tensor_scalar_mul(o, i, -1.0))
                    ctx.do("vector", lambda e, o=dst[:, 0:1], i=cols[:, 4:5],
                           s=bcl: e.tensor_scalar_add(o, i, s), inc="dve")
            pvdone[b0] = ctx.val("dve")
            pvdone[b1] = pvdone[b0]
            if nt + 2 < NT:
                vbcs[nt + 2] = ffn_bcast(nt + 2)
        ctx.wait("tensor", "dve")
        ctx.wait("gpsimd", "dve")
        ctx.wait("gpsimd", "pe")
        ctx.wait("gpsimd", "act")
        wuse = {}
        for ci, (c0, c1) in enumerate(CH):
            mcs = CHM[ci]
            wb = wbuf[ci % 2]
            qsem = "qa" if ci % 2 == 0 else "qb"
            vqa = wready.pop(ci)
            if ci + 1 < len(CH):
                if ((ci + 1) % 2) in wuse:
                    ctx.wait("gpsimd", "pe", wuse[(ci + 1) % 2])
                wready[ci + 1] = load_chunk(ci + 1)
            ctx.wait("sync", "act")
            ctx.dma("sync", b1c[:, 0:4], io["b1a_t"][li, ci], inc="ld")
            ctx.dma("sync", b1c[:, 4:8], io["b1g_t"][li, ci], inc="ld")
            vb1 = ctx.val("ld")
            ctx.wait("tensor", qsem, vqa)
            ctx.wait("scalar", "ld", vb1)
            for mi, (m0, m1) in enumerate(mcs):
                mw = m1 - m0
                for nt in range(NT):
                    tn = nt * TC
                    it = mi * NT + nt
                    bka, bkg = 4 + 2 * (it % 2) - 4 * (it % 2 and False), 5
                    bka = (4, 6)[it % 2]
                    bkg = (5, 7)[it % 2]
                    bankwait(bka)
                    for k in range(NCD):
                        mm(ctx, ps[bka][0:mw, :], wb[:, k * 512 + m0:k * 512 + m1],
                           xnT[:, k * T + tn:k * T + tn + TC], k == 0, k == NCD - 1)
                    bankwait(bkg)
                    for k in range(NCD):
                        mm(ctx, ps[bkg][0:mw, :],
                           wb[:, 4096 + k * 512 + m0:4096 + k * 512 + m1],
                           xnT[:, k * T + tn:k * T + tn + TC], k == 0, k == NCD - 1,
                           inc="pe" if k == NCD - 1 else None)
                    ctx.wait("scalar", "pe")
                    ctx.wait("scalar", "dve")
                    act(ctx, P_b[0:mw, :], ps[bka][0:mw, :], AF.Identity,
                        bias=b1c[0:mw, mi:mi + 1])
                    va = act(ctx, PT_b[0:mw, :], ps[bkg][0:mw, :], AF.Gelu,
                             bias=b1c[0:mw, 4 + mi:5 + mi], inc="act")
                    ctx.wait("vector", "act", va)
                    vu = tt(ctx, "vector", ALU.mult,
                            uT_c[0:mw, mi * T + tn:mi * T + tn + TC],
                            P_b[0:mw, :], PT_b[0:mw, :], inc="dve")
                    pvdone[bka] = ("dve", vu)
                    pvdone[bkg] = ("dve", vu)
            ctx.wait("tensor", "dve")
            for mo in range(NCD):
                for nt in range(NT):
                    tn = nt * TC
                    bk = (mo * NT + nt) % 4
                    bankwait(bk)
                    for kk, (k0, k1) in enumerate(mcs):
                        mm(ctx, ps[bk],
                           wb[0:k1 - k0, 8192 + kk * 1024 + mo * 128:8192 + kk * 1024 + (mo + 1) * 128],
                           uT_c[0:k1 - k0, kk * T + tn:kk * T + tn + TC],
                           kk == 0, kk == len(mcs) - 1,
                           inc="pe" if kk == len(mcs) - 1 else None)
                    ctx.wait("vector", "pe")
                    xs = xres[:, mo * T + tn:mo * T + tn + TC]
                    pvdone[bk] = tt(ctx, "vector", ALU.add, xs, xs, ps[bk],
                                    inc="dve")
            wuse[ci % 2] = ctx.val("pe")
        for c in range(NCD):
            ctx.do("vector", lambda e, o=xres[:, c * T:(c + 1) * T],
                   i=xres[:, c * T:(c + 1) * T], s=bcol[:, 20 + c:21 + c]:
                   e.tensor_scalar_add(o, i, s),
                   inc="dve" if c == NCD - 1 else None)
        ctx.wait("tensor", "dve")
    ctx.fence()
    xin_f = ctx.at_f(LF_ATT, 8192)
    for half in range(2):
        vr = ctx.dma("sync",
                     xin_f.rearrange("p (c t) -> p c t", c=4),
                     io["spill_x"].rearrange("(c p) t -> p c t", p=128)[
                         :, half * 4:half * 4 + 4, :], inc="ld")
        ctx.wait("vector", "ld", vr)
        for c4 in range(4):
            c = half * 4 + c4
            xs = xres[:, c * T:(c + 1) * T]
            tt(ctx, "vector", ALU.add, xs, xs, xin_f[:, c4 * T:(c4 + 1) * T],
               inc="dve" if c4 == 3 else None)
        ctx.wait("sync", "dve")
    ctx.fence()
    ctx._off = LF_ATT
    emit_ln_plain(ctx, ps, xres)


def prep_l1ffn(w):
    DIM_HEAD, WINDOW = 64, 256
    n = T
    F8 = np.float64
    ext = {
        "Wqkv": np.ascontiguousarray(w["L1_ffn_Wqkv"]),
        "Wo": np.ascontiguousarray(w["L1_ffn_Wo"]),
        "W1": np.ascontiguousarray(w["L1_ffn_W1"]),
        "W2": np.ascontiguousarray(w["L1_ffn_W2"]),
    }
    ext["qsrep"] = np.tile(w["L1_ffn_q_scale"], (1, 2))[:, :, None].astype(np.float32)
    ext["ksrep"] = np.tile(w["L1_ffn_k_scale"], (1, 2))[:, :, None].astype(np.float32)
    ext["lng8"] = np.ascontiguousarray(
        w["L1_ffn_ln_g"].reshape(6, 8, 128).transpose(0, 2, 1))
    ext["lnb8"] = np.ascontiguousarray(
        w["L1_ffn_ln_b"].reshape(6, 8, 128).transpose(0, 2, 1))
    ext["b28"] = np.ascontiguousarray(
        w["L1_ffn_b2"].reshape(6, 8, 128).transpose(0, 2, 1))
    CHl = [(0, 512), (512, 1024), (1024, 1536), (1536, 2048), (2048, 2560),
           (2560, 2730)]
    b1a = np.zeros((6, 6, 4, 128), dtype=np.float32)
    b1g = np.zeros((6, 6, 4, 128), dtype=np.float32)
    for li in range(6):
        for ci, (c0, c1) in enumerate(CHl):
            for mi in range(4):
                a0 = c0 + mi * 128
                a1 = min(c1, a0 + 128)
                if a0 >= c1:
                    continue
                b1a[li, ci, mi, 0:a1 - a0] = w["L1_ffn_b1"][li, a0:a1]
                b1g[li, ci, mi, 0:a1 - a0] = w["L1_ffn_b1"][li, 2730 + a0:2730 + a1]
    ext["b1a_t"] = np.ascontiguousarray(b1a.transpose(0, 1, 3, 2))
    ext["b1g_t"] = np.ascontiguousarray(b1g.transpose(0, 1, 3, 2))
    half = np.arange(0, DIM_HEAD, 2, dtype=F8)
    inv_freq = 1.0 / 10000.0 ** (half / DIM_HEAD)
    t = np.arange(n, dtype=F8)
    freqs = t[:, None] * inv_freq[None, :]
    freqs = np.concatenate([freqs, freqs], -1)
    base_scale = (half + 0.4 * DIM_HEAD) / (1.4 * DIM_HEAD)
    power = (t - n // 2) / (WINDOW // 2)
    scale = base_scale[None, :] ** power[:, None]
    scale = np.concatenate([scale, scale], -1)
    cos, sin = np.cos(freqs), np.sin(freqs)
    ext["cs_t"] = np.tile((cos * scale * 8.0).T, (2, 1)).astype(np.float32)
    ext["ss_t"] = np.tile((sin * scale * 8.0).T, (2, 1)).astype(np.float32)
    ext["ci_t"] = np.tile((cos / scale).T, (2, 1)).astype(np.float32)
    ext["si_t"] = np.tile((sin / scale).T, (2, 1)).astype(np.float32)
    R = np.zeros((128, 128), dtype=np.float32)
    for m in range(128):
        if m % 64 < 32:
            R[m + 32, m] = -1.0
        else:
            R[m - 32, m] = 1.0
    ext["rotb"] = R
    hp = np.zeros((8, 128, 16), dtype=np.float32)
    for c in range(8):
        for p in range(128):
            hp[c, p, 2 * c + p // 64] = 1.0
    ext["hpack"] = hp
    def fm_var(b):
        i = np.arange(256)[:, None]
        j = np.arange(512)[None, :]
        q_abs = b * 256 + i
        if b % 2 == 0:
            k_abs = np.where(j < 256, b * 256 + j, (b - 1) * 256 + (j - 256))
        else:
            k_abs = np.where(j < 256, (b - 1) * 256 + j, b * 256 + (j - 256))
        v = (k_abs <= q_abs) & (q_abs <= k_abs + 256)
        return v.astype(np.float32).reshape(2, 128, 512)
    ext["fmask"] = np.stack([fm_var(2), fm_var(1)])
    i0 = np.arange(256)[:, None]
    j0 = np.arange(256)[None, :]
    v0 = (j0 <= i0)
    ext["fmask0"] = v0.astype(np.float32).reshape(2, 128, 256)
    return ext


L1F_SHAPES = {
    "Wqkv": ((6, D, 3072), F32), "Wo": ((6, D, D), F32),
    "W1": ((6, D, 5460), F32), "W2": ((6, 2730, D), F32),
    "qsrep": ((6, 128, 1), F32), "ksrep": ((6, 128, 1), F32),
    "lng8": ((6, 128, 8), F32), "lnb8": ((6, 128, 8), F32),
    "b28": ((6, 128, 8), F32),
    "b1a_t": ((6, 6, 128, 4), F32), "b1g_t": ((6, 6, 128, 4), F32),
    "cs_t": ((128, T), F32), "ss_t": ((128, T), F32),
    "ci_t": ((128, T), F32), "si_t": ((128, T), F32),
    "rotb": ((128, 128), F32), "hpack": ((8, 128, 16), F32),
    "fmask": ((2, 2, 128, 512), F32), "fmask0": ((2, 128, 256), F32),
}



_CACHE = {}


def _get_program():
    if "nc" not in _CACHE:
        allshapes = {}
        for s in (L0A_SHAPES, L0F_SHAPES, L1A_SHAPES, L1F_SHAPES):
            allshapes.update(s)
        _CACHE["nc"] = build_program(
            [phase_l0_attn, phase_l0_ffn, phase_l1_attn, phase_l1_ffn],
            allshapes)
    return _CACHE["nc"]


def kernel(**inputs):
    w = {k: np.asarray(v) for k, v in inputs.items()}
    ext = {}
    for p in (prep_l0attn, prep_l0ffn, prep_l1attn, prep_l1ffn):
        ext.update(p(w))
    nc = _get_program()
    in_maps = []
    for b in range(8):
        mdict = {"x_in": np.ascontiguousarray(w["L0_attn_x"][b].T)}
        mdict.update(ext)
        in_maps.append(mdict)
    res = run_bass_kernel_spmd(nc, in_maps, core_ids=list(range(8)))
    return np.stack([r["out"].T for r in res.results]).astype(np.float32)


# version 18
# speedup vs baseline: 1.1674x; 1.1674x over previous
import contextlib
import numpy as np
import concourse.bass as bass
import concourse.mybir as mybir
from concourse.bass_utils import run_bass_kernel_spmd

F32 = mybir.dt.float32
BF16 = mybir.dt.bfloat16
AF = mybir.ActivationFunctionType
ALU = mybir.AluOpType
AX = mybir.AxisListType

D = 1024
T = 2048
NCD = 8
NT = 4
TC = 512
EPS = 1e-5
NEG = -3.0e38

ENGINES = ("tensor", "vector", "scalar", "gpsimd", "sync")
ARENA_COLS = [102400]
XRES_OFF = 0
FREE0 = 32768


class Ctx:
    def __init__(self, nc):
        self.nc = nc
        self.streams = {e: [] for e in ENGINES}
        self.semh = {}
        self.semc = {}
        self.watermark = {e: {} for e in ENGINES}
        self.C = {}
        self._arena = None
        self._off = 0

    def add_sem(self, name, handle):
        self.semh[name] = handle
        self.semc[name] = 0

    def val(self, sem):
        return self.semc[sem]

    def wait(self, eng, sem, val=None):
        v = self.semc[sem] if val is None else val
        if v <= 0 or self.watermark[eng].get(sem, 0) >= v:
            return
        self.watermark[eng][sem] = v
        h = self.semh[sem]
        self.streams[eng].append(lambda e, h=h, v=v: e.wait_ge(h, v))

    AUTO = {"vector": "dve", "scalar": "act", "gpsimd": "pool"}

    def do(self, eng, fn, inc=None, n=1, is_dma=False):
        if eng in self.AUTO and not is_dma:
            sname = self.AUTO[eng]
            assert inc is None or inc == sname
            self.wait(eng, sname)
            self.semc[sname] += 1
            h = self.semh[sname]
            self.streams[eng].append(lambda e, fn=fn, h=h: fn(e).then_inc(h, 1))
            return self.semc[sname]
        if inc is None:
            self.streams[eng].append(fn)
            return None
        self.semc[inc] += n
        h = self.semh[inc]
        self.streams[eng].append(lambda e, fn=fn, h=h, n=n: fn(e).then_inc(h, n))
        return self.semc[inc]

    def dma(self, eng, out_ap, in_ap, inc):
        self.wait(eng, inc)
        return self.do(eng, lambda e, o=out_ap, i=in_ap: e.dma_start(out=o, in_=i),
                       inc=inc, n=16, is_dma=True)

    def set_arena(self, handle):
        self._arena = handle
        self._off = 0

    def amark(self):
        return self._off

    def arelease(self, mark):
        self._off = mark

    def alloc_b(self, cols):
        cols = (cols + 15) // 16 * 16
        off = self._off
        assert off + cols <= ARENA_COLS[0], f"arena overflow {off}+{cols}"
        self._off = off + cols
        return self._arena.ap()[:, off:off + cols]

    def alloc_f(self, cols):
        return self.alloc_b(2 * cols).bitcast(F32)

    def at_b(self, off, cols):
        assert off + cols <= ARENA_COLS[0], f"arena overflow {off}+{cols}"
        return self._arena.ap()[:, off:off + cols]

    def at_f(self, off, cols):
        return self.at_b(off, 2 * cols).bitcast(F32)

    def fence(self, engines=ENGINES, sems=("pe", "dve", "act", "pool", "ld", "gld", "st", "qa", "qb")):
        for e in engines:
            for s in sems:
                self.wait(e, s)

    def tap(self, ap):
        if "dbg" not in self.C:
            return
        for s in ("pe", "dve", "act", "pool"):
            self.wait("gpsimd", s)
        P, Cc = ap.shape[-2], ap.shape[-1]
        off = self.C["dbg_off"]
        assert off + Cc <= 16384
        self.dma("gpsimd", self.C["dbg"][0:P, off:off + Cc], ap, inc="gld")
        self.C["dbg_off"] = off + Cc



import traceback as _tb


def mm(ctx, out, lhsT, rhs, start, stop, inc=None):
    site = "".join(_tb.format_stack(limit=3)[:-1]) if MM_DEBUG else None

    def run(e, o=out, l=lhsT, r=rhs, s=start, p=stop, site=site):
        try:
            return e.matmul(o, l, r, start=s, stop=p)
        except Exception:
            if site:
                print("MM FAIL at:\n", site)
                print("out", o.shape, "lhsT", l.shape, "rhs", r.shape)
            raise
    return ctx.do("tensor", run, inc=inc)


MM_DEBUG = False


def tt(ctx, eng, op, out, a, b, inc=None):
    return ctx.do(eng, lambda e, o=out, x=a, y=b, q=op: e.tensor_tensor(o, x, y, q),
                  inc=inc)


def copy(ctx, eng, out, a, inc=None):
    return ctx.do(eng, lambda e, o=out, x=a: e.tensor_copy(o, x), inc=inc)


def act(ctx, out, a, func, bias=0.0, scale=1.0, accum=None, inc=None):
    if accum is None:
        return ctx.do("scalar", lambda e, o=out, x=a, f=func, b=bias, s=scale:
                      e.activation(o, x, f, bias=b, scale=s), inc=inc)
    return ctx.do("scalar", lambda e, o=out, x=a, f=func, b=bias, s=scale, ac=accum:
                  e.activation(o, x, f, bias=b, scale=s, accum_out=ac), inc=inc)



def emit_ln_plain(ctx, ps, xres):
    m0 = ctx.amark()
    sqf = ctx.alloc_b(T * NCD)
    rows = ctx.alloc_f(3 * T)
    t1 = ctx.alloc_f(TC)
    C = ctx.C
    rS = rows[0:1, 0:T]
    rQ = rows[0:1, T:2 * T]
    rT = rows[0:1, 2 * T:3 * T]
    for cc in range(NCD):
        act(ctx, sqf[:, cc * T:(cc + 1) * T], xres[:, cc * T:(cc + 1) * T],
            AF.Square, inc="act" if cc == NCD - 1 else None)
    ctx.wait("tensor", "act")
    evdone = {}
    for nt in range(NT):
        t0 = nt * TC
        pA = ps[nt % 2][0:1, 0:TC]
        pB = ps[2 + nt % 2][0:1, 0:TC]
        if nt >= 2:
            ctx.wait("tensor", "dve", evdone[nt % 2])
        for cc in range(NCD):
            mm(ctx, pA, C["onesf"], xres[:, cc * T + t0: cc * T + t0 + TC],
               cc == 0, cc == NCD - 1)
            mm(ctx, pB, C["onesb"], sqf[:, cc * T + t0: cc * T + t0 + TC],
               cc == 0, cc == NCD - 1, inc="pe" if cc == NCD - 1 else None)
        ctx.wait("vector", "pe")
        copy(ctx, "vector", rS[:, t0:t0 + TC], pA)
        evdone[nt % 2] = copy(ctx, "vector", rQ[:, t0:t0 + TC], pB, inc="dve")
    ctx.do("vector", lambda e, o=rS, i=rS: e.tensor_scalar_mul(o, i, 1.0 / D))
    ctx.do("vector", lambda e, o=rQ, i=rQ: e.tensor_scalar_mul(o, i, 1.0 / D))
    tt(ctx, "vector", ALU.mult, rT, rS, rS)
    tt(ctx, "vector", ALU.subtract, rQ, rQ, rT, inc="dve")
    ctx.wait("scalar", "dve")
    act(ctx, rQ, rQ, AF.Ln, bias=C["eps1"])
    act(ctx, rQ, rQ, AF.Exp, scale=-0.5, inc="act")
    ctx.wait("tensor", "act")
    ctx.wait("tensor", "dve")
    apdone = {}

    def bcast(nt):
        t0 = nt * TC
        if nt >= 2:
            ctx.wait("tensor", "dve", apdone[nt % 2])
        mm(ctx, ps[4 + nt % 2], C["ones1f"], rS[:, t0:t0 + TC], True, True)
        mm(ctx, ps[6 + nt % 2], C["ones1f"], rQ[:, t0:t0 + TC], True, True,
           inc="pe")
        return ctx.val("pe")

    def apply(nt, vb):
        t0 = nt * TC
        ctx.wait("vector", "pe", vb)
        for cc in range(NCD):
            xc = xres[:, cc * T + t0: cc * T + t0 + TC]
            tt(ctx, "vector", ALU.subtract, t1, xc, ps[4 + nt % 2])
            tt(ctx, "vector", ALU.mult, xc, t1, ps[6 + nt % 2],
               inc="dve" if cc == NCD - 1 else None)
        apdone[nt % 2] = ctx.val("dve")

    vbs = {}
    vbs[0] = bcast(0)
    vbs[1] = bcast(1)
    apply(0, vbs[0])
    vbs[2] = bcast(2)
    apply(1, vbs[1])
    vbs[3] = bcast(3)
    apply(2, vbs[2])
    apply(3, vbs[3])
    ctx.wait("tensor", "dve")
    ctx.wait("scalar", "dve")
    ctx.arelease(m0)



def build_program(phases, extra_inputs, dbg=False):
    nc = bass.Bass("TRN2", target_bir_lowering=False)
    io = {}
    io["x_in"] = nc.dram_tensor("x_in", [D, T], F32, kind="ExternalInput").ap()
    io["out"] = nc.dram_tensor("out", [D, T], F32, kind="ExternalOutput").ap()
    if dbg:
        io["dbg"] = nc.dram_tensor("dbg", [128, 16384], F32, kind="ExternalOutput").ap()
    for name, (shape, dtype) in extra_inputs.items():
        io[name] = nc.dram_tensor(name, list(shape), dtype, kind="ExternalInput").ap()
    io["spill_x"] = nc.dram_tensor("spill_x", [D, T], F32, kind="Internal").ap()
    io["spill_x2"] = nc.dram_tensor("spill_x2", [D, T], F32, kind="Internal").ap()
    io["spill_b"] = nc.dram_tensor("spill_b", [D, T], BF16, kind="Internal").ap()

    with contextlib.ExitStack() as st:
        ctx = Ctx(nc)
        arena_h = None
        while arena_h is None:
            try:
                arena_h = nc.alloc_sbuf_tensor("arena", [128, ARENA_COLS[0]], BF16)
            except Exception:
                ARENA_COLS[0] -= 2048
                if ARENA_COLS[0] < 90112:
                    raise
        kon_h = nc.alloc_sbuf_tensor("konst", [128, 16], F32)
        onesf_h = nc.alloc_sbuf_tensor("onesf", [128, 2], F32)
        onesb_h = nc.alloc_sbuf_tensor("onesb", [128, 2], BF16)
        ones1f_h = nc.alloc_sbuf_tensor("ones1f", [1, 128], F32)
        ones1b_h = nc.alloc_sbuf_tensor("ones1b", [1, 128], BF16)
        ident_h = nc.alloc_sbuf_tensor("identb", [128, 128], BF16)
        ctx.set_arena(arena_h)
        xres = ctx.at_f(XRES_OFF, NCD * T)
        ctx._off = FREE0
        C = ctx.C
        C["xres"] = xres
        C["onesf"] = onesf_h.ap()[:, 0:1]
        C["onesb"] = onesb_h.ap()[:, 0:1]
        C["ones1f"] = ones1f_h.ap()
        C["ones1b"] = ones1b_h.ap()
        C["ident"] = ident_h.ap()
        C["eps1"] = kon_h.ap()[0:1, 0:1]
        C["eps64"] = kon_h.ap()[64:65, 0:1]
        C["epsc"] = kon_h.ap()[:, 0:1]
        if dbg:
            C["dbg"] = io["dbg"]
            C["dbg_off"] = 0

        for s in ("ld", "gld", "st", "pe", "dve", "act", "pool", "qa", "qb"):
            ctx.add_sem(s, st.enter_context(nc.semaphore(s)))

        ps = [st.enter_context(nc.psum_tensor(f"ps{i}", [128, TC], F32)).ap()
              for i in range(8)]
        C["psb"] = ps[6].bitcast(BF16)

        ctx.do("vector", lambda e: e.memset(onesf_h.ap(), 1.0))
        ctx.do("vector", lambda e: e.memset(onesb_h.ap(), 1.0))
        ctx.do("vector", lambda e: e.memset(ones1f_h.ap(), 1.0))
        ctx.do("vector", lambda e: e.memset(ones1b_h.ap(), 1.0))
        ctx.do("vector", lambda e: e.memset(kon_h.ap()[:, 0:1], EPS), inc="dve")
        ctx.dma("gpsimd", C["ident"], io["ident"], inc="gld")
        ctx.dma("sync", xres.rearrange("p (c t) -> p c t", c=NCD),
                io["x_in"].rearrange("(c p) t -> p c t", p=128), inc="ld")
        for f in phases:
            f(ctx, ps, io)

        for s in ("dve", "pe", "act", "pool"):
            ctx.wait("gpsimd", s)
        vst = ctx.dma("gpsimd", io["out"].rearrange("(c p) t -> p c t", p=128),
                      xres.rearrange("p (c t) -> p c t", c=NCD), inc="st")
        ctx.wait("gpsimd", "st", vst)

        with nc.Block() as block:
            for eng_name in ENGINES:
                stream = ctx.streams[eng_name]

                def make(fns):
                    def body(h):
                        for fn in fns:
                            fn(h)
                    return body
                getattr(block, eng_name)(make(stream))
    return nc



def phase_ln_only(ctx, ps, io):
    ctx.fence()
    emit_ln_plain(ctx, ps, ctx.C["xres"])



NF = 2176
NKC = 17

L0A_A1R = 32768
L0A_A1I = 34944
L0A_OUTW = 37120
L0A_S = 38144
L0A_V = 39168
L0A_UVW = 55552
L0A_XBF = 63744
L0A_UT = 80128
L0A_ZR = 0
L0A_STREAM = 17408
L0A_ZI = 55552
L0A_GRT = 17408
L0A_GIT = 26112
L0A_G = 39168
L0A_OW = 55552


def phase_l0_attn(ctx, ps, io):
    C = ctx.C
    xres = C["xres"]
    psb = C["psb"]

    S_sb = ctx.at_b(L0A_S, 1024)
    a1r_b = ctx.at_b(L0A_A1R, NF)[0:32, :]
    a1i_b = ctx.at_b(L0A_A1I, NF)[0:32, :]
    outw_b = ctx.at_b(L0A_OUTW, 1024)[0:32, :]

    ctx._off = L0A_V
    pos_sb = ctx.alloc_f(32)
    grep_sb = ctx.alloc_f(3 * 1024)
    brep_sb = ctx.alloc_f(3 * 1024)
    go_sb = ctx.alloc_f(1024)
    bo_sb = ctx.alloc_f(1024)
    wrep_sb = ctx.alloc_f(1024)
    fc_b = ctx.alloc_b(96)[0:32, :]
    h_f = ctx.alloc_f(1024)
    hs_f = ctx.alloc_f(1024)
    st_f = ctx.alloc_f(256)
    sil_b = ctx.alloc_b(1024)
    sT_b = ctx.alloc_b(4096)

    ctx.dma("sync", pos_sb, io["rpe_pos_tiles"], inc="ld")
    ctx.dma("sync", wrep_sb, io["rpe_wrep32"], inc="ld")
    ctx.dma("sync", go_sb, io["rpe_gorep32"], inc="ld")
    ctx.dma("sync", bo_sb, io["rpe_borep32"], inc="ld")
    ctx.dma("sync", grep_sb, io["rpe_grep32"], inc="ld")
    ctx.dma("sync", brep_sb, io["rpe_brep32"], inc="ld")
    ctx.dma("gpsimd", fc_b.rearrange("p (i c) -> p i c", i=3),
            io["rpe_fc"].rearrange("i k c -> k i c"), inc="gld")
    ctx.dma("gpsimd", outw_b, io["rpe_outw"], inc="gld")
    for e in ("tensor", "vector", "scalar"):
        ctx.wait(e, "ld")
        ctx.wait(e, "gld")
        ctx.wait(e, "dve")

    def rpe_ln_silu_batch(grep, brep, out_bf):
        s0 = st_f[:, 0:32]
        s1 = st_f[:, 32:64]
        mu = st_f[:, 64:96]
        va = st_f[:, 96:128]
        tmp = st_f[:, 128:160]
        ctx.do("vector", lambda e, o=s0,
               i=h_f.rearrange("p (t c) -> p t c", c=32):
               e.reduce_sum(o, i, axis=AX.X))
        ctx.wait("scalar", "dve")
        vq = act(ctx, hs_f, h_f, AF.Square, inc="act")
        ctx.wait("vector", "act", vq)
        ctx.do("vector", lambda e, o=s1,
               i=hs_f.rearrange("p (t c) -> p t c", c=32):
               e.reduce_sum(o, i, axis=AX.X))
        ctx.do("vector", lambda e, o=mu, i=s0: e.tensor_scalar_mul(o, i, 1.0 / 32))
        ctx.do("vector", lambda e, o=va, i=s1: e.tensor_scalar_mul(o, i, 1.0 / 32))
        tt(ctx, "vector", ALU.mult, tmp, mu, mu)
        tt(ctx, "vector", ALU.subtract, va, va, tmp, inc="dve")
        ctx.wait("scalar", "dve")
        act(ctx, va, va, AF.Ln, bias=C["epsc"])
        va2 = act(ctx, va, va, AF.Exp, scale=-0.5, inc="act")
        ctx.wait("vector", "act", va2)
        for t in range(32):
            ctx.do("vector", lambda e, o=hs_f[:, t * 32:(t + 1) * 32],
                   i=h_f[:, t * 32:(t + 1) * 32], m=mu[:, t:t + 1],
                   r=va[:, t:t + 1]:
                   e.tensor_scalar(o, i, m, r, ALU.subtract, ALU.mult))
        tt(ctx, "vector", ALU.mult, hs_f, hs_f, grep)
        tt(ctx, "vector", ALU.add, hs_f, hs_f, brep, inc="dve")
        ctx.wait("scalar", "dve")
        return act(ctx, out_bf, hs_f, AF.Silu, inc="act")

    for t in range(32):
        ctx.do("vector", lambda e, o=h_f[:, t * 32:(t + 1) * 32],
               i=wrep_sb[:, t * 32:(t + 1) * 32], s=pos_sb[:, t:t + 1]:
               e.tensor_scalar_mul(o, i, s),
               inc="dve" if t == 31 else None)
    for i in range(3):
        vs = rpe_ln_silu_batch(grep_sb[:, i * 1024:(i + 1) * 1024],
                               brep_sb[:, i * 1024:(i + 1) * 1024], sil_b)
        ctx.wait("tensor", "act", vs)
        for grp in range(2):
            if grp == 1:
                ctx.wait("tensor", "dve")
            for b2 in range(2):
                psbk = C["psb"] if b2 == 0 else ps[7].bitcast(BF16)
                for tt_i in range(8):
                    t = grp * 16 + b2 * 8 + tt_i
                    ctx.do("tensor",
                           lambda e, o=psbk[0:32, tt_i * 128:(tt_i + 1) * 128],
                           i=sil_b[:, t * 32:(t + 1) * 32], d=C["ident"]:
                           e.transpose(o, i, d),
                           inc="pe" if (b2 == 1 and tt_i == 7) else None)
            ctx.wait("vector", "pe")
            copy(ctx, "vector", sT_b[0:32, grp * 2048:grp * 2048 + 1024],
                 C["psb"][0:32, 0:1024])
            copy(ctx, "vector", sT_b[0:32, grp * 2048 + 1024:grp * 2048 + 2048],
                 ps[7].bitcast(BF16)[0:32, 0:1024], inc="dve")
        ctx.wait("tensor", "dve")
        for t in range(32):
            bk = t // 16
            mm(ctx, ps[4 + bk][:, (t % 16) * 32:(t % 16) * 32 + 32],
               sT_b[0:32, t * 128:(t + 1) * 128],
               fc_b[:, i * 32:(i + 1) * 32], True, True,
               inc="pe" if t == 31 else None)
        ctx.wait("vector", "pe")
        copy(ctx, "vector", h_f[:, 0:512], ps[4])
        vv2 = copy(ctx, "vector", h_f[:, 512:1024], ps[5], inc="dve")
        ctx.wait("scalar", "dve", vv2)
        ctx.wait("tensor", "dve", vv2)
    rpe_ln_silu_batch(go_sb, bo_sb, S_sb)
    ctx.wait("tensor", "act")

    ctx.fence()
    ctx._off = L0A_V
    fg_sb = [ctx.alloc_b(NF), ctx.alloc_b(NF)]
    a1_f = ctx.alloc_f(NF)[0:32, :]
    NC5 = [(0, 512), (512, 1024), (1024, 1536), (1536, 2048), (2048, NF)]
    pe_done = {}
    for ri, (srcname, dstb) in enumerate((("fgr", a1r_b), ("fgi", a1i_b))):
        src = io[srcname]
        for p in range(32):
            buf = fg_sb[p % 2]
            qsem = "qa" if p % 2 == 0 else "qb"
            key = (p % 2)
            if key in pe_done:
                ctx.wait("gpsimd", "pe", pe_done[key])
            vq = ctx.dma("gpsimd", buf, src[p * 128:(p + 1) * 128, :], inc=qsem)
            ctx.wait("tensor", qsem, vq)
            for ci, (c0, c1) in enumerate(NC5):
                mm(ctx, ps[ci][0:32, 0:c1 - c0], S_sb[:, p * 32:(p + 1) * 32],
                   buf[:, c0:c1], p == 0, p == 31,
                   inc="pe" if ci == len(NC5) - 1 else None)
            pe_done[p % 2] = ctx.val("pe")
        ctx.wait("vector", "pe")
        for ci, (c0, c1) in enumerate(NC5):
            copy(ctx, "vector", a1_f[:, c0:c1], ps[ci][0:32, 0:c1 - c0])
        ctx.do("vector", lambda e, o=dstb, i=a1_f: e.tensor_copy(o, i), inc="dve")
        ctx.wait("tensor", "dve")

    ctx.tap(a1r_b)
    ctx.tap(a1i_b)
    ctx.fence()
    v_sb = ctx.at_b(L0A_V, 16 * 1024)
    uvw_sb = ctx.at_b(L0A_UVW, NCD * 1024)
    xbf = ctx.at_b(L0A_XBF, NCD * T)
    uT_sb = ctx.at_b(L0A_UT, NCD * T)
    for k in range(NCD):
        copy(ctx, "vector", xbf[:, k * T:(k + 1) * T], xres[:, k * T:(k + 1) * T],
             inc="dve" if k == NCD - 1 else None)
    vxbf = ctx.val("dve")
    ctx.wait("tensor", "dve", vxbf)
    ctx.wait("sync", "dve", vxbf)
    vspill = ctx.dma("sync", io["spill_x"].rearrange("(c p) t -> p c t", p=128),
                     xres.rearrange("p (c t) -> p c t", c=NCD), inc="ld")
    ctx.dma("gpsimd", uvw_sb.rearrange("p (c t) -> p c t", c=NCD),
            io["uv_w"].rearrange("(c p) t -> p c t", p=128)[:, :, 1024:2048],
            inc="gld")
    ctx.wait("tensor", "gld")
    bankdone = {}
    for tt_i in range(16):
        for nc2 in range(2):
            bk = (tt_i % 2) * 2 + nc2
            if bk in bankdone:
                ctx.wait("tensor", "act", bankdone[bk])
            for k in range(NCD):
                mm(ctx, ps[bk], xbf[:, k * T + tt_i * 128: k * T + tt_i * 128 + 128],
                   uvw_sb[:, k * 1024 + nc2 * 512: k * 1024 + (nc2 + 1) * 512],
                   k == 0, k == NCD - 1, inc="pe" if k == NCD - 1 else None)
            ctx.wait("scalar", "pe")
            bankdone[bk] = act(
                ctx, v_sb[:, tt_i * 1024 + nc2 * 512: tt_i * 1024 + (nc2 + 1) * 512],
                ps[bk], AF.Silu, inc="act")
    ctx.wait("gpsimd", "pe")
    ctx.dma("gpsimd", uvw_sb.rearrange("p (c t) -> p c t", c=NCD),
            io["uv_w"].rearrange("(c p) t -> p c t", p=128)[:, :, 0:1024],
            inc="gld")
    ctx.wait("tensor", "gld")
    ctx.wait("tensor", "act")
    bankdone = {}
    for m in range(NCD):
        for nt in range(NT):
            bk = (m * NT + nt) % 4
            if bk in bankdone:
                ctx.wait("tensor", "act", bankdone[bk])
            for k in range(NCD):
                mm(ctx, ps[bk], uvw_sb[:, k * 1024 + m * 128: k * 1024 + (m + 1) * 128],
                   xbf[:, k * T + nt * TC: k * T + (nt + 1) * TC],
                   k == 0, k == NCD - 1, inc="pe" if k == NCD - 1 else None)
            ctx.wait("scalar", "pe")
            bankdone[bk] = act(
                ctx, uT_sb[:, m * T + nt * TC: m * T + (nt + 1) * TC],
                ps[bk], AF.Silu, inc="act")
    ctx.wait("vector", "act")

    ctx.tap(v_sb[:, 0:1024])
    ctx.tap(uT_sb[:, 0:512])
    ctx.fence()
    zr_sb = ctx.at_b(L0A_ZR, NKC * 1024)
    zi_sb = ctx.at_b(L0A_ZI, NKC * 1024)
    ctx._off = L0A_STREAM
    fr_sb = [ctx.alloc_b(2048), ctx.alloc_b(2048)]
    fi_sb = [ctx.alloc_b(2048), ctx.alloc_b(2048)]
    vh_b = ctx.alloc_b(2048)
    zf_f = ctx.alloc_f(1024)
    assert ctx._off <= 32768
    ctx.wait("vector", "ld", vspill)
    frdone = {}
    uinfo = []

    def emit_z(u):
        g = u["g"]
        kc, nc2 = u["kc"], u["nc2"]
        vh = vh_b[:, nc2 * 1024:(nc2 + 1) * 1024]
        ctx.wait("vector", "pe", u["vmm"])
        copy(ctx, "vector", vh[:, 0:512], ps[g + 0])
        copy(ctx, "vector", vh[:, 512:1024], ps[g + 1])
        t1 = zf_f[:, 0:512]
        t2 = zf_f[:, 512:1024]
        zrd = zr_sb[:, kc * 1024 + nc2 * 512: kc * 1024 + (nc2 + 1) * 512]
        zid = zi_sb[:, kc * 1024 + nc2 * 512: kc * 1024 + (nc2 + 1) * 512]
        tt(ctx, "vector", ALU.mult, t1, vh[:, 0:512], ps[g + 2])
        tt(ctx, "vector", ALU.mult, t2, vh[:, 512:1024], ps[g + 3])
        tt(ctx, "vector", ALU.subtract, zrd, t1, t2)
        tt(ctx, "vector", ALU.mult, t1, vh[:, 512:1024], ps[g + 2])
        tt(ctx, "vector", ALU.mult, t2, vh[:, 0:512], ps[g + 3])
        tt(ctx, "vector", ALU.add, zid, t1, t2, inc="dve")
        u["vz"] = ctx.val("dve")

    for kc in range(NKC):
        pp = kc % 2
        qsem = "qa" if pp == 0 else "qb"
        if pp in frdone:
            ctx.wait("gpsimd", "pe", frdone[pp])
        ctx.dma("gpsimd", fr_sb[pp].rearrange("p (t f) -> p t f", f=128),
                io["frt"][kc].rearrange("(t p) f -> p t f", p=128), inc=qsem)
        vq = ctx.dma("gpsimd", fi_sb[pp].rearrange("p (t f) -> p t f", f=128),
                io["fit"][kc].rearrange("(t p) f -> p t f", p=128), inc=qsem)
        for nc2 in range(2):
            u = {}
            g = 4 * ((2 * kc + nc2) % 2)
            u["g"] = g
            u["kc"] = kc
            u["nc2"] = nc2
            ctx.wait("tensor", qsem, vq)
            if ("zdone", g) in dict(uinfo[-2:-1] and [] or []):
                pass
            if len(uinfo) >= 2 and uinfo[-2].get("vz"):
                ctx.wait("tensor", "dve", uinfo[-2]["vz"])
            for tt_i in range(16):
                mm(ctx, ps[g + 0], fr_sb[pp][:, tt_i * 128:(tt_i + 1) * 128],
                   v_sb[:, tt_i * 1024 + nc2 * 512: tt_i * 1024 + (nc2 + 1) * 512],
                   tt_i == 0, tt_i == 15)
            for tt_i in range(16):
                mm(ctx, ps[g + 1], fi_sb[pp][:, tt_i * 128:(tt_i + 1) * 128],
                   v_sb[:, tt_i * 1024 + nc2 * 512: tt_i * 1024 + (nc2 + 1) * 512],
                   tt_i == 0, tt_i == 15)
            mm(ctx, ps[g + 2], a1r_b[:, kc * 128:(kc + 1) * 128],
               outw_b[:, nc2 * 512:(nc2 + 1) * 512], True, True)
            u["vmm"] = mm(ctx, ps[g + 3], a1i_b[:, kc * 128:(kc + 1) * 128],
                          outw_b[:, nc2 * 512:(nc2 + 1) * 512], True, True,
                          inc="pe")
            uinfo.append(u)
            if len(uinfo) >= 2:
                emit_z(uinfo[-2])
        frdone[pp] = ctx.val("pe")
    emit_z(uinfo[-1])

    ctx.fence()
    g_sb = ctx.at_b(L0A_G, NCD * T)
    grt_sb = ctx.at_b(L0A_GRT, NKC * 512)
    git_sb = ctx.at_b(L0A_GIT, NKC * 512)
    gdone = {}
    for nt in range(NT):
        if nt > 0:
            ctx.wait("gpsimd", "pe")
        ctx.dma("gpsimd", grt_sb.rearrange("p (k f) -> p k f", k=NKC),
                io["grt"][nt].rearrange("k p f -> p k f"), inc="gld")
        ctx.dma("gpsimd", git_sb.rearrange("p (k f) -> p k f", k=NKC),
                io["git"][nt].rearrange("k p f -> p k f"), inc="gld")
        ctx.wait("tensor", "gld")
        for m in range(NCD):
            bk = m % 4
            if bk in gdone:
                ctx.wait("tensor", "dve", gdone[bk])
            pst = ps[bk]
            for kc in range(NKC):
                mm(ctx, pst, zr_sb[:, kc * 1024 + m * 128: kc * 1024 + (m + 1) * 128],
                   grt_sb[:, kc * 512:(kc + 1) * 512], kc == 0, False)
            for kc in range(NKC):
                mm(ctx, pst, zi_sb[:, kc * 1024 + m * 128: kc * 1024 + (m + 1) * 128],
                   git_sb[:, kc * 512:(kc + 1) * 512], False, kc == NKC - 1,
                   inc="pe" if kc == NKC - 1 else None)
            ctx.wait("vector", "pe")
            gdone[bk] = tt(
                ctx, "vector", ALU.mult,
                g_sb[:, m * T + nt * TC: m * T + (nt + 1) * TC],
                uT_sb[:, m * T + nt * TC: m * T + (nt + 1) * TC], pst, inc="dve")

    ctx.tap(g_sb[:, 0:2048])
    ctx.fence()
    ow_sb = ctx.at_b(L0A_OW, NCD * 1024)
    ctx.wait("sync", "dve")
    vx = ctx.dma("sync", xres.rearrange("p (c t) -> p c t", c=NCD),
                 io["spill_x"].rearrange("(c p) t -> p c t", p=128), inc="ld")
    ctx.wait("gpsimd", "pe")
    vw = ctx.dma("gpsimd", ow_sb.rearrange("p (c t) -> p c t", c=NCD),
                 io["o_w"].rearrange("(c p) t -> p c t", p=128), inc="gld")
    ctx.wait("tensor", "gld", vw)
    ctx.wait("tensor", "ld", vx)
    ctx.wait("vector", "ld", vx)
    resdone = {}
    for m in range(NCD):
        for nt in range(NT):
            bk = (m * NT + nt) % 4
            if bk in resdone:
                ctx.wait("tensor", "dve", resdone[bk])
            pst = ps[bk]
            for k in range(NCD):
                mm(ctx, pst, ow_sb[:, k * 1024 + m * 128: k * 1024 + (m + 1) * 128],
                   g_sb[:, k * T + nt * TC: k * T + (nt + 1) * TC],
                   k == 0, k == NCD - 1, inc="pe" if k == NCD - 1 else None)
            ctx.wait("vector", "pe")
            xs = xres[:, m * T + nt * TC: m * T + (nt + 1) * TC]
            resdone[bk] = tt(ctx, "vector", ALU.add, xs, xs, pst, inc="dve")
    ctx.wait("tensor", "dve")
    ctx.wait("scalar", "dve")
    ctx.tap(xres[:, 0:2048])
    ctx._off = L0A_G + NCD * T
    ctx.fence()
    emit_ln_plain(ctx, ps, xres)


def prep_l0attn(w):
    n, m, nf = T, 2 * T, T + 1
    F8 = np.float64
    ext = {}
    ext["uv_w"] = np.ascontiguousarray(w["L0_attn_uv_w"])
    ext["o_w"] = np.ascontiguousarray(w["L0_attn_o_w"])
    p = np.arange(m, dtype=F8)
    rel = np.where(p <= n, p, p - m)
    rel[n] = 0.0
    ext["rpe_pos_tiles"] = np.ascontiguousarray(
        rel.reshape(32, 128).T).astype(np.float32)
    ext["rpe_wrep32"] = np.tile(w["L0_attn_rpe_pos_w"], (128, 32)).astype(np.float32)
    ext["rpe_grep32"] = np.ascontiguousarray(
        np.tile(w["L0_attn_rpe_ln_w"][:, None, :], (1, 32, 1)).reshape(1, 3 * 1024)
        * np.ones((128, 1), np.float32)).astype(np.float32)
    ext["rpe_brep32"] = np.ascontiguousarray(
        np.tile(w["L0_attn_rpe_ln_b"][:, None, :], (1, 32, 1)).reshape(1, 3 * 1024)
        * np.ones((128, 1), np.float32)).astype(np.float32)
    ext["rpe_gorep32"] = np.tile(w["L0_attn_rpe_out_ln_w"][None, :], (128, 32)).astype(np.float32)
    ext["rpe_borep32"] = np.tile(w["L0_attn_rpe_out_ln_b"][None, :], (128, 32)).astype(np.float32)
    ext["rpe_fc"] = np.ascontiguousarray(w["L0_attn_rpe_fc_w"])
    ext["rpe_outw"] = np.ascontiguousarray(w["L0_attn_rpe_out_w"])
    gamma = 0.99
    g = gamma ** np.abs(rel)
    g[n] = 0.0
    k = np.arange(NF, dtype=F8)
    angA = 2.0 * np.pi * (k[None, :] * p[:, None]) / m
    fgr = np.cos(angA) * g[:, None]
    fgi = -np.sin(angA) * g[:, None]
    fgr[:, nf:] = 0.0
    fgi[:, nf:] = 0.0
    ext["fgr"] = fgr.astype(np.float32)
    ext["fgi"] = fgi.astype(np.float32)
    t = np.arange(n, dtype=F8)
    ang = 2.0 * np.pi * (t[:, None] * k[None, :]) / m
    frt = np.cos(ang)
    fit = -np.sin(ang)
    frt[:, nf:] = 0.0
    fit[:, nf:] = 0.0
    ext["frt"] = np.ascontiguousarray(
        frt.reshape(n, NKC, 128).transpose(1, 0, 2)).astype(np.float32)
    ext["fit"] = np.ascontiguousarray(
        fit.reshape(n, NKC, 128).transpose(1, 0, 2)).astype(np.float32)
    cw = np.full(NF, 2.0, dtype=F8)
    cw[0] = 1.0
    cw[n] = 1.0
    cw[nf:] = 0.0
    grT = (np.cos(ang) * cw[None, :] / m).T
    cwi = cw.copy()
    cwi[0] = 0.0
    cwi[n] = 0.0
    giT = (-np.sin(ang) * cwi[None, :] / m).T
    ext["grt"] = np.ascontiguousarray(
        grT.reshape(NKC, 128, NT, TC).transpose(2, 0, 1, 3)).astype(np.float32)
    ext["git"] = np.ascontiguousarray(
        giT.reshape(NKC, 128, NT, TC).transpose(2, 0, 1, 3)).astype(np.float32)
    ext["ident"] = np.eye(128, dtype=np.float32)
    return ext


L0A_SHAPES = {
    "uv_w": ((D, T), F32), "o_w": ((D, D), F32),
    "rpe_pos_tiles": ((128, 32), F32), "rpe_wrep32": ((128, 1024), F32),
    "rpe_grep32": ((128, 3072), F32), "rpe_brep32": ((128, 3072), F32),
    "rpe_gorep32": ((128, 1024), F32), "rpe_borep32": ((128, 1024), F32),
    "rpe_fc": ((3, 32, 32), F32), "rpe_outw": ((32, D), F32),
    "fgr": ((2 * T, NF), F32), "fgi": ((2 * T, NF), F32),
    "frt": ((NKC, T, 128), F32), "fit": ((NKC, T, 128), F32),
    "grt": ((NT, NKC, 128, TC), F32), "git": ((NT, NKC, 128, TC), F32),
    "ident": ((128, 128), F32),
}



L0F_XBF = 32768
L0F_BDMK = 36864
L0F_BDUP = 40960
L0F_WCT = 57344
L0F_Y = 73728
L0F_Z = 77824
L0F_ACC = 86016


def phase_l0_ffn(ctx, ps, io):
    C = ctx.C
    xres = C["xres"]
    ctx.fence()
    xbf_tc = ctx.at_b(L0F_XBF, 8 * TC)
    bdmk = ctx.at_b(L0F_BDMK, 32 * 128)
    bdup = ctx.at_b(L0F_BDUP, 32 * 512)
    wct_sb = ctx.at_b(L0F_WCT, 16 * 1024)
    ybuf = ctx.at_b(L0F_Y, 8 * TC)
    zbuf = ctx.at_b(L0F_Z, 16 * TC)
    outacc = ctx.at_f(L0F_ACC, 8 * TC)

    ctx.dma("gpsimd", bdmk.rearrange("p (g n) -> p g n", g=32),
            io["bdmk"].rearrange("h c p n -> p (h c) n"), inc="gld")
    ctx.dma("gpsimd", bdup.rearrange("p (g n) -> p g n", g=32),
            io["bdup"].rearrange("h c p n -> p (h c) n"), inc="gld")
    wct_use = None
    for tc in range(NT):
        t0 = tc * TC
        for c in range(NCD):
            copy(ctx, "vector", xbf_tc[:, c * TC:(c + 1) * TC],
                 xres[:, c * T + t0:c * T + t0 + TC])
        vxb = ctx.val("dve")
        ctx.wait("tensor", "dve", vxb)
        ctx.wait("tensor", "gld")
        for h in range(4):
            ydone = {}
            for c in range(NCD):
                bk = c % 4
                if bk in ydone:
                    ctx.wait("tensor", "dve", ydone[bk])
                mm(ctx, ps[bk], bdmk[:, (h * 8 + c) * 128:(h * 8 + c + 1) * 128],
                   xbf_tc[:, c * TC:(c + 1) * TC], True, True, inc="pe")
                ctx.wait("vector", "pe")
                ydone[bk] = copy(ctx, "vector", ybuf[:, c * TC:(c + 1) * TC], ps[bk])
            for half in range(2):
                if wct_use is not None:
                    ctx.wait("gpsimd", "pe", wct_use)
                r0 = h * 4096 + half * 2048
                vw = ctx.dma("gpsimd", wct_sb.rearrange("p (z n) -> p z n", z=16),
                             io["wct"][r0:r0 + 2048, :].rearrange(
                                 "(z p) n -> p z n", p=128), inc="qa")
                zdone = {}
                for zc in range(16):
                    zg = half * 16 + zc
                    kc, sub = zg // 4, zg % 4
                    bk = zc % 4
                    if bk in zdone:
                        ctx.wait("tensor", "act", zdone[bk])
                    ctx.wait("tensor", "dve")
                    mm(ctx, ps[bk],
                       bdup[:, (h * 8 + kc) * 512 + sub * 128:(h * 8 + kc) * 512 + (sub + 1) * 128],
                       ybuf[:, kc * TC:(kc + 1) * TC], True, True, inc="pe")
                    ctx.wait("scalar", "pe")
                    zdone[bk] = act(ctx, zbuf[:, zc * TC:(zc + 1) * TC], ps[bk],
                                    AF.Gelu, inc="act")
                ctx.wait("tensor", "act")
                ctx.wait("tensor", "qa", vw)
                cdone = {}
                for m in range(NCD):
                    bk = 4 + m % 4
                    if bk in cdone:
                        ctx.wait("tensor", "dve", cdone[bk])
                    for zc in range(16):
                        mm(ctx, ps[bk],
                           wct_sb[:, zc * 1024 + m * 128:zc * 1024 + (m + 1) * 128],
                           zbuf[:, zc * TC:(zc + 1) * TC], zc == 0, zc == 15,
                           inc="pe" if zc == 15 else None)
                    ctx.wait("vector", "pe")
                    dst = outacc[:, m * TC:(m + 1) * TC]
                    if h == 0 and half == 0:
                        cdone[bk] = copy(ctx, "vector", dst, ps[bk])
                    else:
                        cdone[bk] = tt(ctx, "vector", ALU.add, dst, dst, ps[bk])
                wct_use = ctx.val("pe")
        ctx.wait("vector", "pe")
        for c in range(NCD):
            xs = xres[:, c * T + t0:c * T + t0 + TC]
            tt(ctx, "vector", ALU.add, xs, xs, outacc[:, c * TC:(c + 1) * TC])
        ctx.wait("tensor", "dve")
    ctx.fence()
    emit_ln_plain(ctx, ps, xres)


def prep_l0ffn(w):
    RM = w["L0_ffn_rail_maker"]
    RU = w["L0_ffn_rail_up_proj"]
    bdmk = np.zeros((4, 8, 128, 128), dtype=np.float32)
    bdup = np.zeros((4, 8, 128, 512), dtype=np.float32)
    for h in range(4):
        for c in range(8):
            for rho in range(8):
                r = 8 * c + rho
                bdmk[h, c, 16 * rho:16 * rho + 16, 16 * rho:16 * rho + 16] = RM[h, r]
                bdup[h, c, 16 * rho:16 * rho + 16, 64 * rho:64 * rho + 64] = \
                    RU[h, r].T
    return {
        "bdmk": bdmk, "bdup": bdup,
        "wct": np.ascontiguousarray(w["L0_ffn_w_cproj"].T),
    }


L0F_SHAPES = {
    "bdmk": ((4, 8, 128, 128), F32), "bdup": ((4, 8, 128, 512), F32),
    "wct": ((16384, D), F32),
}



def attn_pipe(ctx, ps, C, units, P4, PT2, cols, psb2, pvdone):
    def bankwait(bk):
        if bk in pvdone:
            v = pvdone[bk]
            s, v = v if isinstance(v, tuple) else ("dve", v)
            ctx.wait("tensor", s, v)

    def stA(k, u):
        bankwait(k % 4)
        u["_sv"] = u["smm"](ps[k % 4])

    def stB(k, u):
        cl = cols[:, 4 * (k % 4):4 * (k % 4) + 4]
        ctx.wait("vector", "pe", u["_sv"])
        u["_vb"] = ctx.do("vector", lambda e, o=cl[:, 0:1],
                          i=ps[k % 4][:, 0:u["nk"]]:
                          e.reduce_max(o, i, axis=AX.X, negate=True), inc="dve")

    def stC(k, u):
        cl = cols[:, 4 * (k % 4):4 * (k % 4) + 4]
        P_b = P4[k % 4]
        if k - 4 >= 0 and "_vt" in units[k - 4]:
            ctx.wait("scalar", "pe", units[k - 4]["_vt"])
        ctx.wait("scalar", "dve", u["_vb"])
        u["_ve"] = act(ctx, P_b[:, 0:u["nk"]], ps[k % 4][:, 0:u["nk"]], AF.Exp,
                       bias=cl[:, 0:1], inc="act")
        pvdone[k % 4] = ("act", u["_ve"])

    def stD(k, u):
        cl = cols[:, 4 * (k % 4):4 * (k % 4) + 4]
        P_b = P4[k % 4]
        nk = u["nk"]
        ctx.wait("vector", "act", u["_ve"])
        vm = ctx.do("vector", lambda e, o=P_b[:, 0:nk], i=P_b[:, 0:nk],
                    m=u["mask_ap"], a=cl[:, 1:2]:
                    e.scalar_tensor_tensor(o, i, 1.0, m, ALU.mult, ALU.mult,
                                           accum_out=a))
        u["_vr"] = ctx.do("vector", lambda e, o=cl[:, 2:3], i=cl[:, 1:2]:
                          e.reciprocal(o, i), inc="dve")
        u["_vm"] = u["_vr"]

    def stN(k, u):
        cl = cols[:, 4 * (k % 4):4 * (k % 4) + 4]
        P_b = P4[k % 4]
        ctx.wait("scalar", "dve", u["_vr"])
        u["_vn"] = ctx.do("scalar", lambda e, o=P_b[:, 0:u["nk"]],
                          i=P_b[:, 0:u["nk"]], s=cl[:, 2:3]:
                          e.activation(o, i, AF.Identity, scale=s), inc="act")

    def stE(k, u):
        P_b = P4[k % 4]
        psb = psb2[k % 2]
        nk = u["nk"]
        ctx.wait("tensor", "act", u["_vn"])
        pk = 6 + (k % 2)
        bankwait(pk)
        if nk < 128:
            ctx.do("tensor", lambda e, o=psb[0:nk, 0:128], i=P_b[:, 0:nk],
                   d=C["ident"]: e.transpose(o, i, d), inc="pe")
        else:
            for kk in range(nk // 128):
                ctx.do("tensor", lambda e, o=psb[:, kk * 128:(kk + 1) * 128],
                       i=P_b[:, kk * 128:(kk + 1) * 128], d=C["ident"]:
                       e.transpose(o, i, d),
                       inc="pe" if kk == nk // 128 - 1 else None)
        u["_vt"] = ctx.val("pe")

    def stF(k, u):
        PT_b = PT2[k % 2]
        psb = psb2[k % 2]
        nk = u["nk"]
        eng = "vector" if k % 2 == 0 else "scalar"
        sem = "dve" if k % 2 == 0 else "act"
        if k - 2 >= 0 and "_vg" in units[k - 2]:
            ctx.wait(eng, "pe", units[k - 2]["_vg"])
        ctx.wait(eng, "pe", u["_vt"])
        def cp(e, o, i):
            return e.tensor_copy(o, i) if hasattr(e, "tensor_copy") else e.copy(o, i)
        if nk < 128:
            u["_vf"] = ctx.do(eng, lambda e, o=PT_b[0:nk, 0:128],
                              i=psb[0:nk, 0:128]: cp(e, o, i), inc=sem)
        else:
            u["_vf"] = ctx.do(eng, lambda e, o=PT_b[:, 0:nk], i=psb[:, 0:nk]:
                              cp(e, o, i), inc=sem)
        u["_vfs"] = sem
        pvdone[6 + (k % 2)] = (sem, u["_vf"])

    def stG(k, u):
        PT_b = PT2[k % 2]
        nk = u["nk"]
        ob = 4 + (k % 2)
        bankwait(ob)
        ctx.wait("tensor", u["_vfs"], u["_vf"])
        op = u["out_p"]
        if nk < 128:
            mm(ctx, ps[ob][0:op, 0:128], u["vsl"][0], PT_b[0:nk, 0:128],
               True, True, inc="pe")
        else:
            for kk in range(nk // 128):
                mm(ctx, ps[ob][0:op, 0:128], u["vsl"][kk],
                   PT_b[:, kk * 128:(kk + 1) * 128], kk == 0,
                   kk == nk // 128 - 1, inc="pe" if kk == nk // 128 - 1 else None)
        u["_vg"] = ctx.val("pe")

    def stH(k, u):
        ob = 4 + (k % 2)
        ctx.wait("scalar", "pe", u["_vg"])
        vH = ctx.do("scalar", lambda e, o=u["out_ap"],
                    i=ps[ob][0:u["out_p"], 0:128]: e.copy(o, i), inc="act")
        pvdone[ob] = ("act", vH)

    stages = [stA, stB, stC, stD, stN, stE, stF, stG, stH]
    n = len(units)
    for k in range(n + len(stages) - 1):
        for j in reversed(range(len(stages))):
            i = k - j
            if 0 <= i < n:
                stages[j](i, units[i])



L1A_LOC = 32768
L1A_WQKV = 49152
L1A_QBLK = 73728
L1A_KRING = 75776
L1A_VRING = 79872
L1A_XB = 83968
L1A_MASKS = 86016
L1A_SCR = 88064
L1A_GPK = 91648
L1A_GL = 92160
L1A_GR = 96256
L1A_ESEL = 100352
L1A_GQT = 49152
L1A_WG = 57344
L1A_REMT = 65536
L1A_PX = 73728
L1A_XBNT = 78848


def phase_l1_attn(ctx, ps, io):
    C = ctx.C
    xres = C["xres"]
    psb = C["psb"]
    psb2 = [psb, ps[7].bitcast(BF16)]
    ctx.fence()
    localT = ctx.at_b(L1A_LOC, 16384)
    wq_sb = ctx.at_b(L1A_WQKV, 8192)
    wk_sb = ctx.at_b(L1A_WQKV + 8192, 8192)
    wv_sb = ctx.at_b(L1A_WQKV + 16384, 8192)
    qblk = ctx.at_b(L1A_QBLK, 2048)
    kring = [ctx.at_b(L1A_KRING, 2048), ctx.at_b(L1A_KRING + 2048, 2048)]
    vring = [ctx.at_b(L1A_VRING, 2048), ctx.at_b(L1A_VRING + 2048, 2048)]
    xb_blk = ctx.at_b(L1A_XB, 2048)
    masksL = ctx.at_b(L1A_MASKS, 2048)
    scf = ctx.at_f(L1A_SCR, 512)
    scf2 = ctx.at_f(L1A_SCR + 1024, 512)
    P_b = ctx.at_b(L1A_SCR + 2048, 512)
    PT_b = ctx.at_b(L1A_SCR + 2560, 512)
    P_bB = ctx.at_b(L1A_XB, 512)
    PT_bB = ctx.at_b(L1A_XB + 512, 512)
    P4A = [P_b, P_bB, ctx.at_b(L1A_SCR, 512), ctx.at_b(L1A_SCR + 512, 512)]
    cols16 = ctx.at_f(91904, 16)
    remo_c = ctx.at_b(L1A_SCR + 3072, 512)
    cols = ctx.at_f(91904, 8)
    gpl_sb = ctx.at_b(L1A_GPK, 128)
    gpr_sb = ctx.at_b(L1A_GPK + 128, 128)
    glsig = ctx.at_f(L1A_GL, 2048)[0:16, :]
    grsig = ctx.at_f(L1A_GR, 2048)[0:16, :]
    esel = ctx.at_f(L1A_ESEL, 1024)[0:16, :]

    ctx.dma("gpsimd", wq_sb.rearrange("p (c t) -> p c t", c=NCD),
            io["wq"].rearrange("(c p) t -> p c t", p=128), inc="gld")
    ctx.dma("gpsimd", wk_sb.rearrange("p (c t) -> p c t", c=NCD),
            io["wk"].rearrange("(c p) t -> p c t", p=128), inc="gld")
    ctx.dma("gpsimd", wv_sb.rearrange("p (c t) -> p c t", c=NCD),
            io["wv"].rearrange("(c p) t -> p c t", p=128), inc="gld")
    ctx.dma("gpsimd", masksL.rearrange("p (v q f) -> p v q f", v=2, q=2),
            io["lmask"].rearrange("v q p f -> p v q f"), inc="gld")
    ctx.dma("gpsimd", gpl_sb.rearrange("p (c n) -> p c n", c=NCD),
            io["gpackl"].rearrange("(c p) n -> p c n", p=128), inc="gld")
    ctx.dma("gpsimd", gpr_sb.rearrange("p (c n) -> p c n", c=NCD),
            io["gpackr"].rearrange("(c p) n -> p c n", p=128), inc="gld")
    ctx.dma("sync", esel.rearrange("p (c n) -> p c n", c=8),
            io["esel"].rearrange("c h n -> h c n"), inc="ld")
    ctx.wait("tensor", "gld")
    ctx.wait("tensor", "ld")

    pvdone = {}

    def bankwait(bk):
        if bk in pvdone:
            v = pvdone[bk]
            s, v = v if isinstance(v, tuple) else ("dve", v)
            ctx.wait("tensor", s, v)

    for blk in range(8):
        t0 = blk * 256
        ctx.wait("vector", "pe")
        for k in range(NCD):
            copy(ctx, "vector", xb_blk[:, k * 256:(k + 1) * 256],
                 xres[:, k * T + t0:k * T + t0 + 256])
        ctx.wait("tensor", "dve")
        for dst, wsb in ((qblk, wq_sb), (kring[blk % 2], wk_sb)):
            qscaled = dst is qblk
            for m in range(NCD):
                bk = m % 4
                bankwait(bk)
                for k in range(NCD):
                    mm(ctx, ps[bk][:, 0:256],
                       wsb[:, k * 1024 + m * 128:k * 1024 + (m + 1) * 128],
                       xb_blk[:, k * 256:(k + 1) * 256], k == 0, k == NCD - 1,
                       inc="pe" if k == NCD - 1 else None)
                if qscaled:
                    ctx.wait("scalar", "pe")
                    pvdone[bk] = ("act", ctx.do(
                        "scalar", lambda e, o=dst[:, m * 256:(m + 1) * 256],
                        i=ps[bk][:, 0:256]:
                        e.activation(o, i, AF.Identity, scale=0.125), inc="act"))
                else:
                    ctx.wait("vector", "pe")
                    pvdone[bk] = copy(ctx, "vector", dst[:, m * 256:(m + 1) * 256],
                                      ps[bk][:, 0:256])
        for gp, dst, bk in ((gpl_sb, glsig, 4), (gpr_sb, grsig, 5)):
            bankwait(bk)
            for k in range(NCD):
                mm(ctx, ps[bk][0:16, 0:256], gp[:, k * 16:(k + 1) * 16],
                   qblk[:, k * 256:(k + 1) * 256], k == 0, k == NCD - 1,
                   inc="pe" if k == NCD - 1 else None)
            ctx.wait("scalar", "pe")
            va = act(ctx, dst[:, t0:t0 + 256], ps[bk][0:16, 0:256], AF.Sigmoid,
                     scale=8.0)
            pvdone[bk] = None
            ctx.wait("tensor", "act", va)
            pvdone.pop(bk)
        for ti in range(2):
            for nc2 in range(2):
                bk = 2 * ti + nc2
                bankwait(bk)
                for k in range(NCD):
                    mm(ctx, ps[bk],
                       xb_blk[:, k * 256 + ti * 128:k * 256 + ti * 128 + 128],
                       wv_sb[:, k * 1024 + nc2 * 512:k * 1024 + (nc2 + 1) * 512],
                       k == 0, k == NCD - 1, inc="pe" if k == NCD - 1 else None)
            ctx.wait("vector", "pe")
            for nc2 in range(2):
                pvdone[2 * ti + nc2] = ("dve", copy(
                    ctx, "vector",
                    vring[blk % 2][:, ti * 1024 + nc2 * 512:ti * 1024 + (nc2 + 1) * 512],
                    ps[2 * ti + nc2]))
        nk = 256 if blk == 0 else 512
        nkc = nk // 128
        kprev, kcur = kring[(blk - 1) % 2], kring[blk % 2]
        vprev, vcur = vring[(blk - 1) % 2], vring[blk % 2]
        units = []
        for h in range(16):
            ch, rh = h // 2, 64 * (h % 2)
            for qt in range(2):
                lq = qblk[rh:rh + 64, ch * 256 + qt * 128:ch * 256 + qt * 128 + 128]
                kp = kprev[rh:rh + 64, ch * 256:(ch + 1) * 256]
                kc_ = kcur[rh:rh + 64, ch * 256:(ch + 1) * 256]

                def smm(pbank, lq=lq, kp=kp, kc_=kc_, blk=blk):
                    if blk == 0:
                        return mm(ctx, pbank[:, 0:256], lq, kc_, True, True,
                                  inc="pe")
                    mm(ctx, pbank[:, 0:256], lq, kp, True, True)
                    return mm(ctx, pbank[:, 256:512], lq, kc_, True, True,
                              inc="pe")
                mvar = 0 if blk == 0 else 1
                moff = 256 if blk == 0 else 0
                mslice = masksL[:, mvar * 1024 + qt * 512 + moff:
                                mvar * 1024 + qt * 512 + moff + nk]
                vsl = [(vcur if (blk == 0 or kk >= 2) else vprev)[
                    :, (kk % 2) * 1024 + h * 64:(kk % 2) * 1024 + h * 64 + 64]
                    for kk in range(nkc)]
                out_ap = localT[rh:rh + 64,
                                ch * T + t0 + qt * 128:ch * T + t0 + qt * 128 + 128]
                units.append(dict(nk=nk, smm=smm, mask_ap=mslice, scale=0.125,
                                  vsl=vsl, out_ap=out_ap, out_p=64))
        attn_pipe(ctx, ps, C, units, P4A, [PT_b, PT_bB], cols16, psb2, pvdone)
    ctx.fence()

    gqt_sb = ctx.at_b(L1A_GQT, 8192)
    wg_sb = ctx.at_b(L1A_WG, 4096)
    remT = ctx.at_b(L1A_REMT, 8192)
    pxT = ctx.at_f(L1A_PX, 512)
    pxb = ctx.at_b(L1A_PX + 1024, 512)
    gkt_sb = ctx.at_b(L1A_PX + 1536, 256)
    gv_sb = ctx.at_b(L1A_PX + 1792, 512)[0:64, :]
    latmask = ctx.at_b(L1A_PX + 2304, 1024)
    xbnt = ctx.at_b(L1A_XBNT, 4096)

    ctx.dma("gpsimd", latmask.rearrange("p (q n) -> p q n", q=16),
            io["latmask"].rearrange("q p n -> p q n"), inc="gld")
    for k in range(NCD):
        ctx.do("vector", lambda e, o=pxT[:, k * 64:(k + 1) * 64],
               i=xres[:, k * T:(k + 1) * T].rearrange("p (l w) -> p l w", w=32):
               e.reduce_sum(o, i, axis=AX.X))
    ctx.do("vector", lambda e, o=pxb, i=pxT: e.tensor_scalar_mul(o, i, 1.0 / 32),
           inc="dve")
    ctx.wait("tensor", "dve")
    ctx.dma("gpsimd", wg_sb.rearrange("p (c t) -> p c t", c=NCD),
            io["wgk"].rearrange("(c p) t -> p c t", p=128), inc="gld")
    ctx.wait("tensor", "gld")
    for mc in range(4):
        bk = mc % 2
        bankwait(bk)
        for k in range(NCD):
            mm(ctx, ps[bk][:, 0:64], wg_sb[:, k * 512 + mc * 128:k * 512 + (mc + 1) * 128],
               pxb[:, k * 64:(k + 1) * 64], k == 0, k == NCD - 1,
               inc="pe" if k == NCD - 1 else None)
        ctx.wait("vector", "pe")
        pvdone[bk] = copy(ctx, "vector", gkt_sb[:, mc * 64:(mc + 1) * 64],
                          ps[bk][:, 0:64])
    ctx.wait("gpsimd", "pe")
    ctx.dma("gpsimd", wg_sb.rearrange("p (c t) -> p c t", c=NCD),
            io["wgv"].rearrange("(c p) t -> p c t", p=128), inc="gld")
    ctx.wait("tensor", "gld")
    bankwait(2)
    for k in range(NCD):
        mm(ctx, ps[2][0:64, :], pxb[:, k * 64:(k + 1) * 64],
           wg_sb[:, k * 512:(k + 1) * 512], k == 0, k == NCD - 1,
           inc="pe" if k == NCD - 1 else None)
    ctx.wait("vector", "pe")
    pvdone[2] = copy(ctx, "vector", gv_sb, ps[2][0:64, :])
    ctx.wait("gpsimd", "pe")
    ctx.dma("gpsimd", wg_sb.rearrange("p (c t) -> p c t", c=NCD),
            io["wgq"].rearrange("(c p) t -> p c t", p=128), inc="gld")
    ctx.wait("tensor", "gld")
    for nt in range(NT):
        tn = nt * TC
        ctx.wait("vector", "pe")
        for k in range(NCD):
            copy(ctx, "vector", xbnt[:, k * TC:(k + 1) * TC],
                 xres[:, k * T + tn:k * T + tn + TC])
        vx = ctx.val("dve")
        ctx.wait("tensor", "dve", vx)
        for mc in range(4):
            bk = mc % 2
            bankwait(bk)
            for k in range(NCD):
                mm(ctx, ps[bk], wg_sb[:, k * 512 + mc * 128:k * 512 + (mc + 1) * 128],
                   xbnt[:, k * TC:(k + 1) * TC], k == 0, k == NCD - 1,
                   inc="pe" if k == NCD - 1 else None)
            ctx.wait("scalar", "pe")
            pvdone[bk] = ("act", ctx.do(
                "scalar", lambda e, o=gqt_sb[:, mc * T + tn:mc * T + tn + TC],
                i=ps[bk]: e.activation(o, i, AF.Identity, scale=0.125),
                inc="act"))
    units = []
    for lh in range(8):
        rh = 64 * (lh % 2)
        mc = lh // 2
        for qt in range(16):
            gq = gqt_sb[rh:rh + 64, mc * T + qt * 128:mc * T + qt * 128 + 128]
            gk = gkt_sb[rh:rh + 64, mc * 64:(mc + 1) * 64]

            def smm(pbank, gq=gq, gk=gk):
                return mm(ctx, pbank[:, 0:64], gq, gk, True, True, inc="pe")
            lm = latmask[:, qt * 64:(qt + 1) * 64]
            out_ap = remT[rh:rh + 64, mc * T + qt * 128:mc * T + qt * 128 + 128]
            units.append(dict(nk=64, smm=smm, mask_ap=lm, scale=0.125,
                              vsl=[gv_sb[:, lh * 64:(lh + 1) * 64]],
                              out_ap=out_ap, out_p=64))
    attn_pipe(ctx, ps, C, units, P4A, [PT_b, PT_bB], cols16, psb2, pvdone)
    ctx.fence()

    rw_sb = ctx.at_b(L1A_WG, 4096)
    ctx.dma("gpsimd", rw_sb.rearrange("p (c t) -> p c t", c=4),
            io["r_out_w"].rearrange("(c p) t -> p c t", p=128), inc="gld")
    ctx.wait("tensor", "gld")
    for c in range(NCD):
        for nt in range(NT):
            tn = nt * TC
            bk0 = (c * NT + nt) % 2
            bankwait(bk0)
            for kk in range(4):
                mm(ctx, ps[bk0], rw_sb[:, kk * 1024 + c * 128:kk * 1024 + (c + 1) * 128],
                   remT[:, kk * T + tn:kk * T + tn + TC], kk == 0, kk == 3)
            bankwait(2)
            mm(ctx, ps[2], esel[:, c * 128:(c + 1) * 128], glsig[:, tn:tn + TC],
               True, True)
            bankwait(3)
            mm(ctx, ps[3], esel[:, c * 128:(c + 1) * 128], grsig[:, tn:tn + TC],
               True, True, inc="pe")
            ctx.wait("vector", "pe")
            vv = copy(ctx, "vector", remo_c, ps[bk0])
            lt = localT[:, c * T + tn:c * T + tn + TC]
            tt(ctx, "vector", ALU.mult, scf, ps[2], lt)
            pvdone[2] = ctx.val("dve")
            pvdone[bk0] = pvdone[2]
            tt(ctx, "vector", ALU.mult, scf2, ps[3], remo_c)
            pvdone[3] = ctx.val("dve")
            tt(ctx, "vector", ALU.add, lt, scf, scf2, inc="dve")
    ctx.fence()

    wo_sb = ctx.at_b(L1A_GQT, 8192)
    ctx.dma("gpsimd", wo_sb.rearrange("p (c t) -> p c t", c=NCD),
            io["w_out"].rearrange("(c p) t -> p c t", p=128), inc="gld")
    ctx.wait("tensor", "gld")
    resdone = {}
    for m in range(NCD):
        for nt in range(NT):
            bk = (m * NT + nt) % 4
            if bk in resdone:
                ctx.wait("tensor", "dve", resdone[bk])
            for k in range(NCD):
                mm(ctx, ps[bk], wo_sb[:, k * 1024 + m * 128:k * 1024 + (m + 1) * 128],
                   localT[:, k * T + nt * TC:k * T + (nt + 1) * TC],
                   k == 0, k == NCD - 1, inc="pe" if k == NCD - 1 else None)
            ctx.wait("vector", "pe")
            xs = xres[:, m * T + nt * TC:m * T + (nt + 1) * TC]
            resdone[bk] = tt(ctx, "vector", ALU.add, xs, xs, ps[bk], inc="dve")
    ctx.fence()
    emit_ln_plain(ctx, ps, xres)


def prep_l1attn(w):
    ext = {
        "wq": np.ascontiguousarray(w["L1_attn_wq"]),
        "wk": np.ascontiguousarray(w["L1_attn_wk"]),
        "wv": np.ascontiguousarray(w["L1_attn_wv"]),
        "wgq": np.ascontiguousarray(w["L1_attn_wgq"]),
        "wgk": np.ascontiguousarray(w["L1_attn_wgk"]),
        "wgv": np.ascontiguousarray(w["L1_attn_wgv"]),
        "r_out_w": np.ascontiguousarray(w["L1_attn_r_out_w"]),
        "w_out": np.ascontiguousarray(w["L1_attn_w_out"]),
    }
    gpl = np.zeros((1024, 16), dtype=np.float32)
    gpr = np.zeros((1024, 16), dtype=np.float32)
    for h in range(16):
        gpl[64 * h:64 * h + 64, h] = w["L1_attn_gate_local_w"][h]
        gpr[64 * h:64 * h + 64, h] = w["L1_attn_gate_remote_w"][h]
    ext["gpackl"] = gpl
    ext["gpackr"] = gpr
    i = np.arange(256)[:, None]
    j = np.arange(512)[None, :]
    base = (j > i) & (j <= i + 256)
    m1 = base.astype(np.float32).reshape(2, 128, 512)
    m0 = (base & (j >= 256)).astype(np.float32).reshape(2, 128, 512)
    ext["lmask"] = np.stack([m0, m1])
    tq = np.arange(2048)
    lv = (np.arange(64)[None, :] <= (tq // 32)[:, None])
    ext["latmask"] = lv.astype(np.float32).reshape(16, 128, 64)
    es = np.zeros((8, 16, 128), dtype=np.float32)
    for c in range(8):
        for p in range(128):
            es[c, (128 * c + p) // 64, p] = 1.0
    ext["esel"] = es
    return ext


L1A_SHAPES = {
    "wq": ((D, D), F32), "wk": ((D, D), F32), "wv": ((D, D), F32),
    "wgq": ((D, 512), F32), "wgk": ((D, 512), F32), "wgv": ((D, 512), F32),
    "r_out_w": ((512, D), F32), "w_out": ((D, D), F32),
    "gpackl": ((D, 16), F32), "gpackr": ((D, 16), F32),
    "lmask": ((2, 2, 128, 512), F32), "latmask": ((16, 128, 64), F32),
    "esel": ((8, 16, 128), F32),
}



LF_ATT = 32768
LF_WQKV = 49152
LF_XSB = 73728
LF_QBLK = 75776
LF_KRING = 77824
LF_VRING = 81920
LF_SQ = 86016
LF_RN = 88064
LF_QN = 88576
LF_SCF = 89088
LF_COLS = 90112
LF_BCOL = 90176
LF_HPK = 90240
LF_ROTB = 90368
LF_ESEL = 90496
LF_MASK = 91520
LF_CS = 93568
LF_UT = 73728
LF_SQS = 81920
LF_B1C = 84096


def phase_l1_ffn(ctx, ps, io):
    C = ctx.C
    xres = C["xres"]
    psb = C["psb"]
    psb2 = [psb, ps[7].bitcast(BF16)]
    ctx.fence()
    attT = ctx.at_b(LF_ATT, 16384)
    wqkv = ctx.at_b(LF_WQKV, 24576)
    xs_blk = ctx.at_b(LF_XSB, 2048)
    qblk = ctx.at_b(LF_QBLK, 2048)
    kbuf = ctx.at_b(LF_KRING, 4096)
    vring = [ctx.at_b(LF_VRING, 2048), ctx.at_b(LF_VRING + 2048, 2048)]
    sqb = ctx.at_b(LF_SQ, 2048)
    maskv0 = ctx.at_b(101760, 512)
    rnorm = ctx.at_f(LF_RN, 256)[0:16, :]
    PT_b = ctx.at_b(LF_RN, 512)
    qn_b = ctx.at_b(LF_QN, 256)
    t1_b = ctx.at_b(LF_QN + 256, 256)
    P_b = ctx.at_b(LF_QN, 512)
    scf = ctx.at_f(LF_SCF, 512)
    scfB = ctx.at_f(LF_SQ, 512)
    P_bB = ctx.at_b(LF_SQ + 1024, 512)
    PT_bB = ctx.at_b(LF_SQ + 1536, 512)
    P4F = [P_b, P_bB, ctx.at_b(LF_SCF, 512), ctx.at_b(LF_SQ, 512)]
    cols16F = ctx.at_f(LF_SCF + 512, 16)
    cols = ctx.at_f(LF_COLS, 8)
    bcol = ctx.at_f(LF_BCOL, 32)
    hpack = ctx.at_b(LF_HPK, 128)
    rotb = ctx.at_b(LF_ROTB, 128)
    eselb = ctx.at_b(LF_ESEL, 1024)[0:16, :]
    lmask_b = ctx.at_b(LF_MASK, 2048)
    css = [ctx.at_b(LF_CS + i * 2048, 2048) for i in range(4)]

    ctx.dma("gpsimd", css[0], io["cs_t"], inc="gld")
    ctx.dma("gpsimd", css[1], io["ss_t"], inc="gld")
    ctx.dma("gpsimd", css[2], io["ci_t"], inc="gld")
    ctx.dma("gpsimd", css[3], io["si_t"], inc="gld")
    ctx.dma("gpsimd", hpack.rearrange("p (c n) -> p c n", c=8),
            io["hpack"].rearrange("c p n -> p c n"), inc="gld")
    ctx.dma("gpsimd", rotb, io["rotb"], inc="gld")
    ctx.dma("gpsimd", lmask_b.rearrange("p (v q f) -> p v q f", v=2, q=2),
            io["fmask"].rearrange("v q p f -> p v q f"), inc="gld")
    ctx.dma("gpsimd", maskv0.rearrange("p (q f) -> p q f", q=2),
            io["fmask0"].rearrange("q p f -> p q f"), inc="gld")
    ctx.dma("gpsimd", eselb.rearrange("p (c n) -> p c n", c=8),
            io["esel"].rearrange("c h n -> h c n"), inc="gld")
    ctx.wait("tensor", "gld")
    ctx.wait("sync", "dve")
    vspill = ctx.dma("sync", io["spill_x"].rearrange("(c p) t -> p c t", p=128),
                     xres.rearrange("p (c t) -> p c t", c=NCD), inc="ld")

    pvdone = {}

    def bankwait(bk):
        if bk in pvdone:
            v = pvdone[bk]
            s, v = v if isinstance(v, tuple) else ("dve", v)
            ctx.wait("tensor", s, v)

    CH = [(0, 512), (512, 1024), (1024, 1536), (1536, 2048), (2048, 2560),
          (2560, 2730)]

    for li in range(6):
        ctx.fence()
        pvdone.clear()
        xsT = ctx.at_b(49152, 16384)
        attTf = ctx.at_b(49152, 16384)
        QTf = ctx.at_b(65536, 16384)
        KTf = ctx.at_b(0, 16384)
        Vf = ctx.at_b(16384, 16384)
        slab = ctx.at_b(81920, 8192)
        sqp = [ctx.at_b(32768, 2048), ctx.at_b(34816, 2048)]
        rn16 = ctx.at_f(36864, 2048)[0:16, :]
        rnb16 = ctx.at_b(40960, 2048)[0:16, :]
        qn2w = [ctx.at_b(43008, 512), ctx.at_b(43520, 512)]
        t12w = [ctx.at_b(44032, 512), ctx.at_b(44544, 512)]
        scfw = [ctx.at_f(45056, 512), ctx.at_f(46080, 512)]
        P4w = [ctx.at_b(86016 + i * 512, 512) for i in range(4)]
        PT2w = [ctx.at_b(88064, 512), ctx.at_b(88576, 512)]
        colsw = ctx.at_f(89088, 16)
        ctx.dma("sync", bcol[:, 0:1], io["qsrep"][li], inc="ld")
        ctx.dma("sync", bcol[:, 1:2], io["ksrep"][li], inc="ld")
        for k in range(NCD):
            dst = xsT[:, k * T:(k + 1) * T]
            if k < 4:
                copy(ctx, "vector", dst, xres[:, k * T:(k + 1) * T])
            else:
                ctx.do("vector", lambda e, o=dst[:, 0:1]: e.memset(o, 0.0))
                copy(ctx, "vector", dst[:, 1:T], xres[:, k * T:k * T + T - 1])
        vxs = ctx.val("dve")
        ctx.wait("tensor", "dve", vxs)
        ctx.wait("sync", "dve", vxs)
        vspl = ctx.dma("sync", io["spill_x2"].rearrange("(c p) t -> p c t", p=128),
                       xres.rearrange("p (c t) -> p c t", c=NCD), inc="ld")
        ctx.wait("gpsimd", "pe")
        ctx.wait("gpsimd", "dve")
        ctx.wait("gpsimd", "act")
        for part, dstf in ((0, QTf), (1, KTf)):
            qs2 = "qa" if part == 0 else "qb"
            if part == 1:
                ctx.wait("gpsimd", "pe")
            vw = ctx.dma("gpsimd", slab.rearrange("p (c t) -> p c t", c=NCD),
                         io["Wqkv"][li].rearrange("(c p) t -> p c t", p=128)[
                             :, :, part * 1024:(part + 1) * 1024], inc=qs2)
            ctx.wait("tensor", qs2, vw)
            if part == 1:
                ctx.wait("vector", "ld", vspl)
            for m in range(NCD):
                for nt in range(NT):
                    bk = (m * NT + nt) % 4
                    bankwait(bk)
                    for k in range(NCD):
                        mm(ctx, ps[bk], slab[:, k * 1024 + m * 128:k * 1024 + (m + 1) * 128],
                           xsT[:, k * T + nt * TC:k * T + (nt + 1) * TC],
                           k == 0, k == NCD - 1, inc="pe" if k == NCD - 1 else None)
                    ctx.wait("vector", "pe")
                    pvdone[bk] = copy(
                        ctx, "vector",
                        dstf[:, m * T + nt * TC:m * T + (nt + 1) * TC], ps[bk])
        ctx.wait("gpsimd", "dve")
        ctx.wait("gpsimd", "pe")
        vw2 = ctx.dma("gpsimd", slab.rearrange("p (c t) -> p c t", c=NCD),
                      io["Wqkv"][li].rearrange("(c p) t -> p c t", p=128)[
                          :, :, 2048:3072], inc="qa")
        vst = [0]

        def VU():
            i = vst[0]
            if i >= 32:
                return
            vst[0] = i + 1
            tt_i, nc2 = i // 2, i % 2
            bk = i % 4
            if i == 0:
                ctx.wait("tensor", "qa", vw2)
            bankwait(bk)
            vm = None
            for k in range(NCD):
                vm = mm(ctx, ps[bk],
                        xsT[:, k * T + tt_i * 128:k * T + tt_i * 128 + 128],
                        slab[:, k * 1024 + nc2 * 512:k * 1024 + (nc2 + 1) * 512],
                        k == 0, k == NCD - 1,
                        inc="pe" if k == NCD - 1 else None)
            ctx.wait("scalar", "pe", vm)
            pvdone[bk] = ("act", act(
                ctx, Vf[:, tt_i * 1024 + nc2 * 512:tt_i * 1024 + (nc2 + 1) * 512],
                ps[bk], AF.Identity, inc="act"))

        gcyc = [0]
        ctx.wait("scalar", "ld", vspl)
        for which in range(2):
            tgtf = QTf if which == 0 else KTf
            csAf = css[0] if which == 0 else css[2]
            csBf = css[1] if which == 0 else css[3]
            scol = bcol[:, 0:1] if which == 0 else bcol[:, 1:2]
            ctx.wait("scalar", "dve")
            ctx.wait("scalar", "pe")
            sqd = {}
            for bb in (4, 5, 6, 7):
                bankwait(bb)
            for c in range(NCD):
                sq = sqp[c % 2]
                if (c % 2) in sqd:
                    ctx.wait("scalar", "pe", sqd[c % 2])
                vsq = act(ctx, sq, tgtf[:, c * T:(c + 1) * T], AF.Square,
                          inc="act")
                ctx.wait("tensor", "act", vsq)
                for nt in range(NT):
                    sqd[c % 2] = mm(
                        ctx, ps[4 + nt][0:16, 0:TC], hpack[:, c * 16:(c + 1) * 16],
                        sq[:, nt * TC:(nt + 1) * TC], c == 0, c == NCD - 1,
                        inc="pe")
            ctx.wait("vector", "pe")
            for nt in range(NT):
                copy(ctx, "vector", rn16[:, nt * TC:(nt + 1) * TC],
                     ps[4 + nt][0:16, 0:TC],
                     inc="dve" if nt == NT - 1 else None)
            ctx.do("vector", lambda e, o=rn16, i=rn16:
                   e.tensor_scalar_max(o, i, 1e-24), inc="dve")
            for bb in (4, 5, 6, 7):
                pvdone[bb] = ctx.val("dve")
            ctx.wait("scalar", "dve")
            act(ctx, rn16, rn16, AF.Ln)
            va = act(ctx, rn16, rn16, AF.Exp, scale=-0.5, inc="act")
            ctx.wait("vector", "act", va)
            vrb = copy(ctx, "vector", rnb16, rn16, inc="dve")
            ctx.wait("tensor", "dve", vrb)
            st_r = {}

            def R1(u):
                c, nt = u // NT, u % NT
                bankwait(4 + u % 2)
                st_r[("vr1", u)] = mm(ctx, ps[4 + u % 2], eselb[:, c * 128:(c + 1) * 128],
                                      rnb16[:, nt * TC:(nt + 1) * TC], True, True,
                                      inc="pe")

            def R23(u):
                c, nt = u // NT, u % NT
                par = u % 2
                ctx.wait("vector", "pe", st_r[("vr1", u)])
                vq = ctx.do("vector", lambda e, o=qn2w[par],
                            i=tgtf[:, c * T + nt * TC:c * T + (nt + 1) * TC],
                            s=scol, r=ps[4 + par]:
                            e.scalar_tensor_tensor(o, i, s, r, ALU.mult,
                                                   ALU.mult), inc="dve")
                pvdone[4 + par] = vq
                ctx.wait("tensor", "dve", vq)
                bankwait(6 + par)
                st_r[("vrot", u)] = mm(ctx, ps[6 + par], rotb, qn2w[par],
                                       True, True, inc="pe")

            def R4(u):
                c, nt = u // NT, u % NT
                par = u % 2
                sc2 = scfw[par]
                tt(ctx, "vector", ALU.mult, t12w[par], qn2w[par],
                   csAf[:, nt * TC:(nt + 1) * TC])
                ctx.wait("vector", "pe", st_r[("vrot", u)])
                tt(ctx, "vector", ALU.mult, sc2, ps[6 + par],
                   csBf[:, nt * TC:(nt + 1) * TC])
                pvdone[6 + par] = ctx.val("dve")
                tt(ctx, "vector", ALU.add,
                   tgtf[:, c * T + nt * TC:c * T + (nt + 1) * TC],
                   t12w[par], sc2, inc="dve")

            NU = NCD * NT
            for u in range(NU + 1):
                if u < NU:
                    R1(u)
                    R23(u)
                if gcyc[0] % 2 == 0:
                    VU()
                gcyc[0] += 1
                if u >= 1:
                    R4(u - 1)
        while vst[0] < 32:
            VU()
        units = []
        for blk in range(8):
            t0 = blk * 256
            nk = 256 if blk == 0 else 512
            k0 = 0 if blk == 0 else (blk - 1) * 256
            for h in range(16):
                ch, rh = h // 2, 64 * (h % 2)
                for qt in range(2):
                    lq = QTf[rh:rh + 64,
                             ch * T + t0 + qt * 128:ch * T + t0 + qt * 128 + 128]
                    krhs = KTf[rh:rh + 64, ch * T + k0:ch * T + k0 + nk]

                    def smm(pbank, lq=lq, krhs=krhs, nk=nk):
                        return mm(ctx, pbank[:, 0:nk], lq, krhs, True, True,
                                  inc="pe")
                    if blk == 0:
                        msl = maskv0[:, qt * 256:qt * 256 + 256]
                    else:
                        msl = lmask_b[:, 1024 + qt * 512:1024 + qt * 512 + nk]
                    vt0 = 0 if blk == 0 else 2 * (blk - 1)
                    vsl = [Vf[:, (vt0 + kk) * 512 + (0 if True else 0):0] if False
                           else Vf[:, (vt0 + kk) // 2 * 1024 + ((vt0 + kk) % 2) * 512:0]
                           if False else
                           Vf[:, (k0 + kk * 128) // 128 * 0:0]
                           for kk in range(0)]
                    vsl = []
                    for kk in range(nk // 128):
                        tok = k0 + kk * 128
                        ti2 = tok // 128
                        vsl.append(Vf[:, (ti2 // 2) * 1024 + (ti2 % 2) * 512 + 0:0]
                                   if False else
                                   Vf[:, (ti2 // 2) * 1024 + (ti2 % 2) * 512 + h * 0:0]
                                   if False else None)
                    vsl = []
                    for kk in range(nk // 128):
                        tok = k0 + kk * 128
                        vti = tok // 128
                        base = (vti // 2) * 1024 + 0
                        vsl.append(Vf[:, vti * 1024 + h * 64:vti * 1024 + h * 64 + 64])
                    out_ap = attTf[rh:rh + 64,
                                   ch * T + t0 + qt * 128:ch * T + t0 + qt * 128 + 128]
                    units.append(dict(nk=nk, smm=smm, mask_ap=msl,
                                      vsl=vsl, out_ap=out_ap, out_p=64))
        psb2w = [ps[6].bitcast(BF16), ps[7].bitcast(BF16)]
        attn_pipe(ctx, ps, C, units, P4w, PT2w, colsw, psb2w, pvdone)
        ctx.wait("sync", "dve")
        ctx.wait("sync", "pe")
        vxr = ctx.dma("sync", xres.rearrange("p (c t) -> p c t", c=NCD),
                      io["spill_x2"].rearrange("(c p) t -> p c t", p=128),
                      inc="ld")
        ctx.wait("vector", "ld", vxr)
        ctx.wait("tensor", "ld", vxr)
        ctx.fence()
        pvdone.clear()
        wo_sb = ctx.at_b(65536, 8192)
        ctx.dma("gpsimd", wo_sb.rearrange("p (c t) -> p c t", c=NCD),
                io["Wo"][li].rearrange("(c p) t -> p c t", p=128), inc="gld")
        ctx.wait("tensor", "gld")
        if li == 0:
            ctx.wait("vector", "ld", vspill)
        for m in range(NCD):
            for nt in range(NT):
                bk = (m * NT + nt) % 4
                bankwait(bk)
                for k in range(NCD):
                    mm(ctx, ps[bk], wo_sb[:, k * 1024 + m * 128:k * 1024 + (m + 1) * 128],
                       attTf[:, k * T + nt * TC:k * T + (nt + 1) * TC],
                       k == 0, k == NCD - 1, inc="pe" if k == NCD - 1 else None)
                ctx.wait("vector", "pe")
                xs = xres[:, m * T + nt * TC:m * T + (nt + 1) * TC]
                pvdone[bk] = tt(ctx, "vector", ALU.add, xs, xs, ps[bk], inc="dve")

        ctx.fence()
        pvdone.clear()
        xnT = ctx.at_b(LF_ATT, 16384)
        reg1 = ctx.at_f(LF_WQKV, 2048)
        reg2 = ctx.at_f(LF_WQKV + 4096, 2048)
        uT_c = ctx.at_b(LF_WQKV, 8192)
        wbuf = [ctx.at_b(LF_WQKV + 8192, 12288),
                ctx.at_b(LF_WQKV + 20480, 12288)]
        sqs = [ctx.at_b(LF_SQS, 512), ctx.at_b(LF_SQS + 512, 512)]
        b1c = ctx.at_f(LF_B1C, 8)
        ctx.dma("sync", bcol[:, 4:12], io["lng8"][li], inc="ld")
        ctx.dma("sync", bcol[:, 12:20], io["lnb8"][li], inc="ld")
        ctx.dma("sync", bcol[:, 20:28], io["b28"][li], inc="ld")
        vld = ctx.val("ld")
        ctx.wait("vector", "ld", vld)
        ctx.wait("scalar", "ld", vld)

        CHM = [([(0, 128), (128, 256), (256, 384), (384, 512)] if c1 - c0 == 512
                else [(0, 128), (128, 170)]) for c0, c1 in CH]

        def load_chunk(ci):
            c0, c1 = CH[ci]
            w = c1 - c0
            wb = wbuf[ci % 2]
            qsem = "qa" if ci % 2 == 0 else "qb"
            ctx.dma("gpsimd",
                    wb[:, 0:4096].rearrange("p (c t) -> p c t", c=NCD)[:, :, 0:w],
                    io["W1"][li].rearrange("(c p) t -> p c t", p=128)[:, :, c0:c1],
                    inc=qsem)
            ctx.dma("gpsimd",
                    wb[:, 4096:8192].rearrange("p (c t) -> p c t", c=NCD)[:, :, 0:w],
                    io["W1"][li].rearrange("(c p) t -> p c t", p=128)[
                        :, :, 2730 + c0:2730 + c1], inc=qsem)
            for kk, (k0, k1) in enumerate(CHM[ci]):
                ctx.dma("gpsimd",
                        wb[0:k1 - k0, 8192 + kk * 1024:8192 + (kk + 1) * 1024],
                        io["W2"][li][c0 + k0:c0 + k1, :], inc=qsem)
            return ctx.val(qsem)

        wready = {0: load_chunk(0)}
        sqdone = {}
        for half in range(2):
            for nt in range(NT):
                tn = nt * TC
                for c4 in range(4):
                    c = half * 4 + c4
                    sq = sqs[c4 % 2]
                    if (c4 % 2) in sqdone:
                        ctx.wait("scalar", "pe", sqdone[c4 % 2])
                    vsq = act(ctx, sq, xres[:, c * T + tn:c * T + tn + TC],
                              AF.Square, inc="act")
                    ctx.wait("tensor", "act", vsq)
                    mm(ctx, ps[0][0:1, 0:TC], C["onesf"],
                       xres[:, c * T + tn:c * T + tn + TC], c4 == 0, c4 == 3)
                    sqdone[c4 % 2] = mm(ctx, ps[1][0:1, 0:TC], C["onesb"], sq,
                                        c4 == 0, c4 == 3, inc="pe")
                ctx.wait("vector", "pe")
                rS = reg1[0:1, :] if half == 0 else reg1[32:33, :]
                rQ = reg1[64:65, :] if half == 0 else reg1[96:97, :]
                copy(ctx, "vector", rS[:, tn:tn + TC], ps[0][0:1, 0:TC])
                copy(ctx, "vector", rQ[:, tn:tn + TC], ps[1][0:1, 0:TC],
                     inc="dve")
                ctx.wait("tensor", "dve")
        copy(ctx, "vector", reg2[0:1, 1:T], reg1[32:33, 0:T - 1])
        tt(ctx, "vector", ALU.add, reg2[0:1, 1:T], reg2[0:1, 1:T],
           reg1[0:1, 1:T])
        copy(ctx, "vector", reg2[0:1, 0:1], reg1[0:1, 0:1])
        copy(ctx, "vector", reg2[64:65, 1:T], reg1[96:97, 0:T - 1])
        tt(ctx, "vector", ALU.add, reg2[64:65, 1:T], reg2[64:65, 1:T],
           reg1[64:65, 1:T])
        copy(ctx, "vector", reg2[64:65, 0:1], reg1[64:65, 0:1])
        ctx.do("vector", lambda e, o=reg2[0:1, :], i=reg2[0:1, :]:
               e.tensor_scalar_mul(o, i, 1.0 / D))
        ctx.do("vector", lambda e, o=reg2[64:65, :], i=reg2[64:65, :]:
               e.tensor_scalar_mul(o, i, 1.0 / D))
        tt(ctx, "vector", ALU.mult, reg1[64:65, :], reg2[0:1, :], reg2[0:1, :])
        tt(ctx, "vector", ALU.subtract, reg2[64:65, :], reg2[64:65, :],
           reg1[64:65, :], inc="dve")
        ctx.wait("scalar", "dve")
        act(ctx, reg2[64:65, :], reg2[64:65, :], AF.Ln, bias=C["eps64"])
        va = act(ctx, reg1[64:65, :], reg2[64:65, :], AF.Exp, scale=-0.5,
                 inc="act")
        ctx.wait("vector", "act", va)
        copy(ctx, "vector", reg1[0:1, :], reg1[64:65, :], inc="dve")
        ctx.wait("tensor", "act")
        ctx.wait("tensor", "dve")
        def ffn_bcast(nt):
            tn = nt * TC
            b0, b1 = (2, 3) if nt % 2 == 0 else (6, 7)
            bankwait(b0)
            mm(ctx, ps[b0], C["ones1f"], reg2[0:1, tn:tn + TC], True, True)
            bankwait(b1)
            mm(ctx, ps[b1], C["ones1f"], reg1[0:1, tn:tn + TC], True, True,
               inc="pe")
            return ctx.val("pe")

        vbcs = {0: ffn_bcast(0), 1: ffn_bcast(1)}
        for nt in range(NT):
            tn = nt * TC
            b0, b1 = (2, 3) if nt % 2 == 0 else (6, 7)
            ctx.wait("vector", "pe", vbcs[nt])
            for c in range(NCD):
                gcol = bcol[:, 4 + c:5 + c]
                bcl = bcol[:, 12 + c:13 + c]
                dst = xnT[:, c * T + tn:c * T + tn + TC]
                if c < 4:
                    xsrc = xres[:, c * T + tn:c * T + tn + TC]
                elif nt > 0:
                    xsrc = xres[:, c * T + tn - 1:c * T + tn + TC - 1]
                else:
                    xsrc = None
                if xsrc is not None:
                    tt(ctx, "vector", ALU.subtract, scf, xsrc, ps[b0])
                    ctx.do("vector", lambda e, o=scf, i=scf, s=gcol, r=ps[b1]:
                           e.scalar_tensor_tensor(o, i, s, r, ALU.mult, ALU.mult))
                    ctx.do("vector", lambda e, o=dst, i=scf, s=bcl:
                           e.tensor_scalar_add(o, i, s), inc="dve")
                else:
                    tt(ctx, "vector", ALU.subtract, scf[:, 1:TC],
                       xres[:, c * T:c * T + TC - 1], ps[b0][:, 1:TC])
                    ctx.do("vector", lambda e, o=scf[:, 1:TC], i=scf[:, 1:TC],
                           s=gcol, r=ps[b1][:, 1:TC]:
                           e.scalar_tensor_tensor(o, i, s, r, ALU.mult, ALU.mult))
                    ctx.do("vector", lambda e, o=dst[:, 1:TC], i=scf[:, 1:TC],
                           s=bcl: e.tensor_scalar_add(o, i, s))
                    copy(ctx, "vector", cols[:, 3:4], ps[b0][:, 0:1])
                    tt(ctx, "vector", ALU.mult, cols[:, 3:4], cols[:, 3:4],
                       ps[b1][:, 0:1])
                    ctx.do("vector", lambda e, o=cols[:, 4:5], i=cols[:, 3:4],
                           s=gcol: e.tensor_scalar_mul(o, i, s))
                    ctx.do("vector", lambda e, o=cols[:, 4:5], i=cols[:, 4:5]:
                           e.tensor_scalar_mul(o, i, -1.0))
                    ctx.do("vector", lambda e, o=dst[:, 0:1], i=cols[:, 4:5],
                           s=bcl: e.tensor_scalar_add(o, i, s), inc="dve")
            pvdone[b0] = ctx.val("dve")
            pvdone[b1] = pvdone[b0]
            if nt + 2 < NT:
                vbcs[nt + 2] = ffn_bcast(nt + 2)
        ctx.wait("tensor", "dve")
        ctx.wait("gpsimd", "dve")
        ctx.wait("gpsimd", "pe")
        ctx.wait("gpsimd", "act")
        wuse = {}
        for ci, (c0, c1) in enumerate(CH):
            mcs = CHM[ci]
            wb = wbuf[ci % 2]
            qsem = "qa" if ci % 2 == 0 else "qb"
            vqa = wready.pop(ci)
            if ci + 1 < len(CH):
                if ((ci + 1) % 2) in wuse:
                    ctx.wait("gpsimd", "pe", wuse[(ci + 1) % 2])
                wready[ci + 1] = load_chunk(ci + 1)
            ctx.wait("sync", "act")
            ctx.dma("sync", b1c[:, 0:4], io["b1a_t"][li, ci], inc="ld")
            ctx.dma("sync", b1c[:, 4:8], io["b1g_t"][li, ci], inc="ld")
            vb1 = ctx.val("ld")
            ctx.wait("tensor", qsem, vqa)
            ctx.wait("scalar", "ld", vb1)
            for mi, (m0, m1) in enumerate(mcs):
                mw = m1 - m0
                for nt in range(NT):
                    tn = nt * TC
                    it = mi * NT + nt
                    bka, bkg = 4 + 2 * (it % 2) - 4 * (it % 2 and False), 5
                    bka = (4, 6)[it % 2]
                    bkg = (5, 7)[it % 2]
                    bankwait(bka)
                    for k in range(NCD):
                        mm(ctx, ps[bka][0:mw, :], wb[:, k * 512 + m0:k * 512 + m1],
                           xnT[:, k * T + tn:k * T + tn + TC], k == 0, k == NCD - 1)
                    bankwait(bkg)
                    for k in range(NCD):
                        mm(ctx, ps[bkg][0:mw, :],
                           wb[:, 4096 + k * 512 + m0:4096 + k * 512 + m1],
                           xnT[:, k * T + tn:k * T + tn + TC], k == 0, k == NCD - 1,
                           inc="pe" if k == NCD - 1 else None)
                    ctx.wait("scalar", "pe")
                    ctx.wait("scalar", "dve")
                    act(ctx, P_b[0:mw, :], ps[bka][0:mw, :], AF.Identity,
                        bias=b1c[0:mw, mi:mi + 1])
                    va = act(ctx, PT_b[0:mw, :], ps[bkg][0:mw, :], AF.Gelu,
                             bias=b1c[0:mw, 4 + mi:5 + mi], inc="act")
                    ctx.wait("vector", "act", va)
                    vu = tt(ctx, "vector", ALU.mult,
                            uT_c[0:mw, mi * T + tn:mi * T + tn + TC],
                            P_b[0:mw, :], PT_b[0:mw, :], inc="dve")
                    pvdone[bka] = ("dve", vu)
                    pvdone[bkg] = ("dve", vu)
            ctx.wait("tensor", "dve")
            for mo in range(NCD):
                for nt in range(NT):
                    tn = nt * TC
                    bk = (mo * NT + nt) % 4
                    bankwait(bk)
                    for kk, (k0, k1) in enumerate(mcs):
                        mm(ctx, ps[bk],
                           wb[0:k1 - k0, 8192 + kk * 1024 + mo * 128:8192 + kk * 1024 + (mo + 1) * 128],
                           uT_c[0:k1 - k0, kk * T + tn:kk * T + tn + TC],
                           kk == 0, kk == len(mcs) - 1,
                           inc="pe" if kk == len(mcs) - 1 else None)
                    ctx.wait("vector", "pe")
                    xs = xres[:, mo * T + tn:mo * T + tn + TC]
                    pvdone[bk] = tt(ctx, "vector", ALU.add, xs, xs, ps[bk],
                                    inc="dve")
            wuse[ci % 2] = ctx.val("pe")
        for c in range(NCD):
            ctx.do("vector", lambda e, o=xres[:, c * T:(c + 1) * T],
                   i=xres[:, c * T:(c + 1) * T], s=bcol[:, 20 + c:21 + c]:
                   e.tensor_scalar_add(o, i, s),
                   inc="dve" if c == NCD - 1 else None)
        ctx.wait("tensor", "dve")
    ctx.fence()
    xin_f = ctx.at_f(LF_ATT, 8192)
    for half in range(2):
        vr = ctx.dma("sync",
                     xin_f.rearrange("p (c t) -> p c t", c=4),
                     io["spill_x"].rearrange("(c p) t -> p c t", p=128)[
                         :, half * 4:half * 4 + 4, :], inc="ld")
        ctx.wait("vector", "ld", vr)
        for c4 in range(4):
            c = half * 4 + c4
            xs = xres[:, c * T:(c + 1) * T]
            tt(ctx, "vector", ALU.add, xs, xs, xin_f[:, c4 * T:(c4 + 1) * T],
               inc="dve" if c4 == 3 else None)
        ctx.wait("sync", "dve")
    ctx.fence()
    ctx._off = LF_ATT
    emit_ln_plain(ctx, ps, xres)


def prep_l1ffn(w):
    DIM_HEAD, WINDOW = 64, 256
    n = T
    F8 = np.float64
    ext = {
        "Wqkv": np.ascontiguousarray(w["L1_ffn_Wqkv"]),
        "Wo": np.ascontiguousarray(w["L1_ffn_Wo"]),
        "W1": np.ascontiguousarray(w["L1_ffn_W1"]),
        "W2": np.ascontiguousarray(w["L1_ffn_W2"]),
    }
    ext["qsrep"] = np.tile(w["L1_ffn_q_scale"], (1, 2))[:, :, None].astype(np.float32)
    ext["ksrep"] = np.tile(w["L1_ffn_k_scale"], (1, 2))[:, :, None].astype(np.float32)
    ext["lng8"] = np.ascontiguousarray(
        w["L1_ffn_ln_g"].reshape(6, 8, 128).transpose(0, 2, 1))
    ext["lnb8"] = np.ascontiguousarray(
        w["L1_ffn_ln_b"].reshape(6, 8, 128).transpose(0, 2, 1))
    ext["b28"] = np.ascontiguousarray(
        w["L1_ffn_b2"].reshape(6, 8, 128).transpose(0, 2, 1))
    CHl = [(0, 512), (512, 1024), (1024, 1536), (1536, 2048), (2048, 2560),
           (2560, 2730)]
    b1a = np.zeros((6, 6, 4, 128), dtype=np.float32)
    b1g = np.zeros((6, 6, 4, 128), dtype=np.float32)
    for li in range(6):
        for ci, (c0, c1) in enumerate(CHl):
            for mi in range(4):
                a0 = c0 + mi * 128
                a1 = min(c1, a0 + 128)
                if a0 >= c1:
                    continue
                b1a[li, ci, mi, 0:a1 - a0] = w["L1_ffn_b1"][li, a0:a1]
                b1g[li, ci, mi, 0:a1 - a0] = w["L1_ffn_b1"][li, 2730 + a0:2730 + a1]
    ext["b1a_t"] = np.ascontiguousarray(b1a.transpose(0, 1, 3, 2))
    ext["b1g_t"] = np.ascontiguousarray(b1g.transpose(0, 1, 3, 2))
    half = np.arange(0, DIM_HEAD, 2, dtype=F8)
    inv_freq = 1.0 / 10000.0 ** (half / DIM_HEAD)
    t = np.arange(n, dtype=F8)
    freqs = t[:, None] * inv_freq[None, :]
    freqs = np.concatenate([freqs, freqs], -1)
    base_scale = (half + 0.4 * DIM_HEAD) / (1.4 * DIM_HEAD)
    power = (t - n // 2) / (WINDOW // 2)
    scale = base_scale[None, :] ** power[:, None]
    scale = np.concatenate([scale, scale], -1)
    cos, sin = np.cos(freqs), np.sin(freqs)
    ext["cs_t"] = np.tile((cos * scale * 8.0).T, (2, 1)).astype(np.float32)
    ext["ss_t"] = np.tile((sin * scale * 8.0).T, (2, 1)).astype(np.float32)
    ext["ci_t"] = np.tile((cos / scale).T, (2, 1)).astype(np.float32)
    ext["si_t"] = np.tile((sin / scale).T, (2, 1)).astype(np.float32)
    R = np.zeros((128, 128), dtype=np.float32)
    for m in range(128):
        if m % 64 < 32:
            R[m + 32, m] = -1.0
        else:
            R[m - 32, m] = 1.0
    ext["rotb"] = R
    hp = np.zeros((8, 128, 16), dtype=np.float32)
    for c in range(8):
        for p in range(128):
            hp[c, p, 2 * c + p // 64] = 1.0
    ext["hpack"] = hp
    def fm_var(b):
        i = np.arange(256)[:, None]
        j = np.arange(512)[None, :]
        q_abs = b * 256 + i
        if b % 2 == 0:
            k_abs = np.where(j < 256, b * 256 + j, (b - 1) * 256 + (j - 256))
        else:
            k_abs = np.where(j < 256, (b - 1) * 256 + j, b * 256 + (j - 256))
        v = (k_abs <= q_abs) & (q_abs <= k_abs + 256)
        return v.astype(np.float32).reshape(2, 128, 512)
    ext["fmask"] = np.stack([fm_var(2), fm_var(1)])
    i0 = np.arange(256)[:, None]
    j0 = np.arange(256)[None, :]
    v0 = (j0 <= i0)
    ext["fmask0"] = v0.astype(np.float32).reshape(2, 128, 256)
    return ext


L1F_SHAPES = {
    "Wqkv": ((6, D, 3072), F32), "Wo": ((6, D, D), F32),
    "W1": ((6, D, 5460), F32), "W2": ((6, 2730, D), F32),
    "qsrep": ((6, 128, 1), F32), "ksrep": ((6, 128, 1), F32),
    "lng8": ((6, 128, 8), F32), "lnb8": ((6, 128, 8), F32),
    "b28": ((6, 128, 8), F32),
    "b1a_t": ((6, 6, 128, 4), F32), "b1g_t": ((6, 6, 128, 4), F32),
    "cs_t": ((128, T), F32), "ss_t": ((128, T), F32),
    "ci_t": ((128, T), F32), "si_t": ((128, T), F32),
    "rotb": ((128, 128), F32), "hpack": ((8, 128, 16), F32),
    "fmask": ((2, 2, 128, 512), F32), "fmask0": ((2, 128, 256), F32),
}



_CACHE = {}


def _get_program():
    if "nc" not in _CACHE:
        allshapes = {}
        for s in (L0A_SHAPES, L0F_SHAPES, L1A_SHAPES, L1F_SHAPES):
            allshapes.update(s)
        _CACHE["nc"] = build_program(
            [phase_l0_attn, phase_l0_ffn, phase_l1_attn, phase_l1_ffn],
            allshapes)
    return _CACHE["nc"]


def kernel(**inputs):
    w = {k: np.asarray(v) for k, v in inputs.items()}
    ext = {}
    for p in (prep_l0attn, prep_l0ffn, prep_l1attn, prep_l1ffn):
        ext.update(p(w))
    nc = _get_program()
    in_maps = []
    for b in range(8):
        mdict = {"x_in": np.ascontiguousarray(w["L0_attn_x"][b].T)}
        mdict.update(ext)
        in_maps.append(mdict)
    res = run_bass_kernel_spmd(nc, in_maps, core_ids=list(range(8)))
    return np.stack([r["out"].T for r in res.results]).astype(np.float32)


# version 19
# speedup vs baseline: 1.1836x; 1.0139x over previous
import contextlib
import numpy as np
import concourse.bass as bass
import concourse.mybir as mybir
from concourse.bass_utils import run_bass_kernel_spmd

F32 = mybir.dt.float32
BF16 = mybir.dt.bfloat16
AF = mybir.ActivationFunctionType
ALU = mybir.AluOpType
AX = mybir.AxisListType

D = 1024
T = 2048
NCD = 8
NT = 4
TC = 512
EPS = 1e-5
NEG = -3.0e38

ENGINES = ("tensor", "vector", "scalar", "gpsimd", "sync")
ARENA_COLS = [102400]
XRES_OFF = 0
FREE0 = 32768


class Ctx:
    def __init__(self, nc):
        self.nc = nc
        self.streams = {e: [] for e in ENGINES}
        self.semh = {}
        self.semc = {}
        self.watermark = {e: {} for e in ENGINES}
        self.C = {}
        self._arena = None
        self._off = 0

    def add_sem(self, name, handle):
        self.semh[name] = handle
        self.semc[name] = 0

    def val(self, sem):
        return self.semc[sem]

    def wait(self, eng, sem, val=None):
        v = self.semc[sem] if val is None else val
        if v <= 0 or self.watermark[eng].get(sem, 0) >= v:
            return
        self.watermark[eng][sem] = v
        h = self.semh[sem]
        self.streams[eng].append(lambda e, h=h, v=v: e.wait_ge(h, v))

    AUTO = {"vector": "dve", "scalar": "act", "gpsimd": "pool"}

    def do(self, eng, fn, inc=None, n=1, is_dma=False):
        if eng in self.AUTO and not is_dma:
            sname = self.AUTO[eng]
            assert inc is None or inc == sname
            self.wait(eng, sname)
            self.semc[sname] += 1
            h = self.semh[sname]
            self.streams[eng].append(lambda e, fn=fn, h=h: fn(e).then_inc(h, 1))
            return self.semc[sname]
        if inc is None:
            self.streams[eng].append(fn)
            return None
        self.semc[inc] += n
        h = self.semh[inc]
        self.streams[eng].append(lambda e, fn=fn, h=h, n=n: fn(e).then_inc(h, n))
        return self.semc[inc]

    def dma(self, eng, out_ap, in_ap, inc):
        self.wait(eng, inc)
        return self.do(eng, lambda e, o=out_ap, i=in_ap: e.dma_start(out=o, in_=i),
                       inc=inc, n=16, is_dma=True)

    def set_arena(self, handle):
        self._arena = handle
        self._off = 0

    def amark(self):
        return self._off

    def arelease(self, mark):
        self._off = mark

    def alloc_b(self, cols):
        cols = (cols + 15) // 16 * 16
        off = self._off
        assert off + cols <= ARENA_COLS[0], f"arena overflow {off}+{cols}"
        self._off = off + cols
        return self._arena.ap()[:, off:off + cols]

    def alloc_f(self, cols):
        return self.alloc_b(2 * cols).bitcast(F32)

    def at_b(self, off, cols):
        assert off + cols <= ARENA_COLS[0], f"arena overflow {off}+{cols}"
        return self._arena.ap()[:, off:off + cols]

    def at_f(self, off, cols):
        return self.at_b(off, 2 * cols).bitcast(F32)

    def fence(self, engines=ENGINES, sems=("pe", "dve", "act", "pool", "ld", "gld", "st", "qa", "qb")):
        for e in engines:
            for s in sems:
                self.wait(e, s)

    def tap(self, ap):
        if "dbg" not in self.C:
            return
        for s in ("pe", "dve", "act", "pool"):
            self.wait("gpsimd", s)
        P, Cc = ap.shape[-2], ap.shape[-1]
        off = self.C["dbg_off"]
        assert off + Cc <= 16384
        self.dma("gpsimd", self.C["dbg"][0:P, off:off + Cc], ap, inc="gld")
        self.C["dbg_off"] = off + Cc



import traceback as _tb


def mm(ctx, out, lhsT, rhs, start, stop, inc=None):
    site = "".join(_tb.format_stack(limit=3)[:-1]) if MM_DEBUG else None

    def run(e, o=out, l=lhsT, r=rhs, s=start, p=stop, site=site):
        try:
            return e.matmul(o, l, r, start=s, stop=p)
        except Exception:
            if site:
                print("MM FAIL at:\n", site)
                print("out", o.shape, "lhsT", l.shape, "rhs", r.shape)
            raise
    return ctx.do("tensor", run, inc=inc)


MM_DEBUG = False


def tt(ctx, eng, op, out, a, b, inc=None):
    return ctx.do(eng, lambda e, o=out, x=a, y=b, q=op: e.tensor_tensor(o, x, y, q),
                  inc=inc)


def copy(ctx, eng, out, a, inc=None):
    return ctx.do(eng, lambda e, o=out, x=a: e.tensor_copy(o, x), inc=inc)


def act(ctx, out, a, func, bias=0.0, scale=1.0, accum=None, inc=None):
    if accum is None:
        return ctx.do("scalar", lambda e, o=out, x=a, f=func, b=bias, s=scale:
                      e.activation(o, x, f, bias=b, scale=s), inc=inc)
    return ctx.do("scalar", lambda e, o=out, x=a, f=func, b=bias, s=scale, ac=accum:
                  e.activation(o, x, f, bias=b, scale=s, accum_out=ac), inc=inc)



def emit_ln_plain(ctx, ps, xres):
    m0 = ctx.amark()
    sqf = ctx.alloc_b(T * NCD)
    rows = ctx.alloc_f(3 * T)
    t1 = ctx.alloc_f(TC)
    C = ctx.C
    rS = rows[0:1, 0:T]
    rQ = rows[0:1, T:2 * T]
    rT = rows[0:1, 2 * T:3 * T]
    for cc in range(NCD):
        act(ctx, sqf[:, cc * T:(cc + 1) * T], xres[:, cc * T:(cc + 1) * T],
            AF.Square, inc="act" if cc == NCD - 1 else None)
    ctx.wait("tensor", "act")
    evdone = {}
    for nt in range(NT):
        t0 = nt * TC
        pA = ps[nt % 2][0:1, 0:TC]
        pB = ps[2 + nt % 2][0:1, 0:TC]
        if nt >= 2:
            ctx.wait("tensor", "dve", evdone[nt % 2])
        for cc in range(NCD):
            mm(ctx, pA, C["onesf"], xres[:, cc * T + t0: cc * T + t0 + TC],
               cc == 0, cc == NCD - 1)
            mm(ctx, pB, C["onesb"], sqf[:, cc * T + t0: cc * T + t0 + TC],
               cc == 0, cc == NCD - 1, inc="pe" if cc == NCD - 1 else None)
        ctx.wait("vector", "pe")
        copy(ctx, "vector", rS[:, t0:t0 + TC], pA)
        evdone[nt % 2] = copy(ctx, "vector", rQ[:, t0:t0 + TC], pB, inc="dve")
    ctx.do("vector", lambda e, o=rS, i=rS: e.tensor_scalar_mul(o, i, 1.0 / D))
    ctx.do("vector", lambda e, o=rQ, i=rQ: e.tensor_scalar_mul(o, i, 1.0 / D))
    tt(ctx, "vector", ALU.mult, rT, rS, rS)
    tt(ctx, "vector", ALU.subtract, rQ, rQ, rT, inc="dve")
    ctx.wait("scalar", "dve")
    act(ctx, rQ, rQ, AF.Ln, bias=C["eps1"])
    act(ctx, rQ, rQ, AF.Exp, scale=-0.5, inc="act")
    ctx.wait("tensor", "act")
    ctx.wait("tensor", "dve")
    apdone = {}

    def bcast(nt):
        t0 = nt * TC
        if nt >= 2:
            ctx.wait("tensor", "dve", apdone[nt % 2])
        mm(ctx, ps[4 + nt % 2], C["ones1f"], rS[:, t0:t0 + TC], True, True)
        mm(ctx, ps[6 + nt % 2], C["ones1f"], rQ[:, t0:t0 + TC], True, True,
           inc="pe")
        return ctx.val("pe")

    def apply(nt, vb):
        t0 = nt * TC
        ctx.wait("vector", "pe", vb)
        for cc in range(NCD):
            xc = xres[:, cc * T + t0: cc * T + t0 + TC]
            tt(ctx, "vector", ALU.subtract, t1, xc, ps[4 + nt % 2])
            tt(ctx, "vector", ALU.mult, xc, t1, ps[6 + nt % 2],
               inc="dve" if cc == NCD - 1 else None)
        apdone[nt % 2] = ctx.val("dve")

    vbs = {}
    vbs[0] = bcast(0)
    vbs[1] = bcast(1)
    apply(0, vbs[0])
    vbs[2] = bcast(2)
    apply(1, vbs[1])
    vbs[3] = bcast(3)
    apply(2, vbs[2])
    apply(3, vbs[3])
    ctx.wait("tensor", "dve")
    ctx.wait("scalar", "dve")
    ctx.arelease(m0)



def build_program(phases, extra_inputs, dbg=False):
    nc = bass.Bass("TRN2", target_bir_lowering=False)
    io = {}
    io["x_in"] = nc.dram_tensor("x_in", [D, T], F32, kind="ExternalInput").ap()
    io["out"] = nc.dram_tensor("out", [D, T], F32, kind="ExternalOutput").ap()
    if dbg:
        io["dbg"] = nc.dram_tensor("dbg", [128, 16384], F32, kind="ExternalOutput").ap()
    for name, (shape, dtype) in extra_inputs.items():
        io[name] = nc.dram_tensor(name, list(shape), dtype, kind="ExternalInput").ap()
    io["spill_x"] = nc.dram_tensor("spill_x", [D, T], F32, kind="Internal").ap()
    io["spill_x2"] = nc.dram_tensor("spill_x2", [D, T], F32, kind="Internal").ap()
    io["spill_b"] = nc.dram_tensor("spill_b", [D, T], BF16, kind="Internal").ap()

    with contextlib.ExitStack() as st:
        ctx = Ctx(nc)
        arena_h = None
        while arena_h is None:
            try:
                arena_h = nc.alloc_sbuf_tensor("arena", [128, ARENA_COLS[0]], BF16)
            except Exception:
                ARENA_COLS[0] -= 2048
                if ARENA_COLS[0] < 90112:
                    raise
        kon_h = nc.alloc_sbuf_tensor("konst", [128, 16], F32)
        onesf_h = nc.alloc_sbuf_tensor("onesf", [128, 2], F32)
        onesb_h = nc.alloc_sbuf_tensor("onesb", [128, 2], BF16)
        ones1f_h = nc.alloc_sbuf_tensor("ones1f", [1, 128], F32)
        ones1b_h = nc.alloc_sbuf_tensor("ones1b", [1, 128], BF16)
        ident_h = nc.alloc_sbuf_tensor("identb", [128, 128], BF16)
        ctx.set_arena(arena_h)
        xres = ctx.at_f(XRES_OFF, NCD * T)
        ctx._off = FREE0
        C = ctx.C
        C["xres"] = xres
        C["onesf"] = onesf_h.ap()[:, 0:1]
        C["onesb"] = onesb_h.ap()[:, 0:1]
        C["ones1f"] = ones1f_h.ap()
        C["ones1b"] = ones1b_h.ap()
        C["ident"] = ident_h.ap()
        C["eps1"] = kon_h.ap()[0:1, 0:1]
        C["eps64"] = kon_h.ap()[64:65, 0:1]
        C["epsc"] = kon_h.ap()[:, 0:1]
        if dbg:
            C["dbg"] = io["dbg"]
            C["dbg_off"] = 0

        for s in ("ld", "gld", "st", "pe", "dve", "act", "pool", "qa", "qb"):
            ctx.add_sem(s, st.enter_context(nc.semaphore(s)))

        ps = [st.enter_context(nc.psum_tensor(f"ps{i}", [128, TC], F32)).ap()
              for i in range(8)]
        C["psb"] = ps[6].bitcast(BF16)

        ctx.do("vector", lambda e: e.memset(onesf_h.ap(), 1.0))
        ctx.do("vector", lambda e: e.memset(onesb_h.ap(), 1.0))
        ctx.do("vector", lambda e: e.memset(ones1f_h.ap(), 1.0))
        ctx.do("vector", lambda e: e.memset(ones1b_h.ap(), 1.0))
        ctx.do("vector", lambda e: e.memset(kon_h.ap()[:, 0:1], EPS), inc="dve")
        ctx.dma("gpsimd", C["ident"], io["ident"], inc="gld")
        ctx.dma("sync", xres.rearrange("p (c t) -> p c t", c=NCD),
                io["x_in"].rearrange("(c p) t -> p c t", p=128), inc="ld")
        for f in phases:
            f(ctx, ps, io)

        for s in ("dve", "pe", "act", "pool"):
            ctx.wait("gpsimd", s)
        vst = ctx.dma("gpsimd", io["out"].rearrange("(c p) t -> p c t", p=128),
                      xres.rearrange("p (c t) -> p c t", c=NCD), inc="st")
        ctx.wait("gpsimd", "st", vst)

        with nc.Block() as block:
            for eng_name in ENGINES:
                stream = ctx.streams[eng_name]

                def make(fns):
                    def body(h):
                        for fn in fns:
                            fn(h)
                    return body
                getattr(block, eng_name)(make(stream))
    return nc



def phase_ln_only(ctx, ps, io):
    ctx.fence()
    emit_ln_plain(ctx, ps, ctx.C["xres"])



NF = 2176
NKC = 17

L0A_A1R = 32768
L0A_A1I = 34944
L0A_OUTW = 37120
L0A_S = 38144
L0A_V = 39168
L0A_UVW = 55552
L0A_XBF = 63744
L0A_UT = 80128
L0A_ZR = 0
L0A_STREAM = 17408
L0A_ZI = 55552
L0A_GRT = 17408
L0A_GIT = 26112
L0A_G = 39168
L0A_OW = 55552


def phase_l0_attn(ctx, ps, io):
    C = ctx.C
    xres = C["xres"]
    psb = C["psb"]

    S_sb = ctx.at_b(L0A_S, 1024)
    a1r_b = ctx.at_b(L0A_A1R, NF)[0:32, :]
    a1i_b = ctx.at_b(L0A_A1I, NF)[0:32, :]
    outw_b = ctx.at_b(L0A_OUTW, 1024)[0:32, :]

    ctx._off = L0A_V
    pos_sb = ctx.alloc_f(32)
    grep_sb = ctx.alloc_f(3 * 1024)
    brep_sb = ctx.alloc_f(3 * 1024)
    go_sb = ctx.alloc_f(1024)
    bo_sb = ctx.alloc_f(1024)
    wrep_sb = ctx.alloc_f(1024)
    fc_b = ctx.alloc_b(96)[0:32, :]
    h_f = ctx.alloc_f(1024)
    hs_f = ctx.alloc_f(1024)
    st_f = ctx.alloc_f(256)
    sil_b = ctx.alloc_b(1024)
    sT_b = ctx.alloc_b(4096)

    ctx.dma("sync", pos_sb, io["rpe_pos_tiles"], inc="ld")
    ctx.dma("sync", wrep_sb, io["rpe_wrep32"], inc="ld")
    ctx.dma("sync", go_sb, io["rpe_gorep32"], inc="ld")
    ctx.dma("sync", bo_sb, io["rpe_borep32"], inc="ld")
    ctx.dma("sync", grep_sb, io["rpe_grep32"], inc="ld")
    ctx.dma("sync", brep_sb, io["rpe_brep32"], inc="ld")
    ctx.dma("gpsimd", fc_b.rearrange("p (i c) -> p i c", i=3),
            io["rpe_fc"].rearrange("i k c -> k i c"), inc="gld")
    ctx.dma("gpsimd", outw_b, io["rpe_outw"], inc="gld")
    for e in ("tensor", "vector", "scalar"):
        ctx.wait(e, "ld")
        ctx.wait(e, "gld")
        ctx.wait(e, "dve")

    def rpe_ln_silu_batch(grep, brep, out_bf):
        s0 = st_f[:, 0:32]
        s1 = st_f[:, 32:64]
        mu = st_f[:, 64:96]
        va = st_f[:, 96:128]
        tmp = st_f[:, 128:160]
        ctx.do("vector", lambda e, o=s0,
               i=h_f.rearrange("p (t c) -> p t c", c=32):
               e.reduce_sum(o, i, axis=AX.X))
        ctx.wait("scalar", "dve")
        vq = act(ctx, hs_f, h_f, AF.Square, inc="act")
        ctx.wait("vector", "act", vq)
        ctx.do("vector", lambda e, o=s1,
               i=hs_f.rearrange("p (t c) -> p t c", c=32):
               e.reduce_sum(o, i, axis=AX.X))
        ctx.do("vector", lambda e, o=mu, i=s0: e.tensor_scalar_mul(o, i, 1.0 / 32))
        ctx.do("vector", lambda e, o=va, i=s1: e.tensor_scalar_mul(o, i, 1.0 / 32))
        tt(ctx, "vector", ALU.mult, tmp, mu, mu)
        tt(ctx, "vector", ALU.subtract, va, va, tmp, inc="dve")
        ctx.wait("scalar", "dve")
        act(ctx, va, va, AF.Ln, bias=C["epsc"])
        va2 = act(ctx, va, va, AF.Exp, scale=-0.5, inc="act")
        ctx.wait("vector", "act", va2)
        for t in range(32):
            ctx.do("vector", lambda e, o=hs_f[:, t * 32:(t + 1) * 32],
                   i=h_f[:, t * 32:(t + 1) * 32], m=mu[:, t:t + 1],
                   r=va[:, t:t + 1]:
                   e.tensor_scalar(o, i, m, r, ALU.subtract, ALU.mult))
        tt(ctx, "vector", ALU.mult, hs_f, hs_f, grep)
        tt(ctx, "vector", ALU.add, hs_f, hs_f, brep, inc="dve")
        ctx.wait("scalar", "dve")
        return act(ctx, out_bf, hs_f, AF.Silu, inc="act")

    for t in range(32):
        ctx.do("vector", lambda e, o=h_f[:, t * 32:(t + 1) * 32],
               i=wrep_sb[:, t * 32:(t + 1) * 32], s=pos_sb[:, t:t + 1]:
               e.tensor_scalar_mul(o, i, s),
               inc="dve" if t == 31 else None)
    for i in range(3):
        vs = rpe_ln_silu_batch(grep_sb[:, i * 1024:(i + 1) * 1024],
                               brep_sb[:, i * 1024:(i + 1) * 1024], sil_b)
        ctx.wait("tensor", "act", vs)
        for grp in range(2):
            if grp == 1:
                ctx.wait("tensor", "dve")
            for b2 in range(2):
                psbk = C["psb"] if b2 == 0 else ps[7].bitcast(BF16)
                for tt_i in range(8):
                    t = grp * 16 + b2 * 8 + tt_i
                    ctx.do("tensor",
                           lambda e, o=psbk[0:32, tt_i * 128:(tt_i + 1) * 128],
                           i=sil_b[:, t * 32:(t + 1) * 32], d=C["ident"]:
                           e.transpose(o, i, d),
                           inc="pe" if (b2 == 1 and tt_i == 7) else None)
            ctx.wait("vector", "pe")
            copy(ctx, "vector", sT_b[0:32, grp * 2048:grp * 2048 + 1024],
                 C["psb"][0:32, 0:1024])
            copy(ctx, "vector", sT_b[0:32, grp * 2048 + 1024:grp * 2048 + 2048],
                 ps[7].bitcast(BF16)[0:32, 0:1024], inc="dve")
        ctx.wait("tensor", "dve")
        for t in range(32):
            bk = t // 16
            mm(ctx, ps[4 + bk][:, (t % 16) * 32:(t % 16) * 32 + 32],
               sT_b[0:32, t * 128:(t + 1) * 128],
               fc_b[:, i * 32:(i + 1) * 32], True, True,
               inc="pe" if t == 31 else None)
        ctx.wait("vector", "pe")
        copy(ctx, "vector", h_f[:, 0:512], ps[4])
        vv2 = copy(ctx, "vector", h_f[:, 512:1024], ps[5], inc="dve")
        ctx.wait("scalar", "dve", vv2)
        ctx.wait("tensor", "dve", vv2)
    rpe_ln_silu_batch(go_sb, bo_sb, S_sb)
    ctx.wait("tensor", "act")

    ctx.fence()
    ctx._off = L0A_V
    fg_sb = [ctx.alloc_b(NF), ctx.alloc_b(NF)]
    a1_f = ctx.alloc_f(NF)[0:32, :]
    NC5 = [(0, 512), (512, 1024), (1024, 1536), (1536, 2048), (2048, NF)]
    pe_done = {}
    for ri, (srcname, dstb) in enumerate((("fgr", a1r_b), ("fgi", a1i_b))):
        src = io[srcname]
        for p in range(32):
            buf = fg_sb[p % 2]
            qsem = "qa" if p % 2 == 0 else "qb"
            key = (p % 2)
            if key in pe_done:
                ctx.wait("gpsimd", "pe", pe_done[key])
            vq = ctx.dma("gpsimd", buf, src[p * 128:(p + 1) * 128, :], inc=qsem)
            ctx.wait("tensor", qsem, vq)
            for ci, (c0, c1) in enumerate(NC5):
                mm(ctx, ps[ci][0:32, 0:c1 - c0], S_sb[:, p * 32:(p + 1) * 32],
                   buf[:, c0:c1], p == 0, p == 31,
                   inc="pe" if ci == len(NC5) - 1 else None)
            pe_done[p % 2] = ctx.val("pe")
        ctx.wait("vector", "pe")
        for ci, (c0, c1) in enumerate(NC5):
            copy(ctx, "vector", a1_f[:, c0:c1], ps[ci][0:32, 0:c1 - c0])
        ctx.do("vector", lambda e, o=dstb, i=a1_f: e.tensor_copy(o, i), inc="dve")
        ctx.wait("tensor", "dve")

    ctx.tap(a1r_b)
    ctx.tap(a1i_b)
    ctx.fence()
    v_sb = ctx.at_b(L0A_V, 16 * 1024)
    uvw_sb = ctx.at_b(L0A_UVW, NCD * 1024)
    xbf = ctx.at_b(L0A_XBF, NCD * T)
    uT_sb = ctx.at_b(L0A_UT, NCD * T)
    for k in range(NCD):
        copy(ctx, "vector", xbf[:, k * T:(k + 1) * T], xres[:, k * T:(k + 1) * T],
             inc="dve" if k == NCD - 1 else None)
    vxbf = ctx.val("dve")
    ctx.wait("tensor", "dve", vxbf)
    ctx.wait("sync", "dve", vxbf)
    vspill = ctx.dma("sync", io["spill_x"].rearrange("(c p) t -> p c t", p=128),
                     xres.rearrange("p (c t) -> p c t", c=NCD), inc="ld")
    ctx.dma("gpsimd", uvw_sb.rearrange("p (c t) -> p c t", c=NCD),
            io["uv_w"].rearrange("(c p) t -> p c t", p=128)[:, :, 1024:2048],
            inc="gld")
    ctx.wait("tensor", "gld")
    bankdone = {}
    for tt_i in range(16):
        for nc2 in range(2):
            bk = (tt_i % 2) * 2 + nc2
            if bk in bankdone:
                ctx.wait("tensor", "act", bankdone[bk])
            for k in range(NCD):
                mm(ctx, ps[bk], xbf[:, k * T + tt_i * 128: k * T + tt_i * 128 + 128],
                   uvw_sb[:, k * 1024 + nc2 * 512: k * 1024 + (nc2 + 1) * 512],
                   k == 0, k == NCD - 1, inc="pe" if k == NCD - 1 else None)
            ctx.wait("scalar", "pe")
            bankdone[bk] = act(
                ctx, v_sb[:, tt_i * 1024 + nc2 * 512: tt_i * 1024 + (nc2 + 1) * 512],
                ps[bk], AF.Silu, inc="act")
    ctx.wait("gpsimd", "pe")
    ctx.dma("gpsimd", uvw_sb.rearrange("p (c t) -> p c t", c=NCD),
            io["uv_w"].rearrange("(c p) t -> p c t", p=128)[:, :, 0:1024],
            inc="gld")
    ctx.wait("tensor", "gld")
    ctx.wait("tensor", "act")
    bankdone = {}
    for m in range(NCD):
        for nt in range(NT):
            bk = (m * NT + nt) % 4
            if bk in bankdone:
                ctx.wait("tensor", "act", bankdone[bk])
            for k in range(NCD):
                mm(ctx, ps[bk], uvw_sb[:, k * 1024 + m * 128: k * 1024 + (m + 1) * 128],
                   xbf[:, k * T + nt * TC: k * T + (nt + 1) * TC],
                   k == 0, k == NCD - 1, inc="pe" if k == NCD - 1 else None)
            ctx.wait("scalar", "pe")
            bankdone[bk] = act(
                ctx, uT_sb[:, m * T + nt * TC: m * T + (nt + 1) * TC],
                ps[bk], AF.Silu, inc="act")
    ctx.wait("vector", "act")

    ctx.tap(v_sb[:, 0:1024])
    ctx.tap(uT_sb[:, 0:512])
    ctx.fence()
    zr_sb = ctx.at_b(L0A_ZR, NKC * 1024)
    zi_sb = ctx.at_b(L0A_ZI, NKC * 1024)
    ctx._off = L0A_STREAM
    fr_sb = [ctx.alloc_b(2048), ctx.alloc_b(2048)]
    fi_sb = [ctx.alloc_b(2048), ctx.alloc_b(2048)]
    vh_b = ctx.alloc_b(2048)
    zf_f = ctx.alloc_f(1024)
    assert ctx._off <= 32768
    ctx.wait("vector", "ld", vspill)
    frdone = {}
    uinfo = []

    def emit_z(u):
        g = u["g"]
        kc, nc2 = u["kc"], u["nc2"]
        vh = vh_b[:, nc2 * 1024:(nc2 + 1) * 1024]
        ctx.wait("vector", "pe", u["vmm"])
        copy(ctx, "vector", vh[:, 0:512], ps[g + 0])
        copy(ctx, "vector", vh[:, 512:1024], ps[g + 1])
        t1 = zf_f[:, 0:512]
        t2 = zf_f[:, 512:1024]
        zrd = zr_sb[:, kc * 1024 + nc2 * 512: kc * 1024 + (nc2 + 1) * 512]
        zid = zi_sb[:, kc * 1024 + nc2 * 512: kc * 1024 + (nc2 + 1) * 512]
        tt(ctx, "vector", ALU.mult, t1, vh[:, 0:512], ps[g + 2])
        tt(ctx, "vector", ALU.mult, t2, vh[:, 512:1024], ps[g + 3])
        tt(ctx, "vector", ALU.subtract, zrd, t1, t2)
        tt(ctx, "vector", ALU.mult, t1, vh[:, 512:1024], ps[g + 2])
        tt(ctx, "vector", ALU.mult, t2, vh[:, 0:512], ps[g + 3])
        tt(ctx, "vector", ALU.add, zid, t1, t2, inc="dve")
        u["vz"] = ctx.val("dve")

    for kc in range(NKC):
        pp = kc % 2
        qsem = "qa" if pp == 0 else "qb"
        if pp in frdone:
            ctx.wait("gpsimd", "pe", frdone[pp])
        ctx.dma("gpsimd", fr_sb[pp].rearrange("p (t f) -> p t f", f=128),
                io["frt"][kc].rearrange("(t p) f -> p t f", p=128), inc=qsem)
        vq = ctx.dma("gpsimd", fi_sb[pp].rearrange("p (t f) -> p t f", f=128),
                io["fit"][kc].rearrange("(t p) f -> p t f", p=128), inc=qsem)
        for nc2 in range(2):
            u = {}
            g = 4 * ((2 * kc + nc2) % 2)
            u["g"] = g
            u["kc"] = kc
            u["nc2"] = nc2
            ctx.wait("tensor", qsem, vq)
            if ("zdone", g) in dict(uinfo[-2:-1] and [] or []):
                pass
            if len(uinfo) >= 2 and uinfo[-2].get("vz"):
                ctx.wait("tensor", "dve", uinfo[-2]["vz"])
            for tt_i in range(16):
                mm(ctx, ps[g + 0], fr_sb[pp][:, tt_i * 128:(tt_i + 1) * 128],
                   v_sb[:, tt_i * 1024 + nc2 * 512: tt_i * 1024 + (nc2 + 1) * 512],
                   tt_i == 0, tt_i == 15)
            for tt_i in range(16):
                mm(ctx, ps[g + 1], fi_sb[pp][:, tt_i * 128:(tt_i + 1) * 128],
                   v_sb[:, tt_i * 1024 + nc2 * 512: tt_i * 1024 + (nc2 + 1) * 512],
                   tt_i == 0, tt_i == 15)
            mm(ctx, ps[g + 2], a1r_b[:, kc * 128:(kc + 1) * 128],
               outw_b[:, nc2 * 512:(nc2 + 1) * 512], True, True)
            u["vmm"] = mm(ctx, ps[g + 3], a1i_b[:, kc * 128:(kc + 1) * 128],
                          outw_b[:, nc2 * 512:(nc2 + 1) * 512], True, True,
                          inc="pe")
            uinfo.append(u)
            if len(uinfo) >= 2:
                emit_z(uinfo[-2])
        frdone[pp] = ctx.val("pe")
    emit_z(uinfo[-1])

    ctx.fence()
    g_sb = ctx.at_b(L0A_G, NCD * T)
    grt_sb = ctx.at_b(L0A_GRT, NKC * 512)
    git_sb = ctx.at_b(L0A_GIT, NKC * 512)
    gdone = {}
    for nt in range(NT):
        if nt > 0:
            ctx.wait("gpsimd", "pe")
        ctx.dma("gpsimd", grt_sb.rearrange("p (k f) -> p k f", k=NKC),
                io["grt"][nt].rearrange("k p f -> p k f"), inc="gld")
        ctx.dma("gpsimd", git_sb.rearrange("p (k f) -> p k f", k=NKC),
                io["git"][nt].rearrange("k p f -> p k f"), inc="gld")
        ctx.wait("tensor", "gld")
        for m in range(NCD):
            bk = m % 4
            if bk in gdone:
                ctx.wait("tensor", "dve", gdone[bk])
            pst = ps[bk]
            for kc in range(NKC):
                mm(ctx, pst, zr_sb[:, kc * 1024 + m * 128: kc * 1024 + (m + 1) * 128],
                   grt_sb[:, kc * 512:(kc + 1) * 512], kc == 0, False)
            for kc in range(NKC):
                mm(ctx, pst, zi_sb[:, kc * 1024 + m * 128: kc * 1024 + (m + 1) * 128],
                   git_sb[:, kc * 512:(kc + 1) * 512], False, kc == NKC - 1,
                   inc="pe" if kc == NKC - 1 else None)
            ctx.wait("vector", "pe")
            gdone[bk] = tt(
                ctx, "vector", ALU.mult,
                g_sb[:, m * T + nt * TC: m * T + (nt + 1) * TC],
                uT_sb[:, m * T + nt * TC: m * T + (nt + 1) * TC], pst, inc="dve")

    ctx.tap(g_sb[:, 0:2048])
    ctx.fence()
    ow_sb = ctx.at_b(L0A_OW, NCD * 1024)
    ctx.wait("sync", "dve")
    vx = ctx.dma("sync", xres.rearrange("p (c t) -> p c t", c=NCD),
                 io["spill_x"].rearrange("(c p) t -> p c t", p=128), inc="ld")
    ctx.wait("gpsimd", "pe")
    vw = ctx.dma("gpsimd", ow_sb.rearrange("p (c t) -> p c t", c=NCD),
                 io["o_w"].rearrange("(c p) t -> p c t", p=128), inc="gld")
    ctx.wait("tensor", "gld", vw)
    ctx.wait("tensor", "ld", vx)
    ctx.wait("vector", "ld", vx)
    resdone = {}
    for m in range(NCD):
        for nt in range(NT):
            bk = (m * NT + nt) % 4
            if bk in resdone:
                ctx.wait("tensor", "dve", resdone[bk])
            pst = ps[bk]
            for k in range(NCD):
                mm(ctx, pst, ow_sb[:, k * 1024 + m * 128: k * 1024 + (m + 1) * 128],
                   g_sb[:, k * T + nt * TC: k * T + (nt + 1) * TC],
                   k == 0, k == NCD - 1, inc="pe" if k == NCD - 1 else None)
            ctx.wait("vector", "pe")
            xs = xres[:, m * T + nt * TC: m * T + (nt + 1) * TC]
            resdone[bk] = tt(ctx, "vector", ALU.add, xs, xs, pst, inc="dve")
    ctx.wait("tensor", "dve")
    ctx.wait("scalar", "dve")
    ctx.tap(xres[:, 0:2048])
    ctx._off = L0A_G + NCD * T
    ctx.fence()
    emit_ln_plain(ctx, ps, xres)


def prep_l0attn(w):
    n, m, nf = T, 2 * T, T + 1
    F8 = np.float64
    ext = {}
    ext["uv_w"] = np.ascontiguousarray(w["L0_attn_uv_w"])
    ext["o_w"] = np.ascontiguousarray(w["L0_attn_o_w"])
    p = np.arange(m, dtype=F8)
    rel = np.where(p <= n, p, p - m)
    rel[n] = 0.0
    ext["rpe_pos_tiles"] = np.ascontiguousarray(
        rel.reshape(32, 128).T).astype(np.float32)
    ext["rpe_wrep32"] = np.tile(w["L0_attn_rpe_pos_w"], (128, 32)).astype(np.float32)
    ext["rpe_grep32"] = np.ascontiguousarray(
        np.tile(w["L0_attn_rpe_ln_w"][:, None, :], (1, 32, 1)).reshape(1, 3 * 1024)
        * np.ones((128, 1), np.float32)).astype(np.float32)
    ext["rpe_brep32"] = np.ascontiguousarray(
        np.tile(w["L0_attn_rpe_ln_b"][:, None, :], (1, 32, 1)).reshape(1, 3 * 1024)
        * np.ones((128, 1), np.float32)).astype(np.float32)
    ext["rpe_gorep32"] = np.tile(w["L0_attn_rpe_out_ln_w"][None, :], (128, 32)).astype(np.float32)
    ext["rpe_borep32"] = np.tile(w["L0_attn_rpe_out_ln_b"][None, :], (128, 32)).astype(np.float32)
    ext["rpe_fc"] = np.ascontiguousarray(w["L0_attn_rpe_fc_w"])
    ext["rpe_outw"] = np.ascontiguousarray(w["L0_attn_rpe_out_w"])
    gamma = 0.99
    g = gamma ** np.abs(rel)
    g[n] = 0.0
    k = np.arange(NF, dtype=F8)
    angA = 2.0 * np.pi * (k[None, :] * p[:, None]) / m
    fgr = np.cos(angA) * g[:, None]
    fgi = -np.sin(angA) * g[:, None]
    fgr[:, nf:] = 0.0
    fgi[:, nf:] = 0.0
    ext["fgr"] = fgr.astype(np.float32)
    ext["fgi"] = fgi.astype(np.float32)
    t = np.arange(n, dtype=F8)
    ang = 2.0 * np.pi * (t[:, None] * k[None, :]) / m
    frt = np.cos(ang)
    fit = -np.sin(ang)
    frt[:, nf:] = 0.0
    fit[:, nf:] = 0.0
    ext["frt"] = np.ascontiguousarray(
        frt.reshape(n, NKC, 128).transpose(1, 0, 2)).astype(np.float32)
    ext["fit"] = np.ascontiguousarray(
        fit.reshape(n, NKC, 128).transpose(1, 0, 2)).astype(np.float32)
    cw = np.full(NF, 2.0, dtype=F8)
    cw[0] = 1.0
    cw[n] = 1.0
    cw[nf:] = 0.0
    grT = (np.cos(ang) * cw[None, :] / m).T
    cwi = cw.copy()
    cwi[0] = 0.0
    cwi[n] = 0.0
    giT = (-np.sin(ang) * cwi[None, :] / m).T
    ext["grt"] = np.ascontiguousarray(
        grT.reshape(NKC, 128, NT, TC).transpose(2, 0, 1, 3)).astype(np.float32)
    ext["git"] = np.ascontiguousarray(
        giT.reshape(NKC, 128, NT, TC).transpose(2, 0, 1, 3)).astype(np.float32)
    ext["ident"] = np.eye(128, dtype=np.float32)
    return ext


L0A_SHAPES = {
    "uv_w": ((D, T), F32), "o_w": ((D, D), F32),
    "rpe_pos_tiles": ((128, 32), F32), "rpe_wrep32": ((128, 1024), F32),
    "rpe_grep32": ((128, 3072), F32), "rpe_brep32": ((128, 3072), F32),
    "rpe_gorep32": ((128, 1024), F32), "rpe_borep32": ((128, 1024), F32),
    "rpe_fc": ((3, 32, 32), F32), "rpe_outw": ((32, D), F32),
    "fgr": ((2 * T, NF), F32), "fgi": ((2 * T, NF), F32),
    "frt": ((NKC, T, 128), F32), "fit": ((NKC, T, 128), F32),
    "grt": ((NT, NKC, 128, TC), F32), "git": ((NT, NKC, 128, TC), F32),
    "ident": ((128, 128), F32),
}



L0F_XBF = 32768
L0F_BDMK = 36864
L0F_BDUP = 40960
L0F_WCT = 57344
L0F_Y = 73728
L0F_Z = 77824
L0F_ACC = 86016


def phase_l0_ffn(ctx, ps, io):
    C = ctx.C
    xres = C["xres"]
    ctx.fence()
    xbf_tc = ctx.at_b(L0F_XBF, 8 * TC)
    bdmk = ctx.at_b(L0F_BDMK, 32 * 128)
    bdup = ctx.at_b(L0F_BDUP, 32 * 512)
    wct_sb = ctx.at_b(L0F_WCT, 16 * 1024)
    ybuf = ctx.at_b(L0F_Y, 8 * TC)
    zbuf = ctx.at_b(L0F_Z, 16 * TC)
    outacc = ctx.at_f(L0F_ACC, 8 * TC)

    ctx.dma("gpsimd", bdmk.rearrange("p (g n) -> p g n", g=32),
            io["bdmk"].rearrange("h c p n -> p (h c) n"), inc="gld")
    ctx.dma("gpsimd", bdup.rearrange("p (g n) -> p g n", g=32),
            io["bdup"].rearrange("h c p n -> p (h c) n"), inc="gld")
    wct_use = None
    for tc in range(NT):
        t0 = tc * TC
        for c in range(NCD):
            copy(ctx, "vector", xbf_tc[:, c * TC:(c + 1) * TC],
                 xres[:, c * T + t0:c * T + t0 + TC])
        vxb = ctx.val("dve")
        ctx.wait("tensor", "dve", vxb)
        ctx.wait("tensor", "gld")
        for h in range(4):
            ydone = {}
            for c in range(NCD):
                bk = c % 4
                if bk in ydone:
                    ctx.wait("tensor", "dve", ydone[bk])
                mm(ctx, ps[bk], bdmk[:, (h * 8 + c) * 128:(h * 8 + c + 1) * 128],
                   xbf_tc[:, c * TC:(c + 1) * TC], True, True, inc="pe")
                ctx.wait("vector", "pe")
                ydone[bk] = copy(ctx, "vector", ybuf[:, c * TC:(c + 1) * TC], ps[bk])
            for half in range(2):
                if wct_use is not None:
                    ctx.wait("gpsimd", "pe", wct_use)
                r0 = h * 4096 + half * 2048
                vw = ctx.dma("gpsimd", wct_sb.rearrange("p (z n) -> p z n", z=16),
                             io["wct"][r0:r0 + 2048, :].rearrange(
                                 "(z p) n -> p z n", p=128), inc="qa")
                zdone = {}
                for zc in range(16):
                    zg = half * 16 + zc
                    kc, sub = zg // 4, zg % 4
                    bk = zc % 4
                    if bk in zdone:
                        ctx.wait("tensor", "act", zdone[bk])
                    ctx.wait("tensor", "dve")
                    mm(ctx, ps[bk],
                       bdup[:, (h * 8 + kc) * 512 + sub * 128:(h * 8 + kc) * 512 + (sub + 1) * 128],
                       ybuf[:, kc * TC:(kc + 1) * TC], True, True, inc="pe")
                    ctx.wait("scalar", "pe")
                    zdone[bk] = act(ctx, zbuf[:, zc * TC:(zc + 1) * TC], ps[bk],
                                    AF.Gelu, inc="act")
                ctx.wait("tensor", "act")
                ctx.wait("tensor", "qa", vw)
                cdone = {}
                for m in range(NCD):
                    bk = 4 + m % 4
                    if bk in cdone:
                        ctx.wait("tensor", "dve", cdone[bk])
                    for zc in range(16):
                        mm(ctx, ps[bk],
                           wct_sb[:, zc * 1024 + m * 128:zc * 1024 + (m + 1) * 128],
                           zbuf[:, zc * TC:(zc + 1) * TC], zc == 0, zc == 15,
                           inc="pe" if zc == 15 else None)
                    ctx.wait("vector", "pe")
                    dst = outacc[:, m * TC:(m + 1) * TC]
                    if h == 0 and half == 0:
                        cdone[bk] = copy(ctx, "vector", dst, ps[bk])
                    else:
                        cdone[bk] = tt(ctx, "vector", ALU.add, dst, dst, ps[bk])
                wct_use = ctx.val("pe")
        ctx.wait("vector", "pe")
        for c in range(NCD):
            xs = xres[:, c * T + t0:c * T + t0 + TC]
            tt(ctx, "vector", ALU.add, xs, xs, outacc[:, c * TC:(c + 1) * TC])
        ctx.wait("tensor", "dve")
    ctx.fence()
    emit_ln_plain(ctx, ps, xres)


def prep_l0ffn(w):
    RM = w["L0_ffn_rail_maker"]
    RU = w["L0_ffn_rail_up_proj"]
    bdmk = np.zeros((4, 8, 128, 128), dtype=np.float32)
    bdup = np.zeros((4, 8, 128, 512), dtype=np.float32)
    for h in range(4):
        for c in range(8):
            for rho in range(8):
                r = 8 * c + rho
                bdmk[h, c, 16 * rho:16 * rho + 16, 16 * rho:16 * rho + 16] = RM[h, r]
                bdup[h, c, 16 * rho:16 * rho + 16, 64 * rho:64 * rho + 64] = \
                    RU[h, r].T
    return {
        "bdmk": bdmk, "bdup": bdup,
        "wct": np.ascontiguousarray(w["L0_ffn_w_cproj"].T),
    }


L0F_SHAPES = {
    "bdmk": ((4, 8, 128, 128), F32), "bdup": ((4, 8, 128, 512), F32),
    "wct": ((16384, D), F32),
}



def attn_pipe(ctx, ps, C, units, P4, PT2, cols, psb2, pvdone):
    def bankwait(bk):
        if bk in pvdone:
            v = pvdone[bk]
            s, v = v if isinstance(v, tuple) else ("dve", v)
            ctx.wait("tensor", s, v)

    def stA(k, u):
        bankwait(k % 4)
        u["_sv"] = u["smm"](ps[k % 4])

    def stB(k, u):
        cl = cols[:, 4 * (k % 4):4 * (k % 4) + 4]
        ctx.wait("vector", "pe", u["_sv"])
        u["_vb"] = ctx.do("vector", lambda e, o=cl[:, 0:1],
                          i=ps[k % 4][:, 0:u["nk"]]:
                          e.reduce_max(o, i, axis=AX.X, negate=True), inc="dve")

    def stC(k, u):
        cl = cols[:, 4 * (k % 4):4 * (k % 4) + 4]
        P_b = P4[k % 4]
        if k - 4 >= 0 and "_vt" in units[k - 4]:
            ctx.wait("scalar", "pe", units[k - 4]["_vt"])
        ctx.wait("scalar", "dve", u["_vb"])
        u["_ve"] = act(ctx, P_b[:, 0:u["nk"]], ps[k % 4][:, 0:u["nk"]], AF.Exp,
                       bias=cl[:, 0:1], inc="act")
        pvdone[k % 4] = ("act", u["_ve"])

    def stD(k, u):
        cl = cols[:, 4 * (k % 4):4 * (k % 4) + 4]
        P_b = P4[k % 4]
        nk = u["nk"]
        ctx.wait("vector", "act", u["_ve"])
        vm = ctx.do("vector", lambda e, o=P_b[:, 0:nk], i=P_b[:, 0:nk],
                    m=u["mask_ap"], a=cl[:, 1:2]:
                    e.scalar_tensor_tensor(o, i, 1.0, m, ALU.mult, ALU.mult,
                                           accum_out=a))
        u["_vr"] = ctx.do("vector", lambda e, o=cl[:, 2:3], i=cl[:, 1:2]:
                          e.reciprocal(o, i), inc="dve")
        u["_vm"] = u["_vr"]

    def stN(k, u):
        cl = cols[:, 4 * (k % 4):4 * (k % 4) + 4]
        P_b = P4[k % 4]
        ctx.wait("scalar", "dve", u["_vr"])
        u["_vn"] = ctx.do("scalar", lambda e, o=P_b[:, 0:u["nk"]],
                          i=P_b[:, 0:u["nk"]], s=cl[:, 2:3]:
                          e.activation(o, i, AF.Identity, scale=s), inc="act")

    def stE(k, u):
        P_b = P4[k % 4]
        psb = psb2[k % 2]
        nk = u["nk"]
        ctx.wait("tensor", "act", u["_vn"])
        pk = 6 + (k % 2)
        bankwait(pk)
        if nk < 128:
            ctx.do("tensor", lambda e, o=psb[0:nk, 0:128], i=P_b[:, 0:nk],
                   d=C["ident"]: e.transpose(o, i, d), inc="pe")
        else:
            for kk in range(nk // 128):
                ctx.do("tensor", lambda e, o=psb[:, kk * 128:(kk + 1) * 128],
                       i=P_b[:, kk * 128:(kk + 1) * 128], d=C["ident"]:
                       e.transpose(o, i, d),
                       inc="pe" if kk == nk // 128 - 1 else None)
        u["_vt"] = ctx.val("pe")

    def stF(k, u):
        PT_b = PT2[k % 2]
        psb = psb2[k % 2]
        nk = u["nk"]
        eng = "vector" if k % 2 == 0 else "scalar"
        sem = "dve" if k % 2 == 0 else "act"
        if k - 2 >= 0 and "_vg" in units[k - 2]:
            ctx.wait(eng, "pe", units[k - 2]["_vg"])
        ctx.wait(eng, "pe", u["_vt"])
        def cp(e, o, i):
            return e.tensor_copy(o, i) if hasattr(e, "tensor_copy") else e.copy(o, i)
        if nk < 128:
            u["_vf"] = ctx.do(eng, lambda e, o=PT_b[0:nk, 0:128],
                              i=psb[0:nk, 0:128]: cp(e, o, i), inc=sem)
        else:
            u["_vf"] = ctx.do(eng, lambda e, o=PT_b[:, 0:nk], i=psb[:, 0:nk]:
                              cp(e, o, i), inc=sem)
        u["_vfs"] = sem
        pvdone[6 + (k % 2)] = (sem, u["_vf"])

    def stG(k, u):
        PT_b = PT2[k % 2]
        nk = u["nk"]
        ob = 4 + (k % 2)
        bankwait(ob)
        ctx.wait("tensor", u["_vfs"], u["_vf"])
        op = u["out_p"]
        if nk < 128:
            mm(ctx, ps[ob][0:op, 0:128], u["vsl"][0], PT_b[0:nk, 0:128],
               True, True, inc="pe")
        else:
            for kk in range(nk // 128):
                mm(ctx, ps[ob][0:op, 0:128], u["vsl"][kk],
                   PT_b[:, kk * 128:(kk + 1) * 128], kk == 0,
                   kk == nk // 128 - 1, inc="pe" if kk == nk // 128 - 1 else None)
        u["_vg"] = ctx.val("pe")

    def stH(k, u):
        ob = 4 + (k % 2)
        ctx.wait("scalar", "pe", u["_vg"])
        vH = ctx.do("scalar", lambda e, o=u["out_ap"],
                    i=ps[ob][0:u["out_p"], 0:128]: e.copy(o, i), inc="act")
        pvdone[ob] = ("act", vH)

    stages = [stA, stB, stC, stD, stN, stE, stF, stG, stH]
    n = len(units)
    for k in range(n + len(stages) - 1):
        for j in reversed(range(len(stages))):
            i = k - j
            if 0 <= i < n:
                stages[j](i, units[i])



L1A_LOC = 32768
L1A_WQKV = 49152
L1A_QBLK = 73728
L1A_KRING = 75776
L1A_VRING = 79872
L1A_XB = 83968
L1A_MASKS = 86016
L1A_SCR = 88064
L1A_GPK = 91648
L1A_GL = 92160
L1A_GR = 96256
L1A_ESEL = 100352
L1A_GQT = 49152
L1A_WG = 57344
L1A_REMT = 65536
L1A_PX = 73728
L1A_XBNT = 78848


def phase_l1_attn(ctx, ps, io):
    C = ctx.C
    xres = C["xres"]
    psb = C["psb"]
    psb2 = [psb, ps[7].bitcast(BF16)]
    ctx.fence()
    localT = ctx.at_b(L1A_LOC, 16384)
    wq_sb = ctx.at_b(L1A_WQKV, 8192)
    wk_sb = ctx.at_b(L1A_WQKV + 8192, 8192)
    wv_sb = ctx.at_b(L1A_WQKV + 16384, 8192)
    qblk = ctx.at_b(L1A_QBLK, 2048)
    kring = [ctx.at_b(L1A_KRING, 2048), ctx.at_b(L1A_KRING + 2048, 2048)]
    vring = [ctx.at_b(L1A_VRING, 2048), ctx.at_b(L1A_VRING + 2048, 2048)]
    xb_blk = ctx.at_b(L1A_XB, 2048)
    masksL = ctx.at_b(L1A_MASKS, 2048)
    scf = ctx.at_f(L1A_SCR, 512)
    scf2 = ctx.at_f(L1A_SCR + 1024, 512)
    P_b = ctx.at_b(L1A_SCR + 2048, 512)
    PT_b = ctx.at_b(L1A_SCR + 2560, 512)
    P_bB = ctx.at_b(L1A_XB, 512)
    PT_bB = ctx.at_b(L1A_XB + 512, 512)
    P4A = [P_b, P_bB, ctx.at_b(L1A_SCR, 512), ctx.at_b(L1A_SCR + 512, 512)]
    cols16 = ctx.at_f(91904, 16)
    remo_c = ctx.at_b(L1A_SCR + 3072, 512)
    cols = ctx.at_f(91904, 8)
    gpl_sb = ctx.at_b(L1A_GPK, 128)
    gpr_sb = ctx.at_b(L1A_GPK + 128, 128)
    glsig = ctx.at_f(L1A_GL, 2048)[0:16, :]
    grsig = ctx.at_f(L1A_GR, 2048)[0:16, :]
    esel = ctx.at_f(L1A_ESEL, 1024)[0:16, :]

    ctx.dma("gpsimd", wq_sb.rearrange("p (c t) -> p c t", c=NCD),
            io["wq"].rearrange("(c p) t -> p c t", p=128), inc="gld")
    ctx.dma("gpsimd", wk_sb.rearrange("p (c t) -> p c t", c=NCD),
            io["wk"].rearrange("(c p) t -> p c t", p=128), inc="gld")
    ctx.dma("gpsimd", wv_sb.rearrange("p (c t) -> p c t", c=NCD),
            io["wv"].rearrange("(c p) t -> p c t", p=128), inc="gld")
    ctx.dma("gpsimd", masksL.rearrange("p (v q f) -> p v q f", v=2, q=2),
            io["lmask"].rearrange("v q p f -> p v q f"), inc="gld")
    ctx.dma("gpsimd", gpl_sb.rearrange("p (c n) -> p c n", c=NCD),
            io["gpackl"].rearrange("(c p) n -> p c n", p=128), inc="gld")
    ctx.dma("gpsimd", gpr_sb.rearrange("p (c n) -> p c n", c=NCD),
            io["gpackr"].rearrange("(c p) n -> p c n", p=128), inc="gld")
    ctx.dma("sync", esel.rearrange("p (c n) -> p c n", c=8),
            io["esel"].rearrange("c h n -> h c n"), inc="ld")
    ctx.wait("tensor", "gld")
    ctx.wait("tensor", "ld")

    pvdone = {}

    def bankwait(bk):
        if bk in pvdone:
            v = pvdone[bk]
            s, v = v if isinstance(v, tuple) else ("dve", v)
            ctx.wait("tensor", s, v)

    for blk in range(8):
        t0 = blk * 256
        ctx.wait("vector", "pe")
        for k in range(NCD):
            copy(ctx, "vector", xb_blk[:, k * 256:(k + 1) * 256],
                 xres[:, k * T + t0:k * T + t0 + 256])
        ctx.wait("tensor", "dve")
        for dst, wsb in ((qblk, wq_sb), (kring[blk % 2], wk_sb)):
            qscaled = dst is qblk
            for m in range(NCD):
                bk = m % 4
                bankwait(bk)
                for k in range(NCD):
                    mm(ctx, ps[bk][:, 0:256],
                       wsb[:, k * 1024 + m * 128:k * 1024 + (m + 1) * 128],
                       xb_blk[:, k * 256:(k + 1) * 256], k == 0, k == NCD - 1,
                       inc="pe" if k == NCD - 1 else None)
                if qscaled:
                    ctx.wait("scalar", "pe")
                    pvdone[bk] = ("act", ctx.do(
                        "scalar", lambda e, o=dst[:, m * 256:(m + 1) * 256],
                        i=ps[bk][:, 0:256]:
                        e.activation(o, i, AF.Identity, scale=0.125), inc="act"))
                else:
                    ctx.wait("vector", "pe")
                    pvdone[bk] = copy(ctx, "vector", dst[:, m * 256:(m + 1) * 256],
                                      ps[bk][:, 0:256])
        for gp, dst, bk in ((gpl_sb, glsig, 4), (gpr_sb, grsig, 5)):
            bankwait(bk)
            for k in range(NCD):
                mm(ctx, ps[bk][0:16, 0:256], gp[:, k * 16:(k + 1) * 16],
                   qblk[:, k * 256:(k + 1) * 256], k == 0, k == NCD - 1,
                   inc="pe" if k == NCD - 1 else None)
            ctx.wait("scalar", "pe")
            va = act(ctx, dst[:, t0:t0 + 256], ps[bk][0:16, 0:256], AF.Sigmoid,
                     scale=8.0)
            pvdone[bk] = None
            ctx.wait("tensor", "act", va)
            pvdone.pop(bk)
        for ti in range(2):
            for nc2 in range(2):
                bk = 2 * ti + nc2
                bankwait(bk)
                for k in range(NCD):
                    mm(ctx, ps[bk],
                       xb_blk[:, k * 256 + ti * 128:k * 256 + ti * 128 + 128],
                       wv_sb[:, k * 1024 + nc2 * 512:k * 1024 + (nc2 + 1) * 512],
                       k == 0, k == NCD - 1, inc="pe" if k == NCD - 1 else None)
            ctx.wait("vector", "pe")
            for nc2 in range(2):
                pvdone[2 * ti + nc2] = ("dve", copy(
                    ctx, "vector",
                    vring[blk % 2][:, ti * 1024 + nc2 * 512:ti * 1024 + (nc2 + 1) * 512],
                    ps[2 * ti + nc2]))
        nk = 256 if blk == 0 else 512
        nkc = nk // 128
        kprev, kcur = kring[(blk - 1) % 2], kring[blk % 2]
        vprev, vcur = vring[(blk - 1) % 2], vring[blk % 2]
        units = []
        for h in range(16):
            ch, rh = h // 2, 64 * (h % 2)
            for qt in range(2):
                lq = qblk[rh:rh + 64, ch * 256 + qt * 128:ch * 256 + qt * 128 + 128]
                kp = kprev[rh:rh + 64, ch * 256:(ch + 1) * 256]
                kc_ = kcur[rh:rh + 64, ch * 256:(ch + 1) * 256]

                def smm(pbank, lq=lq, kp=kp, kc_=kc_, blk=blk):
                    if blk == 0:
                        return mm(ctx, pbank[:, 0:256], lq, kc_, True, True,
                                  inc="pe")
                    mm(ctx, pbank[:, 0:256], lq, kp, True, True)
                    return mm(ctx, pbank[:, 256:512], lq, kc_, True, True,
                              inc="pe")
                mvar = 0 if blk == 0 else 1
                moff = 256 if blk == 0 else 0
                mslice = masksL[:, mvar * 1024 + qt * 512 + moff:
                                mvar * 1024 + qt * 512 + moff + nk]
                vsl = [(vcur if (blk == 0 or kk >= 2) else vprev)[
                    :, (kk % 2) * 1024 + h * 64:(kk % 2) * 1024 + h * 64 + 64]
                    for kk in range(nkc)]
                out_ap = localT[rh:rh + 64,
                                ch * T + t0 + qt * 128:ch * T + t0 + qt * 128 + 128]
                units.append(dict(nk=nk, smm=smm, mask_ap=mslice, scale=0.125,
                                  vsl=vsl, out_ap=out_ap, out_p=64))
        attn_pipe(ctx, ps, C, units, P4A, [PT_b, PT_bB], cols16, psb2, pvdone)
    ctx.fence()

    gqt_sb = ctx.at_b(L1A_GQT, 8192)
    wg_sb = ctx.at_b(L1A_WG, 4096)
    remT = ctx.at_b(L1A_REMT, 8192)
    pxT = ctx.at_f(L1A_PX, 512)
    pxb = ctx.at_b(L1A_PX + 1024, 512)
    gkt_sb = ctx.at_b(L1A_PX + 1536, 256)
    gv_sb = ctx.at_b(L1A_PX + 1792, 512)[0:64, :]
    latmask = ctx.at_b(L1A_PX + 2304, 1024)
    xbnt = ctx.at_b(L1A_XBNT, 4096)

    ctx.dma("gpsimd", latmask.rearrange("p (q n) -> p q n", q=16),
            io["latmask"].rearrange("q p n -> p q n"), inc="gld")
    for k in range(NCD):
        ctx.do("vector", lambda e, o=pxT[:, k * 64:(k + 1) * 64],
               i=xres[:, k * T:(k + 1) * T].rearrange("p (l w) -> p l w", w=32):
               e.reduce_sum(o, i, axis=AX.X))
    ctx.do("vector", lambda e, o=pxb, i=pxT: e.tensor_scalar_mul(o, i, 1.0 / 32),
           inc="dve")
    ctx.wait("tensor", "dve")
    ctx.dma("gpsimd", wg_sb.rearrange("p (c t) -> p c t", c=NCD),
            io["wgk"].rearrange("(c p) t -> p c t", p=128), inc="gld")
    ctx.wait("tensor", "gld")
    for mc in range(4):
        bk = mc % 2
        bankwait(bk)
        for k in range(NCD):
            mm(ctx, ps[bk][:, 0:64], wg_sb[:, k * 512 + mc * 128:k * 512 + (mc + 1) * 128],
               pxb[:, k * 64:(k + 1) * 64], k == 0, k == NCD - 1,
               inc="pe" if k == NCD - 1 else None)
        ctx.wait("vector", "pe")
        pvdone[bk] = copy(ctx, "vector", gkt_sb[:, mc * 64:(mc + 1) * 64],
                          ps[bk][:, 0:64])
    ctx.wait("gpsimd", "pe")
    ctx.dma("gpsimd", wg_sb.rearrange("p (c t) -> p c t", c=NCD),
            io["wgv"].rearrange("(c p) t -> p c t", p=128), inc="gld")
    ctx.wait("tensor", "gld")
    bankwait(2)
    for k in range(NCD):
        mm(ctx, ps[2][0:64, :], pxb[:, k * 64:(k + 1) * 64],
           wg_sb[:, k * 512:(k + 1) * 512], k == 0, k == NCD - 1,
           inc="pe" if k == NCD - 1 else None)
    ctx.wait("vector", "pe")
    pvdone[2] = copy(ctx, "vector", gv_sb, ps[2][0:64, :])
    ctx.wait("gpsimd", "pe")
    ctx.dma("gpsimd", wg_sb.rearrange("p (c t) -> p c t", c=NCD),
            io["wgq"].rearrange("(c p) t -> p c t", p=128), inc="gld")
    ctx.wait("tensor", "gld")
    for nt in range(NT):
        tn = nt * TC
        ctx.wait("vector", "pe")
        for k in range(NCD):
            copy(ctx, "vector", xbnt[:, k * TC:(k + 1) * TC],
                 xres[:, k * T + tn:k * T + tn + TC])
        vx = ctx.val("dve")
        ctx.wait("tensor", "dve", vx)
        for mc in range(4):
            bk = mc % 2
            bankwait(bk)
            for k in range(NCD):
                mm(ctx, ps[bk], wg_sb[:, k * 512 + mc * 128:k * 512 + (mc + 1) * 128],
                   xbnt[:, k * TC:(k + 1) * TC], k == 0, k == NCD - 1,
                   inc="pe" if k == NCD - 1 else None)
            ctx.wait("scalar", "pe")
            pvdone[bk] = ("act", ctx.do(
                "scalar", lambda e, o=gqt_sb[:, mc * T + tn:mc * T + tn + TC],
                i=ps[bk]: e.activation(o, i, AF.Identity, scale=0.125),
                inc="act"))
    units = []
    for lh in range(8):
        rh = 64 * (lh % 2)
        mc = lh // 2
        for qt in range(16):
            gq = gqt_sb[rh:rh + 64, mc * T + qt * 128:mc * T + qt * 128 + 128]
            gk = gkt_sb[rh:rh + 64, mc * 64:(mc + 1) * 64]

            def smm(pbank, gq=gq, gk=gk):
                return mm(ctx, pbank[:, 0:64], gq, gk, True, True, inc="pe")
            lm = latmask[:, qt * 64:(qt + 1) * 64]
            out_ap = remT[rh:rh + 64, mc * T + qt * 128:mc * T + qt * 128 + 128]
            units.append(dict(nk=64, smm=smm, mask_ap=lm, scale=0.125,
                              vsl=[gv_sb[:, lh * 64:(lh + 1) * 64]],
                              out_ap=out_ap, out_p=64))
    attn_pipe(ctx, ps, C, units, P4A, [PT_b, PT_bB], cols16, psb2, pvdone)
    ctx.fence()

    rw_sb = ctx.at_b(L1A_WG, 4096)
    ctx.dma("gpsimd", rw_sb.rearrange("p (c t) -> p c t", c=4),
            io["r_out_w"].rearrange("(c p) t -> p c t", p=128), inc="gld")
    ctx.wait("tensor", "gld")
    for c in range(NCD):
        for nt in range(NT):
            tn = nt * TC
            bk0 = (c * NT + nt) % 2
            bankwait(bk0)
            for kk in range(4):
                mm(ctx, ps[bk0], rw_sb[:, kk * 1024 + c * 128:kk * 1024 + (c + 1) * 128],
                   remT[:, kk * T + tn:kk * T + tn + TC], kk == 0, kk == 3)
            bankwait(2)
            mm(ctx, ps[2], esel[:, c * 128:(c + 1) * 128], glsig[:, tn:tn + TC],
               True, True)
            bankwait(3)
            mm(ctx, ps[3], esel[:, c * 128:(c + 1) * 128], grsig[:, tn:tn + TC],
               True, True, inc="pe")
            ctx.wait("vector", "pe")
            vv = copy(ctx, "vector", remo_c, ps[bk0])
            lt = localT[:, c * T + tn:c * T + tn + TC]
            tt(ctx, "vector", ALU.mult, scf, ps[2], lt)
            pvdone[2] = ctx.val("dve")
            pvdone[bk0] = pvdone[2]
            tt(ctx, "vector", ALU.mult, scf2, ps[3], remo_c)
            pvdone[3] = ctx.val("dve")
            tt(ctx, "vector", ALU.add, lt, scf, scf2, inc="dve")
    ctx.fence()

    wo_sb = ctx.at_b(L1A_GQT, 8192)
    ctx.dma("gpsimd", wo_sb.rearrange("p (c t) -> p c t", c=NCD),
            io["w_out"].rearrange("(c p) t -> p c t", p=128), inc="gld")
    ctx.wait("tensor", "gld")
    resdone = {}
    for m in range(NCD):
        for nt in range(NT):
            bk = (m * NT + nt) % 4
            if bk in resdone:
                ctx.wait("tensor", "dve", resdone[bk])
            for k in range(NCD):
                mm(ctx, ps[bk], wo_sb[:, k * 1024 + m * 128:k * 1024 + (m + 1) * 128],
                   localT[:, k * T + nt * TC:k * T + (nt + 1) * TC],
                   k == 0, k == NCD - 1, inc="pe" if k == NCD - 1 else None)
            ctx.wait("vector", "pe")
            xs = xres[:, m * T + nt * TC:m * T + (nt + 1) * TC]
            resdone[bk] = tt(ctx, "vector", ALU.add, xs, xs, ps[bk], inc="dve")
    ctx.fence()
    emit_ln_plain(ctx, ps, xres)


def prep_l1attn(w):
    ext = {
        "wq": np.ascontiguousarray(w["L1_attn_wq"]),
        "wk": np.ascontiguousarray(w["L1_attn_wk"]),
        "wv": np.ascontiguousarray(w["L1_attn_wv"]),
        "wgq": np.ascontiguousarray(w["L1_attn_wgq"]),
        "wgk": np.ascontiguousarray(w["L1_attn_wgk"]),
        "wgv": np.ascontiguousarray(w["L1_attn_wgv"]),
        "r_out_w": np.ascontiguousarray(w["L1_attn_r_out_w"]),
        "w_out": np.ascontiguousarray(w["L1_attn_w_out"]),
    }
    gpl = np.zeros((1024, 16), dtype=np.float32)
    gpr = np.zeros((1024, 16), dtype=np.float32)
    for h in range(16):
        gpl[64 * h:64 * h + 64, h] = w["L1_attn_gate_local_w"][h]
        gpr[64 * h:64 * h + 64, h] = w["L1_attn_gate_remote_w"][h]
    ext["gpackl"] = gpl
    ext["gpackr"] = gpr
    i = np.arange(256)[:, None]
    j = np.arange(512)[None, :]
    base = (j > i) & (j <= i + 256)
    m1 = base.astype(np.float32).reshape(2, 128, 512)
    m0 = (base & (j >= 256)).astype(np.float32).reshape(2, 128, 512)
    ext["lmask"] = np.stack([m0, m1])
    tq = np.arange(2048)
    lv = (np.arange(64)[None, :] <= (tq // 32)[:, None])
    ext["latmask"] = lv.astype(np.float32).reshape(16, 128, 64)
    es = np.zeros((8, 16, 128), dtype=np.float32)
    for c in range(8):
        for p in range(128):
            es[c, (128 * c + p) // 64, p] = 1.0
    ext["esel"] = es
    return ext


L1A_SHAPES = {
    "wq": ((D, D), F32), "wk": ((D, D), F32), "wv": ((D, D), F32),
    "wgq": ((D, 512), F32), "wgk": ((D, 512), F32), "wgv": ((D, 512), F32),
    "r_out_w": ((512, D), F32), "w_out": ((D, D), F32),
    "gpackl": ((D, 16), F32), "gpackr": ((D, 16), F32),
    "lmask": ((2, 2, 128, 512), F32), "latmask": ((16, 128, 64), F32),
    "esel": ((8, 16, 128), F32),
}



LF_ATT = 32768
LF_WQKV = 49152
LF_XSB = 73728
LF_QBLK = 75776
LF_KRING = 77824
LF_VRING = 81920
LF_SQ = 86016
LF_RN = 88064
LF_QN = 88576
LF_SCF = 89088
LF_COLS = 90112
LF_BCOL = 90176
LF_HPK = 90240
LF_ROTB = 90368
LF_ESEL = 90496
LF_MASK = 91520
LF_CS = 93568
LF_UT = 73728
LF_SQS = 81920
LF_B1C = 84096


def phase_l1_ffn(ctx, ps, io):
    C = ctx.C
    xres = C["xres"]
    psb = C["psb"]
    psb2 = [psb, ps[7].bitcast(BF16)]
    ctx.fence()
    attT = ctx.at_b(LF_ATT, 16384)
    wqkv = ctx.at_b(LF_WQKV, 24576)
    xs_blk = ctx.at_b(LF_XSB, 2048)
    qblk = ctx.at_b(LF_QBLK, 2048)
    kbuf = ctx.at_b(LF_KRING, 4096)
    vring = [ctx.at_b(LF_VRING, 2048), ctx.at_b(LF_VRING + 2048, 2048)]
    sqb = ctx.at_b(LF_SQ, 2048)
    maskv0 = ctx.at_b(101760, 512)
    rnorm = ctx.at_f(LF_RN, 256)[0:16, :]
    PT_b = ctx.at_b(LF_RN, 512)
    qn_b = ctx.at_b(LF_QN, 256)
    t1_b = ctx.at_b(LF_QN + 256, 256)
    P_b = ctx.at_b(LF_QN, 512)
    scf = ctx.at_f(LF_SCF, 512)
    scfB = ctx.at_f(LF_SQ, 512)
    P_bB = ctx.at_b(LF_SQ + 1024, 512)
    PT_bB = ctx.at_b(LF_SQ + 1536, 512)
    P4F = [P_b, P_bB, ctx.at_b(LF_SCF, 512), ctx.at_b(LF_SQ, 512)]
    cols16F = ctx.at_f(LF_SCF + 512, 16)
    cols = ctx.at_f(LF_COLS, 8)
    bcol = ctx.at_f(LF_BCOL, 32)
    hpack = ctx.at_b(LF_HPK, 128)
    rotb = ctx.at_b(LF_ROTB, 128)
    eselb = ctx.at_b(LF_ESEL, 1024)[0:16, :]
    lmask_b = ctx.at_b(LF_MASK, 2048)
    css = [ctx.at_b(LF_CS + i * 2048, 2048) for i in range(4)]

    ctx.dma("gpsimd", css[0], io["cs_t"], inc="gld")
    ctx.dma("gpsimd", css[1], io["ss_t"], inc="gld")
    ctx.dma("gpsimd", css[2], io["ci_t"], inc="gld")
    ctx.dma("gpsimd", css[3], io["si_t"], inc="gld")
    ctx.dma("gpsimd", hpack.rearrange("p (c n) -> p c n", c=8),
            io["hpack"].rearrange("c p n -> p c n"), inc="gld")
    ctx.dma("gpsimd", rotb, io["rotb"], inc="gld")
    ctx.dma("gpsimd", lmask_b.rearrange("p (v q f) -> p v q f", v=2, q=2),
            io["fmask"].rearrange("v q p f -> p v q f"), inc="gld")
    ctx.dma("gpsimd", maskv0.rearrange("p (q f) -> p q f", q=2),
            io["fmask0"].rearrange("q p f -> p q f"), inc="gld")
    ctx.dma("gpsimd", eselb.rearrange("p (c n) -> p c n", c=8),
            io["esel"].rearrange("c h n -> h c n"), inc="gld")
    ctx.wait("tensor", "gld")
    ctx.wait("sync", "dve")
    vspill = ctx.dma("sync", io["spill_x"].rearrange("(c p) t -> p c t", p=128),
                     xres.rearrange("p (c t) -> p c t", c=NCD), inc="ld")

    pvdone = {}

    def bankwait(bk):
        if bk in pvdone:
            v = pvdone[bk]
            s, v = v if isinstance(v, tuple) else ("dve", v)
            ctx.wait("tensor", s, v)

    CH = [(0, 512), (512, 1024), (1024, 1536), (1536, 2048), (2048, 2560),
          (2560, 2730)]

    for li in range(6):
        ctx.fence()
        pvdone.clear()
        xsT = ctx.at_b(49152, 16384)
        attTf = ctx.at_b(49152, 16384)
        QTf = ctx.at_b(65536, 16384)
        KTf = ctx.at_b(0, 16384)
        Vf = ctx.at_b(16384, 16384)
        slab = ctx.at_b(81920, 8192)
        sqp = [ctx.at_b(86016, 2048), ctx.at_b(88064, 2048)]
        rn16 = ctx.at_f(16384, 2048)[0:16, :]
        rnb16 = ctx.at_b(20480, 2048)[0:16, :]
        qn2w = [ctx.at_b(22528, 512), ctx.at_b(23040, 512)]
        t12w = [ctx.at_b(23552, 512), ctx.at_b(24064, 512)]
        scfw = [ctx.at_f(24576, 512), ctx.at_f(25600, 512)]
        P4w = [ctx.at_b(86016 + i * 512, 512) for i in range(4)]
        PT2w = [ctx.at_b(88064, 512), ctx.at_b(88576, 512)]
        colsw = ctx.at_f(89088, 16)
        ctx.dma("sync", bcol[:, 0:1], io["qsrep"][li], inc="ld")
        ctx.dma("sync", bcol[:, 1:2], io["ksrep"][li], inc="ld")
        for k in range(NCD):
            dst = xsT[:, k * T:(k + 1) * T]
            if k < 4:
                copy(ctx, "vector", dst, xres[:, k * T:(k + 1) * T])
            else:
                ctx.do("vector", lambda e, o=dst[:, 0:1]: e.memset(o, 0.0))
                copy(ctx, "vector", dst[:, 1:T], xres[:, k * T:k * T + T - 1])
        vxs = ctx.val("dve")
        ctx.wait("tensor", "dve", vxs)
        ctx.wait("sync", "dve", vxs)
        vspl = ctx.dma("sync", io["spill_x2"].rearrange("(c p) t -> p c t", p=128),
                       xres.rearrange("p (c t) -> p c t", c=NCD), inc="ld")
        ctx.wait("gpsimd", "pe")
        ctx.wait("gpsimd", "dve")
        ctx.wait("gpsimd", "act")
        for part, dstf in ((0, QTf), (1, KTf)):
            qs2 = "qa" if part == 0 else "qb"
            if part == 1:
                ctx.wait("gpsimd", "pe")
            vw = ctx.dma("gpsimd", slab.rearrange("p (c t) -> p c t", c=NCD),
                         io["Wqkv"][li].rearrange("(c p) t -> p c t", p=128)[
                             :, :, part * 1024:(part + 1) * 1024], inc=qs2)
            ctx.wait("tensor", qs2, vw)
            if part == 1:
                ctx.wait("vector", "ld", vspl)
            for m in range(NCD):
                for nt in range(NT):
                    bk = (m * NT + nt) % 4
                    bankwait(bk)
                    for k in range(NCD):
                        mm(ctx, ps[bk], slab[:, k * 1024 + m * 128:k * 1024 + (m + 1) * 128],
                           xsT[:, k * T + nt * TC:k * T + (nt + 1) * TC],
                           k == 0, k == NCD - 1, inc="pe" if k == NCD - 1 else None)
                    ctx.wait("vector", "pe")
                    pvdone[bk] = copy(
                        ctx, "vector",
                        dstf[:, m * T + nt * TC:m * T + (nt + 1) * TC], ps[bk])
        ctx.wait("scalar", "ld", vspl)
        for which in range(2):
            tgtf = QTf if which == 0 else KTf
            csAf = css[0] if which == 0 else css[2]
            csBf = css[1] if which == 0 else css[3]
            scol = bcol[:, 0:1] if which == 0 else bcol[:, 1:2]
            ctx.wait("scalar", "dve")
            ctx.wait("scalar", "pe")
            sqd = {}
            for bb in (4, 5, 6, 7):
                bankwait(bb)
            for c in range(NCD):
                sq = sqp[c % 2]
                if (c % 2) in sqd:
                    ctx.wait("scalar", "pe", sqd[c % 2])
                vsq = act(ctx, sq, tgtf[:, c * T:(c + 1) * T], AF.Square,
                          inc="act")
                ctx.wait("tensor", "act", vsq)
                for nt in range(NT):
                    sqd[c % 2] = mm(
                        ctx, ps[4 + nt][0:16, 0:TC], hpack[:, c * 16:(c + 1) * 16],
                        sq[:, nt * TC:(nt + 1) * TC], c == 0, c == NCD - 1,
                        inc="pe")
            ctx.wait("vector", "pe")
            for nt in range(NT):
                copy(ctx, "vector", rn16[:, nt * TC:(nt + 1) * TC],
                     ps[4 + nt][0:16, 0:TC],
                     inc="dve" if nt == NT - 1 else None)
            ctx.do("vector", lambda e, o=rn16, i=rn16:
                   e.tensor_scalar_max(o, i, 1e-24), inc="dve")
            for bb in (4, 5, 6, 7):
                pvdone[bb] = ctx.val("dve")
            ctx.wait("scalar", "dve")
            act(ctx, rn16, rn16, AF.Ln)
            va = act(ctx, rn16, rn16, AF.Exp, scale=-0.5, inc="act")
            ctx.wait("vector", "act", va)
            vrb = copy(ctx, "vector", rnb16, rn16, inc="dve")
            ctx.wait("tensor", "dve", vrb)
            st_r = {}

            def R1(u):
                c, nt = u // NT, u % NT
                bankwait(4 + u % 2)
                st_r[("vr1", u)] = mm(ctx, ps[4 + u % 2], eselb[:, c * 128:(c + 1) * 128],
                                      rnb16[:, nt * TC:(nt + 1) * TC], True, True,
                                      inc="pe")

            def R23(u):
                c, nt = u // NT, u % NT
                par = u % 2
                ctx.wait("vector", "pe", st_r[("vr1", u)])
                vq = ctx.do("vector", lambda e, o=qn2w[par],
                            i=tgtf[:, c * T + nt * TC:c * T + (nt + 1) * TC],
                            s=scol, r=ps[4 + par]:
                            e.scalar_tensor_tensor(o, i, s, r, ALU.mult,
                                                   ALU.mult), inc="dve")
                pvdone[4 + par] = vq
                ctx.wait("tensor", "dve", vq)
                bankwait(6 + par)
                st_r[("vrot", u)] = mm(ctx, ps[6 + par], rotb, qn2w[par],
                                       True, True, inc="pe")

            def R4(u):
                c, nt = u // NT, u % NT
                par = u % 2
                sc2 = scfw[par]
                tt(ctx, "vector", ALU.mult, t12w[par], qn2w[par],
                   csAf[:, nt * TC:(nt + 1) * TC])
                ctx.wait("vector", "pe", st_r[("vrot", u)])
                tt(ctx, "vector", ALU.mult, sc2, ps[6 + par],
                   csBf[:, nt * TC:(nt + 1) * TC])
                pvdone[6 + par] = ctx.val("dve")
                tt(ctx, "vector", ALU.add,
                   tgtf[:, c * T + nt * TC:c * T + (nt + 1) * TC],
                   t12w[par], sc2, inc="dve")

            NU = NCD * NT
            for u in range(NU + 1):
                if u < NU:
                    R1(u)
                    R23(u)
                if u >= 1:
                    R4(u - 1)
        ctx.wait("gpsimd", "dve")
        ctx.wait("gpsimd", "pe")
        vw = ctx.dma("gpsimd", slab.rearrange("p (c t) -> p c t", c=NCD),
                     io["Wqkv"][li].rearrange("(c p) t -> p c t", p=128)[
                         :, :, 2048:3072], inc="qa")
        ctx.wait("tensor", "qa", vw)
        for tt_i in range(16):
            for nc2 in range(2):
                bk = (tt_i * 2 + nc2) % 4
                bankwait(bk)
                for k in range(NCD):
                    mm(ctx, ps[bk],
                       xsT[:, k * T + tt_i * 128:k * T + tt_i * 128 + 128],
                       slab[:, k * 1024 + nc2 * 512:k * 1024 + (nc2 + 1) * 512],
                       k == 0, k == NCD - 1, inc="pe" if k == NCD - 1 else None)
                ctx.wait("vector", "pe")
                pvdone[bk] = copy(
                    ctx, "vector",
                    Vf[:, tt_i * 1024 + nc2 * 512:tt_i * 1024 + (nc2 + 1) * 512],
                    ps[bk])
        units = []
        for blk in range(8):
            t0 = blk * 256
            nk = 256 if blk == 0 else 512
            k0 = 0 if blk == 0 else (blk - 1) * 256
            for h in range(16):
                ch, rh = h // 2, 64 * (h % 2)
                for qt in range(2):
                    lq = QTf[rh:rh + 64,
                             ch * T + t0 + qt * 128:ch * T + t0 + qt * 128 + 128]
                    krhs = KTf[rh:rh + 64, ch * T + k0:ch * T + k0 + nk]

                    def smm(pbank, lq=lq, krhs=krhs, nk=nk):
                        return mm(ctx, pbank[:, 0:nk], lq, krhs, True, True,
                                  inc="pe")
                    if blk == 0:
                        msl = maskv0[:, qt * 256:qt * 256 + 256]
                    else:
                        msl = lmask_b[:, 1024 + qt * 512:1024 + qt * 512 + nk]
                    vt0 = 0 if blk == 0 else 2 * (blk - 1)
                    vsl = [Vf[:, (vt0 + kk) * 512 + (0 if True else 0):0] if False
                           else Vf[:, (vt0 + kk) // 2 * 1024 + ((vt0 + kk) % 2) * 512:0]
                           if False else
                           Vf[:, (k0 + kk * 128) // 128 * 0:0]
                           for kk in range(0)]
                    vsl = []
                    for kk in range(nk // 128):
                        tok = k0 + kk * 128
                        ti2 = tok // 128
                        vsl.append(Vf[:, (ti2 // 2) * 1024 + (ti2 % 2) * 512 + 0:0]
                                   if False else
                                   Vf[:, (ti2 // 2) * 1024 + (ti2 % 2) * 512 + h * 0:0]
                                   if False else None)
                    vsl = []
                    for kk in range(nk // 128):
                        tok = k0 + kk * 128
                        vti = tok // 128
                        base = (vti // 2) * 1024 + 0
                        vsl.append(Vf[:, vti * 1024 + h * 64:vti * 1024 + h * 64 + 64])
                    out_ap = attTf[rh:rh + 64,
                                   ch * T + t0 + qt * 128:ch * T + t0 + qt * 128 + 128]
                    units.append(dict(nk=nk, smm=smm, mask_ap=msl,
                                      vsl=vsl, out_ap=out_ap, out_p=64))
        psb2w = [ps[6].bitcast(BF16), ps[7].bitcast(BF16)]
        wo_sb = ctx.at_b(32768, 8192)
        ctx.wait("gpsimd", "pe")
        ctx.wait("gpsimd", "dve")
        ctx.wait("gpsimd", "act")
        vwo = ctx.dma("gpsimd", wo_sb.rearrange("p (c t) -> p c t", c=NCD),
                      io["Wo"][li].rearrange("(c p) t -> p c t", p=128),
                      inc="gld")
        attn_pipe(ctx, ps, C, units, P4w, PT2w, colsw, psb2w, pvdone)
        ctx.wait("sync", "dve")
        ctx.wait("sync", "pe")
        vxr = ctx.dma("sync", xres.rearrange("p (c t) -> p c t", c=NCD),
                      io["spill_x2"].rearrange("(c p) t -> p c t", p=128),
                      inc="ld")
        ctx.wait("vector", "ld", vxr)
        ctx.wait("tensor", "ld", vxr)
        ctx.fence()
        pvdone.clear()
        ctx.wait("tensor", "gld", vwo)
        if li == 0:
            ctx.wait("vector", "ld", vspill)
        for m in range(NCD):
            for nt in range(NT):
                bk = (m * NT + nt) % 4
                bankwait(bk)
                for k in range(NCD):
                    mm(ctx, ps[bk], wo_sb[:, k * 1024 + m * 128:k * 1024 + (m + 1) * 128],
                       attTf[:, k * T + nt * TC:k * T + (nt + 1) * TC],
                       k == 0, k == NCD - 1, inc="pe" if k == NCD - 1 else None)
                ctx.wait("vector", "pe")
                xs = xres[:, m * T + nt * TC:m * T + (nt + 1) * TC]
                pvdone[bk] = tt(ctx, "vector", ALU.add, xs, xs, ps[bk], inc="dve")

        ctx.fence()
        pvdone.clear()
        xnT = ctx.at_b(LF_ATT, 16384)
        reg1 = ctx.at_f(LF_WQKV, 2048)
        reg2 = ctx.at_f(LF_WQKV + 4096, 2048)
        uT_c = ctx.at_b(LF_WQKV, 8192)
        wbuf = [ctx.at_b(LF_WQKV + 8192, 12288),
                ctx.at_b(LF_WQKV + 20480, 12288)]
        sqs = [ctx.at_b(LF_SQS, 512), ctx.at_b(LF_SQS + 512, 512)]
        b1c = ctx.at_f(LF_B1C, 8)
        ctx.dma("sync", bcol[:, 4:12], io["lng8"][li], inc="ld")
        ctx.dma("sync", bcol[:, 12:20], io["lnb8"][li], inc="ld")
        ctx.dma("sync", bcol[:, 20:28], io["b28"][li], inc="ld")
        vld = ctx.val("ld")
        ctx.wait("vector", "ld", vld)
        ctx.wait("scalar", "ld", vld)

        CHM = [([(0, 128), (128, 256), (256, 384), (384, 512)] if c1 - c0 == 512
                else [(0, 128), (128, 170)]) for c0, c1 in CH]

        def load_chunk(ci):
            c0, c1 = CH[ci]
            w = c1 - c0
            wb = wbuf[ci % 2]
            qsem = "qa" if ci % 2 == 0 else "qb"
            ctx.dma("gpsimd",
                    wb[:, 0:4096].rearrange("p (c t) -> p c t", c=NCD)[:, :, 0:w],
                    io["W1"][li].rearrange("(c p) t -> p c t", p=128)[:, :, c0:c1],
                    inc=qsem)
            ctx.dma("gpsimd",
                    wb[:, 4096:8192].rearrange("p (c t) -> p c t", c=NCD)[:, :, 0:w],
                    io["W1"][li].rearrange("(c p) t -> p c t", p=128)[
                        :, :, 2730 + c0:2730 + c1], inc=qsem)
            for kk, (k0, k1) in enumerate(CHM[ci]):
                ctx.dma("gpsimd",
                        wb[0:k1 - k0, 8192 + kk * 1024:8192 + (kk + 1) * 1024],
                        io["W2"][li][c0 + k0:c0 + k1, :], inc=qsem)
            return ctx.val(qsem)

        wready = {0: load_chunk(0)}
        sqdone = {}
        for half in range(2):
            for nt in range(NT):
                tn = nt * TC
                for c4 in range(4):
                    c = half * 4 + c4
                    sq = sqs[c4 % 2]
                    if (c4 % 2) in sqdone:
                        ctx.wait("scalar", "pe", sqdone[c4 % 2])
                    vsq = act(ctx, sq, xres[:, c * T + tn:c * T + tn + TC],
                              AF.Square, inc="act")
                    ctx.wait("tensor", "act", vsq)
                    mm(ctx, ps[0][0:1, 0:TC], C["onesf"],
                       xres[:, c * T + tn:c * T + tn + TC], c4 == 0, c4 == 3)
                    sqdone[c4 % 2] = mm(ctx, ps[1][0:1, 0:TC], C["onesb"], sq,
                                        c4 == 0, c4 == 3, inc="pe")
                ctx.wait("vector", "pe")
                rS = reg1[0:1, :] if half == 0 else reg1[32:33, :]
                rQ = reg1[64:65, :] if half == 0 else reg1[96:97, :]
                copy(ctx, "vector", rS[:, tn:tn + TC], ps[0][0:1, 0:TC])
                copy(ctx, "vector", rQ[:, tn:tn + TC], ps[1][0:1, 0:TC],
                     inc="dve")
                ctx.wait("tensor", "dve")
        copy(ctx, "vector", reg2[0:1, 1:T], reg1[32:33, 0:T - 1])
        tt(ctx, "vector", ALU.add, reg2[0:1, 1:T], reg2[0:1, 1:T],
           reg1[0:1, 1:T])
        copy(ctx, "vector", reg2[0:1, 0:1], reg1[0:1, 0:1])
        copy(ctx, "vector", reg2[64:65, 1:T], reg1[96:97, 0:T - 1])
        tt(ctx, "vector", ALU.add, reg2[64:65, 1:T], reg2[64:65, 1:T],
           reg1[64:65, 1:T])
        copy(ctx, "vector", reg2[64:65, 0:1], reg1[64:65, 0:1])
        ctx.do("vector", lambda e, o=reg2[0:1, :], i=reg2[0:1, :]:
               e.tensor_scalar_mul(o, i, 1.0 / D))
        ctx.do("vector", lambda e, o=reg2[64:65, :], i=reg2[64:65, :]:
               e.tensor_scalar_mul(o, i, 1.0 / D))
        tt(ctx, "vector", ALU.mult, reg1[64:65, :], reg2[0:1, :], reg2[0:1, :])
        tt(ctx, "vector", ALU.subtract, reg2[64:65, :], reg2[64:65, :],
           reg1[64:65, :], inc="dve")
        ctx.wait("scalar", "dve")
        act(ctx, reg2[64:65, :], reg2[64:65, :], AF.Ln, bias=C["eps64"])
        va = act(ctx, reg1[64:65, :], reg2[64:65, :], AF.Exp, scale=-0.5,
                 inc="act")
        ctx.wait("vector", "act", va)
        copy(ctx, "vector", reg1[0:1, :], reg1[64:65, :], inc="dve")
        ctx.wait("tensor", "act")
        ctx.wait("tensor", "dve")
        def ffn_bcast(nt):
            tn = nt * TC
            b0, b1 = (2, 3) if nt % 2 == 0 else (6, 7)
            bankwait(b0)
            mm(ctx, ps[b0], C["ones1f"], reg2[0:1, tn:tn + TC], True, True)
            bankwait(b1)
            mm(ctx, ps[b1], C["ones1f"], reg1[0:1, tn:tn + TC], True, True,
               inc="pe")
            return ctx.val("pe")

        vbcs = {0: ffn_bcast(0), 1: ffn_bcast(1)}
        for nt in range(NT):
            tn = nt * TC
            b0, b1 = (2, 3) if nt % 2 == 0 else (6, 7)
            ctx.wait("vector", "pe", vbcs[nt])
            for c in range(NCD):
                gcol = bcol[:, 4 + c:5 + c]
                bcl = bcol[:, 12 + c:13 + c]
                dst = xnT[:, c * T + tn:c * T + tn + TC]
                if c < 4:
                    xsrc = xres[:, c * T + tn:c * T + tn + TC]
                elif nt > 0:
                    xsrc = xres[:, c * T + tn - 1:c * T + tn + TC - 1]
                else:
                    xsrc = None
                if xsrc is not None:
                    tt(ctx, "vector", ALU.subtract, scf, xsrc, ps[b0])
                    ctx.do("vector", lambda e, o=scf, i=scf, s=gcol, r=ps[b1]:
                           e.scalar_tensor_tensor(o, i, s, r, ALU.mult, ALU.mult))
                    ctx.do("vector", lambda e, o=dst, i=scf, s=bcl:
                           e.tensor_scalar_add(o, i, s), inc="dve")
                else:
                    tt(ctx, "vector", ALU.subtract, scf[:, 1:TC],
                       xres[:, c * T:c * T + TC - 1], ps[b0][:, 1:TC])
                    ctx.do("vector", lambda e, o=scf[:, 1:TC], i=scf[:, 1:TC],
                           s=gcol, r=ps[b1][:, 1:TC]:
                           e.scalar_tensor_tensor(o, i, s, r, ALU.mult, ALU.mult))
                    ctx.do("vector", lambda e, o=dst[:, 1:TC], i=scf[:, 1:TC],
                           s=bcl: e.tensor_scalar_add(o, i, s))
                    copy(ctx, "vector", cols[:, 3:4], ps[b0][:, 0:1])
                    tt(ctx, "vector", ALU.mult, cols[:, 3:4], cols[:, 3:4],
                       ps[b1][:, 0:1])
                    ctx.do("vector", lambda e, o=cols[:, 4:5], i=cols[:, 3:4],
                           s=gcol: e.tensor_scalar_mul(o, i, s))
                    ctx.do("vector", lambda e, o=cols[:, 4:5], i=cols[:, 4:5]:
                           e.tensor_scalar_mul(o, i, -1.0))
                    ctx.do("vector", lambda e, o=dst[:, 0:1], i=cols[:, 4:5],
                           s=bcl: e.tensor_scalar_add(o, i, s), inc="dve")
            pvdone[b0] = ctx.val("dve")
            pvdone[b1] = pvdone[b0]
            if nt + 2 < NT:
                vbcs[nt + 2] = ffn_bcast(nt + 2)
        ctx.wait("tensor", "dve")
        ctx.wait("gpsimd", "dve")
        ctx.wait("gpsimd", "pe")
        ctx.wait("gpsimd", "act")
        wuse = {}
        for ci, (c0, c1) in enumerate(CH):
            mcs = CHM[ci]
            wb = wbuf[ci % 2]
            qsem = "qa" if ci % 2 == 0 else "qb"
            vqa = wready.pop(ci)
            if ci + 1 < len(CH):
                if ((ci + 1) % 2) in wuse:
                    ctx.wait("gpsimd", "pe", wuse[(ci + 1) % 2])
                wready[ci + 1] = load_chunk(ci + 1)
            ctx.wait("sync", "act")
            ctx.dma("sync", b1c[:, 0:4], io["b1a_t"][li, ci], inc="ld")
            ctx.dma("sync", b1c[:, 4:8], io["b1g_t"][li, ci], inc="ld")
            vb1 = ctx.val("ld")
            ctx.wait("tensor", qsem, vqa)
            ctx.wait("scalar", "ld", vb1)
            for mi, (m0, m1) in enumerate(mcs):
                mw = m1 - m0
                for nt in range(NT):
                    tn = nt * TC
                    it = mi * NT + nt
                    bka, bkg = 4 + 2 * (it % 2) - 4 * (it % 2 and False), 5
                    bka = (4, 6)[it % 2]
                    bkg = (5, 7)[it % 2]
                    bankwait(bka)
                    for k in range(NCD):
                        mm(ctx, ps[bka][0:mw, :], wb[:, k * 512 + m0:k * 512 + m1],
                           xnT[:, k * T + tn:k * T + tn + TC], k == 0, k == NCD - 1)
                    bankwait(bkg)
                    for k in range(NCD):
                        mm(ctx, ps[bkg][0:mw, :],
                           wb[:, 4096 + k * 512 + m0:4096 + k * 512 + m1],
                           xnT[:, k * T + tn:k * T + tn + TC], k == 0, k == NCD - 1,
                           inc="pe" if k == NCD - 1 else None)
                    ctx.wait("scalar", "pe")
                    ctx.wait("scalar", "dve")
                    act(ctx, P_b[0:mw, :], ps[bka][0:mw, :], AF.Identity,
                        bias=b1c[0:mw, mi:mi + 1])
                    va = act(ctx, PT_b[0:mw, :], ps[bkg][0:mw, :], AF.Gelu,
                             bias=b1c[0:mw, 4 + mi:5 + mi], inc="act")
                    ctx.wait("vector", "act", va)
                    vu = tt(ctx, "vector", ALU.mult,
                            uT_c[0:mw, mi * T + tn:mi * T + tn + TC],
                            P_b[0:mw, :], PT_b[0:mw, :], inc="dve")
                    pvdone[bka] = ("dve", vu)
                    pvdone[bkg] = ("dve", vu)
            ctx.wait("tensor", "dve")
            for mo in range(NCD):
                for nt in range(NT):
                    tn = nt * TC
                    bk = (mo * NT + nt) % 4
                    bankwait(bk)
                    for kk, (k0, k1) in enumerate(mcs):
                        mm(ctx, ps[bk],
                           wb[0:k1 - k0, 8192 + kk * 1024 + mo * 128:8192 + kk * 1024 + (mo + 1) * 128],
                           uT_c[0:k1 - k0, kk * T + tn:kk * T + tn + TC],
                           kk == 0, kk == len(mcs) - 1,
                           inc="pe" if kk == len(mcs) - 1 else None)
                    ctx.wait("vector", "pe")
                    xs = xres[:, mo * T + tn:mo * T + tn + TC]
                    pvdone[bk] = tt(ctx, "vector", ALU.add, xs, xs, ps[bk],
                                    inc="dve")
            wuse[ci % 2] = ctx.val("pe")
        for c in range(NCD):
            ctx.do("vector", lambda e, o=xres[:, c * T:(c + 1) * T],
                   i=xres[:, c * T:(c + 1) * T], s=bcol[:, 20 + c:21 + c]:
                   e.tensor_scalar_add(o, i, s),
                   inc="dve" if c == NCD - 1 else None)
        ctx.wait("tensor", "dve")
    ctx.fence()
    xin_f = ctx.at_f(LF_ATT, 8192)
    for half in range(2):
        vr = ctx.dma("sync",
                     xin_f.rearrange("p (c t) -> p c t", c=4),
                     io["spill_x"].rearrange("(c p) t -> p c t", p=128)[
                         :, half * 4:half * 4 + 4, :], inc="ld")
        ctx.wait("vector", "ld", vr)
        for c4 in range(4):
            c = half * 4 + c4
            xs = xres[:, c * T:(c + 1) * T]
            tt(ctx, "vector", ALU.add, xs, xs, xin_f[:, c4 * T:(c4 + 1) * T],
               inc="dve" if c4 == 3 else None)
        ctx.wait("sync", "dve")
    ctx.fence()
    ctx._off = LF_ATT
    emit_ln_plain(ctx, ps, xres)


def prep_l1ffn(w):
    DIM_HEAD, WINDOW = 64, 256
    n = T
    F8 = np.float64
    ext = {
        "Wqkv": np.ascontiguousarray(w["L1_ffn_Wqkv"]),
        "Wo": np.ascontiguousarray(w["L1_ffn_Wo"]),
        "W1": np.ascontiguousarray(w["L1_ffn_W1"]),
        "W2": np.ascontiguousarray(w["L1_ffn_W2"]),
    }
    ext["qsrep"] = np.tile(w["L1_ffn_q_scale"], (1, 2))[:, :, None].astype(np.float32)
    ext["ksrep"] = np.tile(w["L1_ffn_k_scale"], (1, 2))[:, :, None].astype(np.float32)
    ext["lng8"] = np.ascontiguousarray(
        w["L1_ffn_ln_g"].reshape(6, 8, 128).transpose(0, 2, 1))
    ext["lnb8"] = np.ascontiguousarray(
        w["L1_ffn_ln_b"].reshape(6, 8, 128).transpose(0, 2, 1))
    ext["b28"] = np.ascontiguousarray(
        w["L1_ffn_b2"].reshape(6, 8, 128).transpose(0, 2, 1))
    CHl = [(0, 512), (512, 1024), (1024, 1536), (1536, 2048), (2048, 2560),
           (2560, 2730)]
    b1a = np.zeros((6, 6, 4, 128), dtype=np.float32)
    b1g = np.zeros((6, 6, 4, 128), dtype=np.float32)
    for li in range(6):
        for ci, (c0, c1) in enumerate(CHl):
            for mi in range(4):
                a0 = c0 + mi * 128
                a1 = min(c1, a0 + 128)
                if a0 >= c1:
                    continue
                b1a[li, ci, mi, 0:a1 - a0] = w["L1_ffn_b1"][li, a0:a1]
                b1g[li, ci, mi, 0:a1 - a0] = w["L1_ffn_b1"][li, 2730 + a0:2730 + a1]
    ext["b1a_t"] = np.ascontiguousarray(b1a.transpose(0, 1, 3, 2))
    ext["b1g_t"] = np.ascontiguousarray(b1g.transpose(0, 1, 3, 2))
    half = np.arange(0, DIM_HEAD, 2, dtype=F8)
    inv_freq = 1.0 / 10000.0 ** (half / DIM_HEAD)
    t = np.arange(n, dtype=F8)
    freqs = t[:, None] * inv_freq[None, :]
    freqs = np.concatenate([freqs, freqs], -1)
    base_scale = (half + 0.4 * DIM_HEAD) / (1.4 * DIM_HEAD)
    power = (t - n // 2) / (WINDOW // 2)
    scale = base_scale[None, :] ** power[:, None]
    scale = np.concatenate([scale, scale], -1)
    cos, sin = np.cos(freqs), np.sin(freqs)
    ext["cs_t"] = np.tile((cos * scale * 8.0).T, (2, 1)).astype(np.float32)
    ext["ss_t"] = np.tile((sin * scale * 8.0).T, (2, 1)).astype(np.float32)
    ext["ci_t"] = np.tile((cos / scale).T, (2, 1)).astype(np.float32)
    ext["si_t"] = np.tile((sin / scale).T, (2, 1)).astype(np.float32)
    R = np.zeros((128, 128), dtype=np.float32)
    for m in range(128):
        if m % 64 < 32:
            R[m + 32, m] = -1.0
        else:
            R[m - 32, m] = 1.0
    ext["rotb"] = R
    hp = np.zeros((8, 128, 16), dtype=np.float32)
    for c in range(8):
        for p in range(128):
            hp[c, p, 2 * c + p // 64] = 1.0
    ext["hpack"] = hp
    def fm_var(b):
        i = np.arange(256)[:, None]
        j = np.arange(512)[None, :]
        q_abs = b * 256 + i
        if b % 2 == 0:
            k_abs = np.where(j < 256, b * 256 + j, (b - 1) * 256 + (j - 256))
        else:
            k_abs = np.where(j < 256, (b - 1) * 256 + j, b * 256 + (j - 256))
        v = (k_abs <= q_abs) & (q_abs <= k_abs + 256)
        return v.astype(np.float32).reshape(2, 128, 512)
    ext["fmask"] = np.stack([fm_var(2), fm_var(1)])
    i0 = np.arange(256)[:, None]
    j0 = np.arange(256)[None, :]
    v0 = (j0 <= i0)
    ext["fmask0"] = v0.astype(np.float32).reshape(2, 128, 256)
    return ext


L1F_SHAPES = {
    "Wqkv": ((6, D, 3072), F32), "Wo": ((6, D, D), F32),
    "W1": ((6, D, 5460), F32), "W2": ((6, 2730, D), F32),
    "qsrep": ((6, 128, 1), F32), "ksrep": ((6, 128, 1), F32),
    "lng8": ((6, 128, 8), F32), "lnb8": ((6, 128, 8), F32),
    "b28": ((6, 128, 8), F32),
    "b1a_t": ((6, 6, 128, 4), F32), "b1g_t": ((6, 6, 128, 4), F32),
    "cs_t": ((128, T), F32), "ss_t": ((128, T), F32),
    "ci_t": ((128, T), F32), "si_t": ((128, T), F32),
    "rotb": ((128, 128), F32), "hpack": ((8, 128, 16), F32),
    "fmask": ((2, 2, 128, 512), F32), "fmask0": ((2, 128, 256), F32),
}



_CACHE = {}


def _get_program():
    if "nc" not in _CACHE:
        allshapes = {}
        for s in (L0A_SHAPES, L0F_SHAPES, L1A_SHAPES, L1F_SHAPES):
            allshapes.update(s)
        _CACHE["nc"] = build_program(
            [phase_l0_attn, phase_l0_ffn, phase_l1_attn, phase_l1_ffn],
            allshapes)
    return _CACHE["nc"]


def kernel(**inputs):
    w = {k: np.asarray(v) for k, v in inputs.items()}
    ext = {}
    for p in (prep_l0attn, prep_l0ffn, prep_l1attn, prep_l1ffn):
        ext.update(p(w))
    nc = _get_program()
    in_maps = []
    for b in range(8):
        mdict = {"x_in": np.ascontiguousarray(w["L0_attn_x"][b].T)}
        mdict.update(ext)
        in_maps.append(mdict)
    res = run_bass_kernel_spmd(nc, in_maps, core_ids=list(range(8)))
    return np.stack([r["out"].T for r in res.results]).astype(np.float32)
